# version 10
# baseline (speedup 1.0000x reference)
import contextlib

import numpy as np

import concourse.bass as bass
import concourse.mybir as mybir
from concourse.alu_op_type import AluOpType as AO
from concourse.bass_utils import run_bass_kernel_spmd

F32 = mybir.dt.float32
BF16 = mybir.dt.bfloat16
I32 = mybir.dt.int32
AF = mybir.ActivationFunctionType
AX = mybir.AxisListType

D = 1024
IFF = 2816
DC = D // 128
ICN = IFF // 128
NTAP = 40
QMAX = 127.0
EPS = 1e-5
HALO = 128
CLIP_LO = -(QMAX + 1.49)
CLIP_HI = QMAX + 0.49


def _tiles(total, step=512):
    out, t = [], 0
    while t < total:
        out.append((t, min(step, total - t)))
        t += step
    return out


def build_nc(t_own, dbg=False, stop_after=None):
    nc = bass.Bass("TRN2", target_bir_lowering=False)
    t_tot = t_own + HALO
    n128 = t_tot // 128
    n128o = t_own // 128
    NT = _tiles(t_tot)
    TSEQ = 2 * t_own

    x_in = nc.dram_tensor("x", [t_tot, D], F32, kind="ExternalInput").ap()
    hm_in = nc.dram_tensor("hmask", [1, 1], F32, kind="ExternalInput").ap()
    bsel_in = nc.dram_tensor("bsel", [1, 4], F32, kind="ExternalInput").ap()
    w1_in = nc.dram_tensor("W1", [D, D], F32, kind="ExternalInput").ap()
    b1_in = nc.dram_tensor("b1", [D], F32, kind="ExternalInput").ap()
    w2_in = nc.dram_tensor("W2", [D, D], F32, kind="ExternalInput").ap()
    b2_in = nc.dram_tensor("b2", [D], F32, kind="ExternalInput").ap()
    mw_in = nc.dram_tensor("mw", [NTAP + 1, D], F32, kind="ExternalInput").ap()
    cx_in = nc.dram_tensor("clip_x", [1, 1], F32, kind="ExternalInput").ap()
    cd_in = nc.dram_tensor("clip_down", [1, 1], F32, kind="ExternalInput").ap()
    wg_in = nc.dram_tensor("w_gate", [IFF, D], F32, kind="ExternalInput").ap()
    wu_in = nc.dram_tensor("w_up", [IFF, D], F32, kind="ExternalInput").ap()
    wd_in = nc.dram_tensor("w_down", [D, IFF], F32, kind="ExternalInput").ap()
    out_ext = nc.dram_tensor("out", [t_own, D], F32, kind="ExternalOutput").ap()

    KI = "ExternalOutput" if dbg else "Internal"
    xp_h = nc.dram_tensor("xp_h", [t_tot, D], F32, kind=KI).ap()
    xpp_h = nc.dram_tensor("xpp_h", [t_own, D], F32, kind=KI).ap()
    qa_h = nc.dram_tensor("qa_h", [t_tot, D], BF16, kind=KI).ap()
    h_h = nc.dram_tensor("h_h", [IFF, t_tot], F32, kind=KI).ap()
    dbg_h = nc.dram_tensor("dbg_h", [128, 64], F32, kind=KI).ap() if dbg \
        else None
    qwg_h = nc.dram_tensor("qwg_h", [IFF, D], BF16, kind="Internal").ap()
    qwu_h = nc.dram_tensor("qwu_h", [IFF, D], BF16, kind="Internal").ap()
    qwd_h = nc.dram_tensor("qwd_h", [D, IFF], BF16, kind="Internal").ap()
    wgT_h = nc.dram_tensor("wgT_h", [D, IFF], BF16, kind="Internal").ap()
    wuT_h = nc.dram_tensor("wuT_h", [D, IFF], BF16, kind="Internal").ap()
    wdT_h = nc.dram_tensor("wdT_h", [IFF, D], BF16, kind="Internal").ap()
    misc_h = nc.dram_tensor("misc_h", [4096], F32, kind="Internal").ap()
    diag_h = nc.dram_tensor("diag_h", [DC * 128 * NTAP * 128], BF16,
                            kind="Internal").ap()
    cc_t = {}
    for i, shp in (("1", [1, 1]), ("2", [1, 1]), ("3", [32, 128]),
                   ("4", [1, 1]), ("5", [1, 1])):
        cc_t["i" + i] = nc.dram_tensor("cc%si" % i, shp, F32, kind="Internal")
        cc_t["o" + i] = nc.dram_tensor("cc%so" % i, shp, F32, kind="Internal")

    ctx = contextlib.ExitStack()
    with ctx:
        sbuf = lambda name, shape, dt: ctx.enter_context(
            nc.sbuf_tensor(name, shape, dt))

        AR_N = max(DC * t_tot, 2 * ICN * 512 + ICN * 1024,
                   2 * DC * 1024 + 2 * DC * 512 + DC * 640 + DC * 512)
        arena = sbuf("arena", [128, AR_N], BF16)
        qx_fm = arena[:, :DC * t_tot].rearrange("p (c t) -> p c t", c=DC)
        qh_v = arena[:, :2 * ICN * 512].rearrange(
            "p (b c t) -> p b c t", b=2, c=ICN)
        wdT_v = arena[:, 2 * ICN * 512:2 * ICN * 512 + ICN * 1024].rearrange(
            "p (c t) -> p c t", c=ICN)
        _o = 0
        w1_v = arena[:, _o:_o + DC * 1024].rearrange("p (c t) -> p c t", c=DC)
        _o += DC * 1024
        w2_v = arena[:, _o:_o + DC * 1024].rearrange("p (c t) -> p c t", c=DC)
        _o += DC * 1024
        x2f_v = arena[:, _o:_o + 3 * DC * 512].rearrange(
            "p (b c t) -> p b c t", b=3, c=DC)
        _o += 3 * DC * 512
        pr_v = arena[:, _o:_o + DC * 640].rearrange("p (c t) -> p c t", c=DC)
        _o += DC * 640
        z_v = arena[:, _o:_o + DC * 512].rearrange("p (c t) -> p c t", c=DC)
        _o += DC * 512
        stg_v = arena[:, :IFF]
        stg2_v = arena[:, IFF:IFF + 1024]

        xa = sbuf("xa", [128, 2, D], F32)
        xb = sbuf("xb", [128, 2, D], F32)
        qt = sbuf("qt", [128, 2, D], F32)
        qb = sbuf("qb", [128, 2, D], BF16)
        ab = sbuf("ab", [128, 2, D], F32)
        ob = sbuf("ob", [128, 2, D], F32)
        wgc = sbuf("wgc", [128, 2, DC, 128], BF16)
        wuc = sbuf("wuc", [128, 2, DC, 128], BF16)
        st_t = sbuf("st_t", [128, 4, 512], F32)
        hlu = sbuf("hlu", [128, 2, 2, 512], F32)
        ht = hlu[:, 0, :, :]
        hl = hlu
        hq_ti = sbuf("hq_ti", [128, 2, 2, 512], I32)
        qi = hq_ti[:, 0].rearrange("p a b -> p (a b)")
        diagc = sbuf("diagc", [128, 2, NTAP, 128], BF16)
        idf = sbuf("idf", [128, 128], F32)
        idb = sbuf("idb", [128, 128], BF16)
        onesf = sbuf("onesf", [1, 128], F32)
        io1 = sbuf("io1", [128, 128], I32)
        io2 = sbuf("io2", [128, 128], I32)

        mstat = sbuf("mstat", [128, 6 * n128], F32)
        mr = lambda w, t: mstat[:, w * n128 + t:w * n128 + t + 1]
        mrr = lambda w, n: mstat[:, w * n128:w * n128 + n]
        sums = sbuf("sums", [128, 2 * n128], F32)
        s_sum = lambda t: sums[:, t:t + 1]
        s_ssq = lambda t: sums[:, n128 + t:n128 + t + 1]
        sgu = sbuf("sgu", [128, 2 * ICN + DC], F32)
        sgc = lambda i: sgu[:, i:i + 1]
        suc = lambda i: sgu[:, ICN + i:ICN + i + 1]
        sdc = lambda i: sgu[:, 2 * ICN + i:2 * ICN + i + 1]
        cgu = sbuf("cgu", [128, 2 * ICN], F32)
        cgc = lambda i: cgu[:, i:i + 1]
        cuc = lambda i: cgu[:, ICN + i:ICN + i + 1]
        mwf = sbuf("mwf", [128, DC, NTAP + 1], F32)
        mwe = sbuf("mwe", [128, DC, NTAP], F32)
        b8 = sbuf("b8", [8, 2 * 128], F32)
        b1f = sbuf("b1f", [128, DC], F32)
        b2f = sbuf("b2f", [128, DC], F32)
        b1T = sbuf("b1T", [128, DC], F32)
        scal = sbuf("scal", [128, 8], F32)
        CSX, CSH, HMASK, SHS = 0, 1, 2, 3
        sc_c = lambda i: scal[:, i:i + 1]
        s1r = sbuf("s1r", [1, 32], F32)
        vrow = sbuf("vrow", [128, D], F32)
        tot8 = sbuf("tot8", [128, DC], F32)
        ctr32 = sbuf("ctr32", [128, 32], F32)
        ar32 = sbuf("ar32", [32, 128], F32)
        allT = sbuf("allT", [128, 32], F32)
        bsel_b = sbuf("bsel_b", [128, 4], F32)
        rowD = sbuf("rowD", [1, D], F32)
        rowD2 = sbuf("rowD2", [1, D], F32)
        t1 = sbuf("t1", [128, 64], F32)
        t2 = sbuf("t2", [128, 64], F32)
        t3 = sbuf("t3", [128, 64], F32)
        t4 = sbuf("t4", [128, 64], F32)
        gy = sbuf("gy", [128, 1], F32)
        gh = sbuf("gh", [128, 1], F32)
        u8 = sbuf("u8", [128, DC], F32)
        u8b = sbuf("u8b", [128, DC], BF16)
        t8b = sbuf("t8b", [128, DC], BF16)
        totf = sbuf("totf", [128, DC], F32)

        ps = [ctx.enter_context(nc.psum_tensor("ps%d" % i, [128, 512], F32))
              for i in range(8)]

        sems = {}
        for s in ["VA", "VB", "VC", "AA", "AB", "TA", "TB", "TC",
                  "PFA", "DG", "CC", "GI"]:
            sems[s] = [ctx.enter_context(nc.semaphore(s)), 0]
        for s in ["LA", "LB", "LW", "TR", "SQ", "SG", "SO", "LH", "CQ"]:
            sems[s + "0"] = [ctx.enter_context(nc.semaphore(s + "0")), 0]
            sems[s + "1"] = [ctx.enter_context(nc.semaphore(s + "1")), 0]

        def inc(inst, s, n):
            sems[s][1] += n
            inst.then_inc(sems[s][0], n)
            return sems[s][1]

        def sh(s):
            return sems[s][0]

        def sv(s):
            return sems[s][1]

        class Stm:

            def __init__(self, name, per=1):
                self.k = [name + "0", name + "1"]
                self.b = [sv(self.k[0]), sv(self.k[1])]
                self.per = per
                self.d = 0

            def issue(self, inst):
                step = self.d // self.per
                inc(inst, self.k[step % 2], 16)
                self.d += 1

            def wait(self, eng, step):
                eng.wait_ge(sh(self.k[step % 2]),
                            self.b[step % 2] + 16 * self.per * (step // 2 + 1))

            def wait_all(self, eng):
                eng.wait_ge(sh(self.k[0]), sv(self.k[0]))
                eng.wait_ge(sh(self.k[1]), sv(self.k[1]))

        def wait_all_stores(eng):
            for s in ("SQ0", "SQ1", "SG0", "SG1", "SO0", "SO1"):
                eng.wait_ge(sh(s), sv(s))

        def entry_guard():
            vals = {s: sv(s) for s in ("SQ0", "SQ1", "SG0", "SG1",
                                       "SO0", "SO1")}

            def g(eng):
                for s, v in vals.items():
                    eng.wait_ge(sh(s), v)
            return g

        with nc.Block() as blk:
            L0a = Stm("LW")

            @blk.sync
            def _(sync):
                L0a.issue(sync.dma_start(qt[0:NTAP + 1, 0, :], mw_in[:, :]))
                L0a.issue(sync.dma_start(
                    b8[0:8, 0:128], b1_in.rearrange("(c p) -> c p", p=128)))
                L0a.issue(sync.dma_start(
                    b8[0:8, 128:256], b2_in.rearrange("(c p) -> c p", p=128)))
                sync.wait_ge(sh("GI"), 1)
                L0a.issue(sync.dma_start(s1r[0:1, 0:1], cx_in[:, :]))
                L0a.issue(sync.dma_start(s1r[0:1, 1:2], cd_in[:, :]))
                L0a.issue(sync.dma_start(s1r[0:1, 2:3], hm_in[:, :]))
                L0a.issue(sync.dma_start(s1r[0:1, 4:8], bsel_in[:, :]))

            @blk.gpsimd
            def _(g):
                g.iota(io1[:, :], [[1, 128]], channel_multiplier=0)
                g.iota(io2[:, :], [[0, 128]], channel_multiplier=1)
                g.memset(onesf[:, :], 1.0)
                g.memset(gy[:, :], 0.0)
                g.memset(gh[:, :], 0.0)
                g.memset(tot8[:, :], 0.0)
                g.memset(s1r[:, :], 0.0)
                g.memset(t1[:, :], 0.0)
                g.memset(scal[:, :], 0.0)
                g.memset(vrow[:, :], 0.0)
                inc(g.memset(pr_v[:, :, :], 0.0), "GI", 1)

            @blk.vector
            def _(vec):
                vec.wait_ge(sh("GI"), 1)
                vec.tensor_tensor(idf[:, :], io1[:, :], io2[:, :],
                                  op=AO.is_equal)
                vec.drain()
                vec.tensor_copy(idb[:, :], idf[:, :])
                vec.drain()

        with nc.Block() as blk:
            bTA, bVA = sv("TA"), sv("VA")
            SQb = Stm("SQ")

            @blk.tensor
            def _(t):
                L0a.wait_all(t)
                for c in range(DC):
                    inc(t.matmul(ps[c][:, 0:NTAP + 1],
                                 qt[0:NTAP + 1, 0, c * 128:(c + 1) * 128],
                                 idf[0:NTAP + 1, 0:NTAP + 1],
                                 start=True, stop=True), "TA", 1)
                t.wait_ge(sh("VA"), bVA + 1)
                inc(t.matmul(ps[0][:, 0:DC], b8[0:8, 0:128], idf[0:8, 0:8],
                             start=True, stop=True), "TA", 1)
                inc(t.matmul(ps[1][:, 0:DC], b8[0:8, 128:256], idf[0:8, 0:8],
                             start=True, stop=True), "TA", 1)
                inc(t.matmul(ps[2][:, 0:1], onesf[:, :], s1r[0:1, 2:3],
                             start=True, stop=True), "TA", 1)
                inc(t.matmul(ps[3][:, 0:4], onesf[:, :], s1r[0:1, 4:8],
                             start=True, stop=True), "TA", 1)
                t.wait_ge(sh("VA"), bVA + 2)
                inc(t.matmul(ps[4][0:DC, 0:128], b2f[:, :], idf[:, :],
                             start=True, stop=True), "TA", 1)

            @blk.vector
            def _(vec):
                vec.wait_ge(sh("TA"), bTA + DC)
                for c in range(DC):
                    vec.tensor_copy(mwf[:, c, :], ps[c][:, 0:NTAP + 1])
                vec.drain()
                inc(vec.tensor_copy(t1[:, 0:1], mwf[:, 0, 0:1]), "VA", 1)
                vec.wait_ge(sh("TA"), bTA + DC + 4)
                vec.tensor_copy(b1f[:, :], ps[0][:, 0:DC])
                vec.tensor_copy(b2f[:, :], ps[1][:, 0:DC])
                vec.tensor_copy(sc_c(HMASK), ps[2][:, 0:1])
                vec.tensor_copy(bsel_b[:, :], ps[3][:, 0:4])
                vec.drain()
                vec.tensor_scalar(b1T[:, :], b1f[:, :], float(TSEQ), None,
                                  op0=AO.mult)
                for c in range(DC):
                    vec.tensor_scalar(mwe[:, c, :], mwf[:, c, 1:],
                                      mwf[:, c, 0:1], None, op0=AO.subtract)
                vec.drain()
                inc(vec.tensor_copy(t1[:, 0:1], b1f[:, 0:1]), "VA", 1)
                vec.wait_ge(sh("TA"), bTA + DC + 5)
                inc(vec.tensor_copy(ar32[0:DC, :], ps[4][0:DC, 0:128]),
                    "VA", 1)
                vec.drain()

            @blk.scalar
            def _(s):
                s.wait_ge(sh("VA"), bVA + 3)
                SQb.issue(s.dma_start(misc_h[1024:1024 + D].rearrange(
                    "(c p) -> c p", p=128), ar32[0:DC, :]))
                SQb.wait(s, 0)
                SQb.issue(s.dma_start(rowD2[0:1, :],
                                      misc_h[None, 1024:1024 + D]))
                SQb.wait(s, 1)

        with nc.Block() as blk:
            bVA = sv("VA")
            SQp = Stm("SQ")

            @blk.vector
            def _(vec):
                for c in range(DC):
                    if c >= 2:
                        SQp.wait(vec, c - 2)
                    for j in range(NTAP):
                        vec.tensor_scalar(diagc[:, c % 2, j, :], idb[:, :],
                                          mwe[:, c, j:j + 1], None,
                                          op0=AO.mult)
                    vec.drain()
                    inc(vec.tensor_copy(t1[:, 0:1], t1[:, 0:1]), "VA", 1)
                    vec.drain()

            @blk.scalar
            def _(s):
                for c in range(DC):
                    s.wait_ge(sh("VA"), bVA + c + 1)
                    SQp.issue(s.dma_start(
                        bass.AP(diag_h.tensor, c * 128 * NTAP * 128,
                                [[NTAP * 128, 128], [1, NTAP * 128]]),
                        diagc[:, c % 2, :, :]))

        def head_merged():
            PCS = [(0, 1024), (1024, 1024), (2048, 768)]
            qunits = ([('g', t) for t in range(ICN)] +
                      [('u', t) for t in range(ICN)] +
                      [('d', t) for t in range(DC)])
            U = []
            li = qi_ = 0
            nqu = len(qunits)
            while li < n128 or qi_ < nqu:
                take_q = 1 if qi_ < nqu else 0
                if take_q and (li >= n128 or qi_ * n128 <= li * nqu):
                    U.append(qunits[qi_])
                    qi_ += 1
                else:
                    U.append(('l', li))
                    li += 1
            la_steps = {}
            qstep_of = {}
            la = 0
            qs = 0
            for u in U:
                kind, i = u
                if kind in ('g', 'u'):
                    la_steps[u] = [la]
                    la += 1
                    qstep_of[u] = [qs]
                    qs += 1
                elif kind == 'd':
                    qstep_of[u] = [qs, qs + 1, qs + 2]
                    qs += 3
            nq = qs
            wd_stage = [(qt[:, 0, :], qt[:, 1, :]), None]

            with nc.Block() as blk:
                bVA, bVB, bVC = sv("VA"), sv("VB"), sv("VC")
                bAA, bAB, bTB, bTC = sv("AA"), sv("AB"), sv("TB"), sv("TC")
                LA, SQ, LB = Stm("LA"), Stm("SQ"), Stm("LB")
                LDW = Stm("LH", per=3)
                eg = entry_guard()

                @blk.sync
                def _(sync):
                    wait_all_stores(sync)
                    for u in U:
                        kind, i = u
                        if kind in ('g', 'u'):
                            q = qstep_of[u][0]
                            if q >= 2:
                                sync.wait_ge(sh("VA"), bVA + (q - 1))
                            wap = wg_in if kind == 'g' else wu_in
                            LA.issue(sync.dma_start(
                                xa[:, q % 2, :],
                                wap[i * 128:(i + 1) * 128, :]))
                        elif kind == 'd':
                            if i >= 1:
                                sync.wait_ge(sh("VB"), bVB + i)
                            for p3, (o3, n3) in enumerate(PCS):
                                dst = (qt[:, 0, :n3], qt[:, 1, :n3],
                                       ob[:, 0, :n3])[p3]
                                LDW.issue(sync.dma_start(
                                    dst,
                                    wd_in[i * 128:(i + 1) * 128,
                                          o3:o3 + n3]))
                        else:
                            t = i
                            if t >= 2:
                                sync.wait_ge(sh("AA"), bAA + 2 * t - 2)
                            LB.issue(sync.dma_start(
                                xb[:, t % 2, :],
                                x_in[t * 128:(t + 1) * 128, :]))

                @blk.vector
                def _(vec):
                    eg(vec)
                    for u in U:
                        kind, i = u
                        if kind in ('g', 'u'):
                            q = qstep_of[u][0]
                            LA.wait(vec, la_steps[u][0])
                            if q >= 2:
                                SQ.wait(vec, q - 2)
                            w_t = xa[:, q % 2, :]
                            scol = sgc if kind == 'g' else suc
                            vec.tensor_reduce(t1[:, 0:1], w_t, axis=AX.X,
                                              op=AO.max,
                                              apply_absolute_value=True)
                            vec.drain()
                            vec.reciprocal(t1[:, 1:2], t1[:, 0:1])
                            vec.drain()
                            vec.tensor_scalar(scol(i), t1[:, 0:1],
                                              1.0 / QMAX, None, op0=AO.mult)
                            vec.tensor_scalar(t1[:, 2:3], t1[:, 1:2], QMAX,
                                              None, op0=AO.mult)
                            vec.drain()
                            vec.tensor_scalar(qi[:, :], w_t, t1[:, 2:3],
                                              None, op0=AO.mult)
                            vec.drain()
                            inc(vec.tensor_copy(qb[:, q % 2, :], qi[:, :]),
                                "VA", 1)
                            vec.drain()
                        elif kind == 'd':
                            LDW.wait(vec, i)
                            srcs = (qt[:, 0, :], qt[:, 1, :], ob[:, 0, :])
                            for p3, (o3, n3) in enumerate(PCS):
                                vec.tensor_reduce(t1[:, 3 + p3:4 + p3],
                                                  srcs[p3][:, :n3], axis=AX.X,
                                                  op=AO.max,
                                                  apply_absolute_value=True)
                                vec.drain()
                            vec.tensor_reduce(t1[:, 6:7], t1[:, 3:6],
                                              axis=AX.X, op=AO.max)
                            vec.drain()
                            vec.reciprocal(t1[:, 7:8], t1[:, 6:7])
                            vec.drain()
                            vec.tensor_scalar(sdc(i), t1[:, 6:7], 1.0 / QMAX,
                                              None, op0=AO.mult)
                            vec.tensor_scalar(t1[:, 8:9], t1[:, 7:8], QMAX,
                                              None, op0=AO.mult)
                            vec.drain()
                            vec.tensor_scalar(t1[:, 9:10], sdc(i), 0.5, None,
                                              op0=AO.mult)
                            vec.drain()
                            for p3, (o3, n3) in enumerate(PCS):
                                q = qstep_of[u][p3]
                                if q >= 2:
                                    SQ.wait(vec, q - 2)
                                vec.tensor_scalar(qi[:, :n3],
                                                  srcs[p3][:, :n3],
                                                  t1[:, 8:9], None,
                                                  op0=AO.mult)
                                vec.drain()
                                inc(vec.tensor_scalar(qb[:, q % 2, :n3],
                                                      qi[:, :n3],
                                                      t1[:, 9:10], None,
                                                      op0=AO.mult), "VA", 1)
                                vec.drain()
                            inc(vec.tensor_copy(t1[:, 0:1], t1[:, 0:1]),
                                "VB", 1)
                        else:
                            t = i
                            c = 10 + 8 * (t % 2)
                            LB.wait(vec, t)
                            inc(vec.tensor_reduce(s_sum(t), xb[:, t % 2, :],
                                                  axis=AX.X, op=AO.add),
                                "VC", 1)
                            vec.wait_ge(sh("AA"), bAA + 2 * t + 1)
                            vec.tensor_scalar(mr(0, t), s_sum(t), 1.0 / D,
                                              None, op0=AO.mult)
                            vec.tensor_scalar(t1[:, c + 1:c + 2], s_ssq(t),
                                              1.0 / D, EPS, op0=AO.mult,
                                              op1=AO.add)
                            vec.drain()
                            vec.tensor_tensor(t1[:, c:c + 1], mr(0, t),
                                              mr(0, t), op=AO.mult)
                            vec.drain()
                            vec.tensor_tensor(t1[:, c + 2:c + 3],
                                              t1[:, c + 1:c + 2],
                                              t1[:, c:c + 1],
                                              op=AO.subtract)
                            vec.drain()
                            inc(vec.reciprocal(t1[:, c + 3:c + 4],
                                               t1[:, c + 2:c + 3]), "TB", 1)
                            vec.drain()
                            vec.wait_ge(sh("AB"), bAB + t + 1)
                            vec.reciprocal(t1[:, c + 5:c + 6],
                                           t1[:, c + 4:c + 5])
                            vec.drain()
                            vec.tensor_tensor(t1[:, c + 5:c + 6],
                                              t1[:, c + 5:c + 6],
                                              t1[:, c + 3:c + 4],
                                              op=AO.mult)
                            vec.drain()
                            vec.tensor_tensor(t1[:, c + 5:c + 6],
                                              t1[:, c + 5:c + 6],
                                              t1[:, c + 4:c + 5], op=AO.add)
                            vec.drain()
                            vec.tensor_scalar(mr(1, t), t1[:, c + 5:c + 6],
                                              0.5, None, op0=AO.mult)
                            vec.drain()
                            inc(vec.scalar_tensor_tensor(
                                t1[:, c + 6:c + 7], mr(0, t), -1.0,
                                mr(1, t), op0=AO.mult, op1=AO.mult),
                                "TC", 1)
                            vec.wait_ge(sh("AA"), bAA + 2 * t + 2)
                            vec.tensor_reduce(t1[:, c + 7:c + 8],
                                              ab[:, t % 2, :],
                                              axis=AX.X, op=AO.max)
                            vec.drain()
                            inc(vec.tensor_tensor(gy[:, :], gy[:, :],
                                                  t1[:, c + 7:c + 8],
                                                  op=AO.max), "VC", 1)
                            vec.drain()

                @blk.scalar
                def _(s):
                    eg(s)
                    for u in U:
                        kind, i = u
                        if kind in ('g', 'u'):
                            q = qstep_of[u][0]
                            s.wait_ge(sh("VA"), bVA + q + 1)
                            q_dst = qwg_h if kind == 'g' else qwu_h
                            SQ.issue(s.dma_start(
                                q_dst[i * 128:(i + 1) * 128, :],
                                qb[:, q % 2, :]))
                        elif kind == 'd':
                            for p3, (o3, n3) in enumerate(PCS):
                                q = qstep_of[u][p3]
                                s.wait_ge(sh("VA"), bVA + q + 1)
                                SQ.issue(s.dma_start(
                                    qwd_h[i * 128:(i + 1) * 128,
                                          o3:o3 + n3],
                                    qb[:, q % 2, :n3]))
                        else:
                            t = i
                            c = 10 + 8 * (t % 2)
                            s.wait_ge(sh("VC"), bVC + 2 * t + 1)
                            inc(s.activation(ab[:, t % 2, :], xb[:, t % 2, :],
                                             AF.Square, accum_out=s_ssq(t)),
                                "AA", 1)
                            s.wait_ge(sh("TB"), bTB + t + 1)
                            inc(s.activation(t1[:, c + 4:c + 5],
                                             t1[:, c + 3:c + 4], AF.Sqrt),
                                "AB", 1)
                            s.wait_ge(sh("TC"), bTC + t + 1)
                            inc(s.activation(ab[:, t % 2, :], xb[:, t % 2, :],
                                             AF.Abs,
                                             bias=t1[:, c + 6:c + 7],
                                             scale=mr(1, t)), "AA", 1)

        def quant_weight(wap, nchunk, scol, q_dst):
            with nc.Block() as blk:
                bVA = sv("VA")
                LA, SQ = Stm("LA"), Stm("SQ")
                eg = entry_guard()

                @blk.sync
                def _(sync):
                    for i in range(nchunk):
                        if i >= 2:
                            sync.wait_ge(sh("VA"), bVA + (i - 1))
                        LA.issue(sync.dma_start(
                            xa[:, i % 2, :], wap[i * 128:(i + 1) * 128, :]))

                @blk.vector
                def _(vec):
                    eg(vec)
                    for i in range(nchunk):
                        LA.wait(vec, i)
                        if i >= 2:
                            SQ.wait(vec, i - 2)
                        w_t = xa[:, i % 2, :]
                        vec.tensor_reduce(t1[:, 0:1], w_t, axis=AX.X,
                                          op=AO.max, apply_absolute_value=True)
                        vec.drain()
                        vec.reciprocal(t1[:, 1:2], t1[:, 0:1])
                        vec.drain()
                        vec.tensor_scalar(scol(i), t1[:, 0:1], 1.0 / QMAX,
                                          None, op0=AO.mult)
                        vec.tensor_scalar(t1[:, 2:3], t1[:, 1:2], QMAX, None,
                                          op0=AO.mult)
                        vec.drain()
                        vec.tensor_scalar(qi[:, :], w_t, t1[:, 2:3],
                                          None, op0=AO.mult)
                        vec.drain()
                        inc(vec.tensor_copy(qb[:, i % 2, :], qi[:, :]),
                            "VA", 1)
                        vec.drain()

                @blk.scalar
                def _(s):
                    for i in range(nchunk):
                        s.wait_ge(sh("VA"), bVA + (i + 1))
                        SQ.issue(s.dma_start(q_dst[i * 128:(i + 1) * 128, :],
                                             qb[:, i % 2, :]))

        head_merged()

        def quant_weight_wide():
            PCS = [(0, 1024), (1024, 1024), (2048, 768)]
            np3 = len(PCS)
            with nc.Block() as blk:
                bVA, bVB = sv("VA"), sv("VB")
                LA, SQ = Stm("LA"), Stm("SQ")
                eg = entry_guard()

                @blk.sync
                def _(sync):
                    j = 0
                    for i in range(DC):
                        for p3, (o3, n3) in enumerate(PCS):
                            if j >= 2:
                                sync.wait_ge(sh("VB"), bVB + (j - 1))
                            LA.issue(sync.dma_start(
                                xa[:, j % 2, :n3],
                                wd_in[i * 128:(i + 1) * 128, o3:o3 + n3]))
                            j += 1
                        for p3, (o3, n3) in enumerate(PCS):
                            sync.wait_ge(sh("VB"), bVB + (j - 1))
                            LA.issue(sync.dma_start(
                                xa[:, j % 2, :n3],
                                wd_in[i * 128:(i + 1) * 128, o3:o3 + n3]))
                            j += 1

                @blk.vector
                def _(vec):
                    eg(vec)
                    j = 0
                    for i in range(DC):
                        for p3, (o3, n3) in enumerate(PCS):
                            LA.wait(vec, j)
                            vec.tensor_reduce(t1[:, p3:p3 + 1],
                                              xa[:, j % 2, :n3], axis=AX.X,
                                              op=AO.max,
                                              apply_absolute_value=True)
                            vec.drain()
                            inc(vec.tensor_copy(t1[:, 8:9], t1[:, 0:1]),
                                "VB", 1)
                            j += 1
                        vec.tensor_reduce(t1[:, 4:5], t1[:, 0:np3], axis=AX.X,
                                          op=AO.max)
                        vec.drain()
                        vec.reciprocal(t1[:, 5:6], t1[:, 4:5])
                        vec.drain()
                        vec.tensor_scalar(sdc(i), t1[:, 4:5], 1.0 / QMAX,
                                          None, op0=AO.mult)
                        vec.tensor_scalar(t1[:, 6:7], t1[:, 5:6], QMAX, None,
                                          op0=AO.mult)
                        vec.drain()
                        vec.tensor_scalar(t1[:, 7:8], sdc(i), 0.5, None,
                                          op0=AO.mult)
                        vec.drain()
                        for p3, (o3, n3) in enumerate(PCS):
                            LA.wait(vec, j)
                            jq = i * np3 + p3
                            if jq >= 2:
                                SQ.wait(vec, jq - 2)
                            vec.tensor_scalar(qi[:, :n3],
                                              xa[:, j % 2, :n3], t1[:, 6:7],
                                              None, op0=AO.mult)
                            vec.drain()
                            inc(vec.tensor_scalar(qb[:, jq % 2, :n3],
                                                  qi[:, :n3], t1[:, 7:8],
                                                  None, op0=AO.mult),
                                "VA", 1)
                            inc(vec.tensor_copy(t1[:, 8:9], t1[:, 0:1]),
                                "VB", 1)
                            vec.drain()
                            j += 1

                @blk.scalar
                def _(s):
                    for i in range(DC):
                        for p3, (o3, n3) in enumerate(PCS):
                            jq = i * np3 + p3
                            s.wait_ge(sh("VA"), bVA + (jq + 1))
                            SQ.issue(s.dma_start(
                                qwd_h[i * 128:(i + 1) * 128, o3:o3 + n3],
                                qb[:, jq % 2, :n3]))


        def ln_batch_mr(ntile, m_which, r_which):
            with nc.Block() as blk:
                bVA, bAA = sv("VA"), sv("AA")
                n = ntile

                @blk.vector
                def _(vec):
                    vec.tensor_scalar(mrr(m_which, n), sums[:, 0:n], 1.0 / D,
                                      None, op0=AO.mult)
                    vec.drain()
                    vec.tensor_tensor(t1[:, :n], mrr(m_which, n),
                                      mrr(m_which, n), op=AO.mult)
                    vec.drain()
                    vec.scalar_tensor_tensor(t2[:, :n], sums[:, n128:n128 + n],
                                             1.0 / D, t1[:, :n], op0=AO.mult,
                                             op1=AO.subtract)
                    vec.drain()
                    vec.tensor_scalar(t2[:, :n], t2[:, :n], EPS, None,
                                      op0=AO.add)
                    vec.drain()
                    inc(vec.reciprocal(t3[:, :n], t2[:, :n]), "VA", 1)
                    vec.drain()
                    vec.wait_ge(sh("AA"), bAA + 1)
                    vec.reciprocal(t2[:, :n], t4[:, :n])
                    vec.drain()
                    vec.tensor_tensor(t2[:, :n], t2[:, :n], t3[:, :n],
                                      op=AO.mult)
                    vec.drain()
                    vec.tensor_tensor(t2[:, :n], t2[:, :n], t4[:, :n],
                                      op=AO.add)
                    vec.drain()
                    vec.tensor_scalar(mrr(r_which, n), t2[:, :n], 0.5, None,
                                      op0=AO.mult)
                    vec.drain()

                @blk.scalar
                def _(s):
                    s.wait_ge(sh("VA"), bVA + 1)
                    inc(s.activation(t4[:, :n], t3[:, :n], AF.Sqrt), "AA", 1)

        def ln_stats_ymax(src_h, ntile, m_which, r_which, vrow_add, gdst):
            with nc.Block() as blk:
                bVA, bVB, bVC = sv("VA"), sv("VB"), sv("VC")
                bAA, bAB = sv("AA"), sv("AB")
                LA = Stm("LA")

                @blk.sync
                def _(sync):
                    wait_all_stores(sync)
                    for t in range(ntile):
                        if t >= 2:
                            sync.wait_ge(sh("AA"), bAA + 2 * t - 2)
                        LA.issue(sync.dma_start(
                            xa[:, t % 2, :], src_h[t * 128:(t + 1) * 128, :]))

                @blk.vector
                def _(vec):
                    for t in range(ntile):
                        c = 10 + 8 * (t % 2)
                        LA.wait(vec, t)
                        if vrow_add:
                            vec.tensor_tensor(xa[:, t % 2, :], xa[:, t % 2, :],
                                              vrow[:, :], op=AO.add)
                            vec.drain()
                        inc(vec.tensor_reduce(s_sum(t), xa[:, t % 2, :],
                                              axis=AX.X, op=AO.add), "VA", 1)
                        vec.wait_ge(sh("AA"), bAA + 2 * t + 1)
                        vec.tensor_scalar(mr(m_which, t), s_sum(t), 1.0 / D,
                                          None, op0=AO.mult)
                        vec.tensor_scalar(t1[:, c + 1:c + 2], s_ssq(t),
                                          1.0 / D, EPS, op0=AO.mult,
                                          op1=AO.add)
                        vec.drain()
                        vec.tensor_tensor(t1[:, c:c + 1], mr(m_which, t),
                                          mr(m_which, t), op=AO.mult)
                        vec.drain()
                        vec.tensor_tensor(t1[:, c + 2:c + 3],
                                          t1[:, c + 1:c + 2], t1[:, c:c + 1],
                                          op=AO.subtract)
                        vec.drain()
                        inc(vec.reciprocal(t1[:, c + 3:c + 4],
                                           t1[:, c + 2:c + 3]), "VB", 1)
                        vec.drain()
                        vec.wait_ge(sh("AB"), bAB + t + 1)
                        vec.reciprocal(t1[:, c + 5:c + 6], t1[:, c + 4:c + 5])
                        vec.drain()
                        vec.tensor_tensor(t1[:, c + 5:c + 6],
                                          t1[:, c + 5:c + 6],
                                          t1[:, c + 3:c + 4], op=AO.mult)
                        vec.drain()
                        vec.tensor_tensor(t1[:, c + 5:c + 6],
                                          t1[:, c + 5:c + 6],
                                          t1[:, c + 4:c + 5], op=AO.add)
                        vec.drain()
                        vec.tensor_scalar(mr(r_which, t), t1[:, c + 5:c + 6],
                                          0.5, None, op0=AO.mult)
                        vec.drain()
                        inc(vec.scalar_tensor_tensor(
                            t1[:, c + 6:c + 7], mr(m_which, t), -1.0,
                            mr(r_which, t), op0=AO.mult, op1=AO.mult),
                            "VC", 1)
                        vec.wait_ge(sh("AA"), bAA + 2 * t + 2)
                        vec.tensor_reduce(t1[:, c + 7:c + 8], ab[:, t % 2, :],
                                          axis=AX.X, op=AO.max)
                        vec.drain()
                        inc(vec.tensor_tensor(gdst[:, :], gdst[:, :],
                                              t1[:, c + 7:c + 8], op=AO.max),
                            "VA", 1)
                        vec.drain()

                @blk.scalar
                def _(s):
                    for t in range(ntile):
                        c = 10 + 8 * (t % 2)
                        s.wait_ge(sh("VA"), bVA + 2 * t + 1)
                        inc(s.activation(ab[:, t % 2, :], xa[:, t % 2, :],
                                         AF.Square, accum_out=s_ssq(t)),
                            "AA", 1)
                        s.wait_ge(sh("VB"), bVB + t + 1)
                        inc(s.activation(t1[:, c + 4:c + 5],
                                         t1[:, c + 3:c + 4], AF.Sqrt),
                            "AB", 1)
                        s.wait_ge(sh("VC"), bVC + t + 1)
                        inc(s.activation(ab[:, t % 2, :], xa[:, t % 2, :],
                                         AF.Abs, bias=t1[:, c + 6:c + 7],
                                         scale=mr(r_which, t)), "AA", 1)

        def ln_stats(src_h, ntile, m_which, r_which, vrow_add):
            with nc.Block() as blk:
                bVA, bAA = sv("VA"), sv("AA")
                LA = Stm("LA")

                @blk.sync
                def _(sync):
                    wait_all_stores(sync)
                    for t in range(ntile):
                        if t >= 2:
                            sync.wait_ge(sh("AA"), bAA + (t - 1))
                        LA.issue(sync.dma_start(
                            xa[:, t % 2, :], src_h[t * 128:(t + 1) * 128, :]))

                @blk.vector
                def _(vec):
                    for t in range(ntile):
                        LA.wait(vec, t)
                        if vrow_add:
                            vec.tensor_tensor(xa[:, t % 2, :], xa[:, t % 2, :],
                                              vrow[:, :], op=AO.add)
                            vec.drain()
                        inc(vec.tensor_reduce(s_sum(t), xa[:, t % 2, :],
                                              axis=AX.X, op=AO.add), "VA", 1)

                @blk.scalar
                def _(s):
                    for t in range(ntile):
                        s.wait_ge(sh("VA"), bVA + t + 1)
                        inc(s.activation(ab[:, t % 2, :], xa[:, t % 2, :],
                                         AF.Square, accum_out=s_ssq(t)),
                            "AA", 1)

            ln_batch_mr(ntile, m_which, r_which)

        def ymax(src_h, ntile, m_which, r_which, vrow_add, gdst):
            with nc.Block() as blk:
                bVA, bVB, bAA = sv("VA"), sv("VB"), sv("AA")
                LB = Stm("LB")

                @blk.sync
                def _(sync):
                    wait_all_stores(sync)
                    for t in range(ntile):
                        if t >= 2:
                            sync.wait_ge(sh("AA"), bAA + (t - 1))
                        LB.issue(sync.dma_start(
                            xb[:, t % 2, :], src_h[t * 128:(t + 1) * 128, :]))

                @blk.vector
                def _(vec):
                    for t in range(ntile):
                        LB.wait(vec, t)
                        if vrow_add:
                            vec.tensor_tensor(xb[:, t % 2, :], xb[:, t % 2, :],
                                              vrow[:, :], op=AO.add)
                            vec.drain()
                        inc(vec.scalar_tensor_tensor(
                            t1[:, 32 + (t % 2):33 + (t % 2)], mr(m_which, t),
                            -1.0, mr(r_which, t), op0=AO.mult, op1=AO.mult),
                            "VB", 1)
                        vec.wait_ge(sh("AA"), bAA + t + 1)
                        vec.tensor_reduce(t1[:, 34:35], ab[:, t % 2, :],
                                          axis=AX.X, op=AO.max)
                        vec.drain()
                        inc(vec.tensor_tensor(gdst[:, :], gdst[:, :],
                                              t1[:, 34:35], op=AO.max),
                            "VA", 1)
                        vec.drain()

                @blk.scalar
                def _(s):
                    for t in range(ntile):
                        s.wait_ge(sh("VB"), bVB + t + 1)
                        inc(s.activation(ab[:, t % 2, :], xb[:, t % 2, :],
                                         AF.Abs,
                                         bias=t1[:, 32 + (t % 2):33 + (t % 2)],
                                         scale=mr(r_which, t)), "AA", 1)

        def armax(gsrc, cci, cco, clip_slot, csx_dst, s_dst):
            with nc.Block() as blk:
                bTA, bAA, bVA, bCC = sv("TA"), sv("AA"), sv("VA"), sv("CC")
                SG = Stm("SG")

                @blk.tensor
                def _(t):
                    inc(t.matmul(ps[6][0:1, 0:128], gsrc[:, :], idf[:, :],
                                 start=True, stop=True), "TA", 1)
                    t.wait_ge(sh("VA"), bVA + 2)
                    inc(t.matmul(ps[7][0:128, 0:2], onesf[:, :],
                                 s1r[0:1, 16:18], start=True, stop=True),
                        "TA", 1)

                @blk.vector
                def _(vec):
                    vec.wait_ge(sh("TA"), bTA + 1)
                    inc(vec.tensor_reduce(s1r[0:1, 9:10], ps[6][0:1, 0:128],
                                          axis=AX.X, op=AO.max), "VA", 1)
                    vec.wait_ge(sh("CC"), bCC + 1)
                    SG.wait(vec, 1)
                    vec.wait_ge(sh("AA"), bAA + 1)
                    vec.tensor_tensor(s1r[0:1, 12:13], s1r[0:1, 11:12],
                                      s1r[0:1, 14:15], op=AO.mult)
                    vec.drain()
                    vec.reciprocal(s1r[0:1, 13:14], s1r[0:1, 12:13])
                    vec.drain()
                    vec.tensor_scalar(s1r[0:1, 16:17], s1r[0:1, 13:14], QMAX,
                                      None, op0=AO.mult)
                    inc(vec.tensor_scalar(s1r[0:1, 17:18], s1r[0:1, 12:13],
                                          1.0 / QMAX, None, op0=AO.mult),
                        "VA", 1)
                    vec.drain()
                    vec.wait_ge(sh("TA"), bTA + 2)
                    vec.tensor_copy(csx_dst, ps[7][0:128, 0:1])
                    inc(vec.tensor_copy(s_dst, ps[7][0:128, 1:2]), "VA", 1)
                    vec.drain()

                @blk.scalar
                def _(s):
                    inc(s.activation(s1r[0:1, 11:12], clip_slot, AF.Sigmoid),
                        "AA", 1)

                @blk.gpsimd
                def _(g):
                    g.wait_ge(sh("VA"), bVA + 1)
                    SG.issue(g.dma_start(cci.ap()[:, :], s1r[0:1, 9:10]))
                    SG.wait(g, 0)
                    inc(g.collective_compute(
                        "AllReduce", AO.max, replica_groups=[list(range(8))],
                        ins=[cci.ap().opt()], outs=[cco.ap().opt()]), "CC", 1)
                    g.wait_ge(sh("CC"), bCC + 1)
                    SG.issue(g.dma_start(s1r[0:1, 14:15], cco.ap()[:, :]))

        def quant_apply(src_h, ntile, m_which, r_which, csx_col, vrow_add,
                        do_round, qxt_tok=None, ph0d=False):
            with nc.Block() as blk:
                bVA, bVB, bAA = sv("VA"), sv("VB"), sv("AA")
                LA, SQ = Stm("LA"), Stm("SQ")
                if qxt_tok is not None:
                    TRq = Stm("TR", per=DC)
                    sq0_done = sv("SQ0") + 16 * ((ntile + 1) // 2)
                    sq1_done = sv("SQ1") + 16 * (ntile // 2)
                if ph0d:
                    LWp, SOp = Stm("LW"), Stm("SO")
                    strips = ([('g', hc) for hc in range(DC)] +
                              [('u', hc) for hc in range(DC)] +
                              [('d', ic) for ic in range(ICN)])
                eg = entry_guard()

                @blk.sync
                def _(sync):
                    wait_all_stores(sync)
                    for t in range(ntile):
                        if t >= 2:
                            sync.wait_ge(sh("AA"), bAA + (t - 1))
                        LA.issue(sync.dma_start(
                            xa[:, t % 2, :], src_h[t * 128:(t + 1) * 128, :]))
                    if qxt_tok is not None:
                        sync.wait_ge(sh("SQ0"), sq0_done)
                        sync.wait_ge(sh("SQ1"), sq1_done)
                        for hc in range(DC):
                            TRq.issue(sync.dma_start(
                                qx_fm[:, hc, 0:qxt_tok],
                                qa_h[0:qxt_tok, hc * 128:(hc + 1) * 128],
                                transpose=True))
                        TRq.wait(sync, 0)

                dflat = diagc[:, :, :, :].rearrange("p b j q -> p b (j q)")

                def strip_info(k):
                    kind, j = strips[k]
                    if kind == 'g':
                        return (qwg_h[:, j * 128:(j + 1) * 128],
                                wgT_h[j * 128:(j + 1) * 128, :], IFF)
                    if kind == 'u':
                        return (qwu_h[:, j * 128:(j + 1) * 128],
                                wuT_h[j * 128:(j + 1) * 128, :], IFF)
                    return (qwd_h[:, j * 128:(j + 1) * 128],
                            wdT_h[j * 128:(j + 1) * 128, :], 1024)

                @blk.gpsimd
                def _(g):
                    if not ph0d:
                        return
                    for k in range(len(strips)):
                        _, pdst, pw = strip_info(k)
                        LWp.wait(g, k)
                        SOp.issue(g.dma_start(pdst, dflat[:, k % 2, 0:pw]))

                @blk.vector
                def _(vec):
                    eg(vec)
                    for t in range(ntile):
                        LA.wait(vec, t)
                        if vrow_add:
                            vec.tensor_tensor(xa[:, t % 2, :], xa[:, t % 2, :],
                                              vrow[:, :], op=AO.add)
                            vec.drain()
                        if do_round:
                            vec.tensor_tensor(t2[:, 32 + (t % 2):33 + (t % 2)],
                                              mr(r_which, t), csx_col,
                                              op=AO.mult)
                        else:
                            vec.tensor_copy(t2[:, 32 + (t % 2):33 + (t % 2)],
                                            mr(r_which, t))
                        vec.drain()
                        inc(vec.scalar_tensor_tensor(
                            t3[:, 32 + (t % 2):33 + (t % 2)], mr(m_which, t),
                            -1.0, t2[:, 32 + (t % 2):33 + (t % 2)],
                            op0=AO.mult, op1=AO.mult), "VB", 1)
                        if do_round:
                            vec.wait_ge(sh("AA"), bAA + (t + 1))
                            if t >= 2:
                                SQ.wait(vec, t - 2)
                            vec.tensor_scalar(qi[:, :], qt[:, t % 2, :],
                                              CLIP_LO, CLIP_HI, op0=AO.max,
                                              op1=AO.min)
                            vec.drain()
                            inc(vec.tensor_copy(qb[:, t % 2, :], qi[:, :]),
                                "VA", 1)
                            vec.drain()

                @blk.scalar
                def _(s):
                    eg(s)
                    kph = [0]

                    def ph0d_loads(upto):
                        while kph[0] < min(upto, len(strips)):
                            k = kph[0]
                            src, _, w = strip_info(k)
                            if k >= 2:
                                SOp.wait(s, k - 2)
                            LWp.issue(s.dma_start(dflat[:, k % 2, 0:w], src,
                                                  transpose=True))
                            kph[0] += 1

                    if ph0d:
                        for s_ in ("SQ0", "SQ1"):
                            s.wait_ge(sh(s_), sv(s_))
                    for t in range(ntile):
                        if ph0d:
                            ph0d_loads((t + 1) * len(strips) // ntile)
                        s.wait_ge(sh("VB"), bVB + t + 1)
                        if do_round:
                            if t >= 2:
                                s.wait_ge(sh("VA"), bVA + (t - 1))
                            inc(s.activation(
                                qt[:, t % 2, :], xa[:, t % 2, :], AF.Identity,
                                bias=t3[:, 32 + (t % 2):33 + (t % 2)],
                                scale=t2[:, 32 + (t % 2):33 + (t % 2)]),
                                "AA", 1)
                            s.wait_ge(sh("VA"), bVA + (t + 1))
                        else:
                            if t >= 2:
                                SQ.wait(s, t - 2)
                            aa_val = inc(s.activation(
                                qb[:, t % 2, :], xa[:, t % 2, :], AF.Identity,
                                bias=t3[:, 32 + (t % 2):33 + (t % 2)],
                                scale=t2[:, 32 + (t % 2):33 + (t % 2)]),
                                "AA", 1)
                            s.wait_ge(sh("AA"), aa_val)
                        SQ.issue(s.dma_start(qa_h[t * 128:(t + 1) * 128, :],
                                             qb[:, t % 2, :]))

        def qx_transpose(ttok):
            with nc.Block() as blk:
                TRq = Stm("TR", per=DC)

                @blk.sync
                def _(sync):
                    wait_all_stores(sync)
                    for hc in range(DC):
                        TRq.issue(sync.dma_start(
                            qx_fm[:, hc, 0:ttok],
                            qa_h[0:ttok, hc * 128:(hc + 1) * 128],
                            transpose=True))
                    TRq.wait(sync, 0)

        def ffn_gateup(ttot):
            NTl = _tiles(ttot)
            nnt = len(NTl)
            with nc.Block() as blk:
                bTA = sv("TA")
                bVA, bVB, bPFA = sv("VA"), sv("VB"), sv("PFA")
                LW = Stm("LW", per=2)
                HS = Stm("SG")
                eg = entry_guard()

                @blk.sync
                def _(sync):
                    for ic in range(ICN):
                        if ic >= 2:
                            sync.wait_ge(sh("TA"), bTA + (ic - 1) * nnt)
                        LW.issue(sync.dma_start(
                            wgc[:, ic % 2, :, :],
                            bass.AP(wgT_h.tensor, ic * 128,
                                    [[IFF, 128], [128 * IFF, DC], [1, 128]])))
                        LW.issue(sync.dma_start(
                            wuc[:, ic % 2, :, :],
                            bass.AP(wuT_h.tensor, ic * 128,
                                    [[IFF, 128], [128 * IFF, DC], [1, 128]])))

                @blk.vector
                def _(vec):
                    eg(vec)
                    for i in range(ICN):
                        vec.tensor_tensor(cgc(i), sgc(i), sc_c(SHS),
                                          op=AO.mult)
                        vec.tensor_tensor(cuc(i), suc(i), sc_c(SHS),
                                          op=AO.mult)
                    vec.drain()
                    inc(vec.tensor_copy(t1[:, 0:1], cgc(0)), "VB", 1)
                    k = 0
                    for ic in range(ICN):
                        for (nt0, nts) in NTl:
                            vec.wait_ge(sh("PFA"), bPFA + 2 * (k + 1))
                            if k >= 2:
                                HS.wait(vec, k - 2)
                            vec.tensor_tensor(ht[:, k % 2, :nts],
                                              st_t[:, 2 * (k % 2), :nts],
                                              st_t[:, 2 * (k % 2) + 1, :nts],
                                              op=AO.mult)
                            vec.drain()
                            vec.tensor_reduce(t1[:, 34:35], ht[:, k % 2, :nts],
                                              axis=AX.X, op=AO.max,
                                              apply_absolute_value=True)
                            vec.drain()
                            inc(vec.tensor_tensor(gh[:, :], gh[:, :],
                                                  t1[:, 34:35], op=AO.max),
                                "VA", 1)
                            vec.drain()
                            k += 1

                @blk.tensor
                def _(t):
                    k = 0
                    for ic in range(ICN):
                        LW.wait(t, ic)
                        for (nt0, nts) in NTl:
                            pg = ps[2 * (k % 2)]
                            pu = ps[2 * (k % 2) + 1]
                            if k >= 2:
                                t.wait_ge(sh("PFA"), bPFA + 2 * (k - 1))
                            for hc in range(DC):
                                t.matmul(pg[:, :nts], wgc[:, ic % 2, hc, :],
                                         qx_fm[:, hc, nt0:nt0 + nts],
                                         start=(hc == 0), stop=(hc == DC - 1))
                            inst = None
                            for hc in range(DC):
                                inst = t.matmul(pu[:, :nts],
                                                wuc[:, ic % 2, hc, :],
                                                qx_fm[:, hc, nt0:nt0 + nts],
                                                start=(hc == 0),
                                                stop=(hc == DC - 1))
                            inc(inst, "TA", 1)
                            k += 1

                @blk.scalar
                def _(s):
                    s.wait_ge(sh("VB"), bVB + 1)
                    k = 0
                    for ic in range(ICN):
                        for (nt0, nts) in NTl:
                            pg = ps[2 * (k % 2)]
                            pu = ps[2 * (k % 2) + 1]
                            s.wait_ge(sh("TA"), bTA + k + 1)
                            if k >= 2:
                                s.wait_ge(sh("VA"), bVA + (k - 1))
                            inc(s.activation(st_t[:, 2 * (k % 2), :nts],
                                             pg[:, :nts], AF.Silu,
                                             scale=cgc(ic)), "PFA", 1)
                            inc(s.activation(st_t[:, 2 * (k % 2) + 1, :nts],
                                             pu[:, :nts], AF.Copy,
                                             scale=cuc(ic)), "PFA", 1)
                            k += 1

                @blk.gpsimd
                def _(g):
                    k = 0
                    for ic in range(ICN):
                        for (nt0, nts) in NTl:
                            g.wait_ge(sh("VA"), bVA + k + 1)
                            HS.issue(g.dma_start(
                                bass.AP(h_h.tensor, ic * 128 * t_tot + nt0,
                                        [[t_tot, 128], [1, nts]]),
                                ht[:, k % 2, :nts]))
                            k += 1

        def srow_build():
            with nc.Block() as blk:
                bTA, bVA = sv("TA"), sv("VA")
                SQ = Stm("SQ")
                eg = entry_guard()

                @blk.vector
                def _(vec):
                    eg(vec)
                    for dc in range(DC):
                        vec.tensor_scalar(t1[:, 40 + dc:41 + dc], sdc(dc),
                                          sc_c(SHS), 0.5, op0=AO.mult,
                                          op1=AO.mult)
                    vec.drain()
                    inc(vec.tensor_copy(t1[:, 48:49], t1[:, 40:41]), "VA", 1)
                    vec.wait_ge(sh("TA"), bTA + 1)
                    inc(vec.tensor_copy(ar32[0:DC, :], ps[6][0:DC, 0:128]),
                        "VA", 1)
                    vec.drain()
                    for half in range(2):
                        vec.wait_ge(sh("TA"), bTA + 2 + half)
                        vec.tensor_copy(srow[:, half * 512:(half + 1) * 512],
                                        ps[7 - half][:, 0:512])
                        vec.drain()
                        inc(vec.tensor_copy(t1[:, 49:50], t1[:, 40:41]),
                            "VA", 1)

                @blk.tensor
                def _(t):
                    t.wait_ge(sh("VA"), bVA + 1)
                    inc(t.matmul(ps[6][0:DC, 0:128], t1[:, 40:40 + DC],
                                 idf[:, :], start=True, stop=True), "TA", 1)
                    SQ.wait(t, 1)
                    for half in range(2):
                        if half == 1:
                            t.wait_ge(sh("VA"), bVA + 3)
                        inc(t.matmul(ps[7 - half][:, 0:512], onesf[:, :],
                                     rowD[0:1, half * 512:(half + 1) * 512],
                                     start=True, stop=True), "TA", 1)

                @blk.scalar
                def _(s):
                    s.wait_ge(sh("VA"), bVA + 2)
                    SQ.issue(s.dma_start(
                        misc_h[0:D].rearrange("(c p) -> c p", p=128),
                        ar32[0:DC, :]))
                    SQ.wait(s, 0)
                    SQ.issue(s.dma_start(rowD[0:1, :], misc_h[None, 0:D]))

        def ffn_down(ttot, res_h, out_h, next_stats, final, tok_shift):
            NTl = _tiles(ttot)
            nnt = len(NTl)
            KS = []
            for ni, (nt0, nts) in enumerate(NTl):
                for ms in range((nts + 127) // 128):
                    KS.append((ni, nt0, nts, ms))
            ks_before = [0] * (nnt + 1)
            for ni in range(nnt):
                ks_before[ni + 1] = ks_before[ni] + (NTl[ni][1] + 127) // 128
            with nc.Block() as blk:
                bTA, bAA, bVA = sv("TA"), sv("AA"), sv("VA")
                bVB, bPFA, bAB = sv("VB"), sv("PFA"), sv("AB")
                bDGd = sv("DG")
                LW = Stm("LW")
                LH = Stm("LH")
                XB = Stm("LB")
                ST = Stm("SO" if final else "SG")
                eg = entry_guard()

                ICN2 = ICN // 2

                @blk.sync
                def _(sync):
                    wait_all_stores(sync)
                    LW.issue(sync.dma_start(
                        wdT_v[:, :, :],
                        bass.AP(wdT_h.tensor, 0,
                                [[1024, 128], [128 * 1024, ICN], [1, 1024]])))
                    def xb_issue(k):
                        kni, knt0, knts, kms = KS[k]
                        if k >= 2:
                            sync.wait_ge(sh("DG"), bDGd + (k - 1))
                        XB.issue(sync.dma_start(
                            xb[:, k % 2, :],
                            res_h[knt0 + kms * 128:knt0 + kms * 128 + 128,
                                  :]))

                    kk = 0
                    for ni, (nt0, nts) in enumerate(NTl):
                        for icp in range(ICN2):
                            if kk >= 2:
                                sync.wait_ge(sh("AA"), bAA + (kk - 1))
                            LH.issue(sync.dma_start(
                                hl[:, kk % 2, :, :nts],
                                bass.AP(h_h.tensor,
                                        (2 * icp) * 128 * t_tot + nt0,
                                        [[t_tot, 128], [128 * t_tot, 2],
                                         [1, nts]])))
                            kk += 1
                        if ni >= 1:
                            for k in range(ks_before[ni - 1], ks_before[ni]):
                                xb_issue(k)
                    for k in range(ks_before[nnt - 1], ks_before[nnt]):
                        xb_issue(k)

                @blk.gpsimd
                def _(g):
                    for s in ("SG0", "SG1"):
                        g.wait_ge(sh(s), sv(s))
                    for k, (ni, nt0, nts, ms) in enumerate(KS):
                        g.wait_ge(sh("DG"), bDGd + (k + 1))
                        row = nt0 + ms * 128 - tok_shift
                        ST.issue(g.dma_start(out_h[row:row + 128, :],
                                             ob[:, k % 2, :]))

                @blk.scalar
                def _(s):
                    def emit_square(k):
                        ni, nt0, nts, ms = KS[k]
                        s.wait_ge(sh("DG"), bDGd + (k + 1))
                        tt = (nt0 + ms * 128) // 128
                        inc(s.activation(ab[:, k % 2, :], ob[:, k % 2, :],
                                         AF.Square, accum_out=s_ssq(tt)),
                            "AB", 1)

                    kk = 0
                    for ni, (nt0, nts) in enumerate(NTl):
                        for icp in range(ICN2):
                            LH.wait(s, kk)
                            if kk >= 2:
                                s.wait_ge(sh("VB"), bVB + (kk - 1))
                            inc(s.activation(hq_ti[:, kk % 2, :, :nts],
                                             hl[:, kk % 2, :, :nts], AF.Copy,
                                             scale=sc_c(CSH)), "AA", 1)
                            kk += 1
                        if next_stats and ni >= 1:
                            for k in range(ks_before[ni - 1], ks_before[ni]):
                                emit_square(k)
                    if next_stats:
                        for k in range(ks_before[nnt - 1], ks_before[nnt]):
                            emit_square(k)

                @blk.vector
                def _(vec):
                    eg(vec)

                    def emit_evac(k):
                        ni, nt0, nts, ms = KS[k]
                        mts = min(128, nts - ms * 128)
                        pd0 = ps[2 * (k % 4)]
                        pd1 = ps[2 * (k % 4) + 1]
                        vec.wait_ge(sh("TA"), bTA + k + 1)
                        XB.wait(vec, k)
                        if k >= 2:
                            ST.wait(vec, k - 2)
                            if next_stats:
                                vec.wait_ge(sh("AB"), bAB + (k - 1))
                        for half, pd in ((0, pd0), (1, pd1)):
                            vec.scalar_tensor_tensor(
                                ob[:mts, k % 2, half * 512:(half + 1) * 512],
                                pd[:mts, :],
                                scal[:mts, SHS:SHS + 1],
                                xb[:mts, k % 2, half * 512:(half + 1) * 512],
                                op0=AO.mult, op1=AO.add)
                        vec.drain()
                        inc(vec.tensor_copy(t1[:, 50:51], t1[:, 50:51]),
                            "PFA", 2)
                        if final:
                            vec.tensor_tensor(ob[:mts, k % 2, :],
                                              ob[:mts, k % 2, :],
                                              vrow[:mts, :], op=AO.add)
                            vec.drain()
                        inc(vec.tensor_copy(t1[:, 51:52], t1[:, 51:52]),
                            "DG", 1)
                        if next_stats:
                            tt = (nt0 + ms * 128) // 128
                            vec.tensor_reduce(s_sum(tt), ob[:, k % 2, :],
                                              axis=AX.X, op=AO.add)
                            vec.drain()

                    kk = 0
                    for ni, (nt0, nts) in enumerate(NTl):
                        for icp in range(ICN2):
                            vec.wait_ge(sh("AA"), bAA + kk + 1)
                            if ni >= 2 and icp == 0:
                                vec.wait_ge(sh("TA"), bTA + ks_before[ni - 1])
                            inc(vec.tensor_scalar(
                                qh_v[:, ni % 2, 2 * icp:2 * icp + 2, 0:nts],
                                hq_ti[:, kk % 2, :, :nts],
                                -(QMAX + 1.0), QMAX, op0=AO.max, op1=AO.min),
                                "VB", 1)
                            vec.drain()
                            kk += 1
                        if ni >= 1:
                            for k in range(ks_before[ni - 1], ks_before[ni]):
                                emit_evac(k)
                    for k in range(ks_before[nnt - 1], ks_before[nnt]):
                        emit_evac(k)

                @blk.tensor
                def _(t):
                    LW.wait(t, 0)
                    for k, (ni, nt0, nts, ms) in enumerate(KS):
                        mts = min(128, nts - ms * 128)
                        pd0 = ps[2 * (k % 4)]
                        pd1 = ps[2 * (k % 4) + 1]
                        if k >= 4:
                            t.wait_ge(sh("PFA"), bPFA + 2 * (k - 3))
                        t.wait_ge(sh("VB"), bVB + (ni + 1) * ICN2)
                        inst = None
                        for ic in range(ICN):
                            t.matmul(pd0[:mts, :],
                                     qh_v[:, ni % 2, ic,
                                          ms * 128:ms * 128 + mts],
                                     wdT_v[:, ic, 0:512],
                                     start=(ic == 0), stop=(ic == ICN - 1))
                            inst = t.matmul(
                                pd1[:mts, :],
                                qh_v[:, ni % 2, ic,
                                     ms * 128:ms * 128 + mts],
                                wdT_v[:, ic, 512:1024],
                                start=(ic == 0), stop=(ic == ICN - 1))
                        inc(inst, "TA", 1)

        def attn_main():
            nnt = len(NT)
            OMS = []
            for ni, (nt0, nts) in enumerate(NT):
                OMS.append([ms for ms in range((nts + 127) // 128)
                            if nt0 + ms * 128 >= HALO])
            oms_before = [0] * (nnt + 1)
            for ni in range(nnt):
                oms_before[ni + 1] = oms_before[ni] + len(OMS[ni])

            def ko_of(ko):
                ni = next(i for i in range(nnt) if oms_before[i + 1] > ko)
                return ni, OMS[ni][ko - oms_before[ni]], NT[ni][0]

            with nc.Block() as blk:
                bTA, bTB, bTC = sv("TA"), sv("TB"), sv("TC")
                bAA, bVB, bVC = sv("AA"), sv("VB"), sv("VC")
                bPFA, bDG, bVA = sv("PFA"), sv("DG"), sv("VA")
                LW = Stm("LW")
                TR = Stm("TR", per=DC)
                XB = Stm("LB")
                XO = Stm("SG")
                LD = Stm("LH")
                CCQ = Stm("CQ")
                bCC = sv("CC")
                eg = entry_guard()

                @blk.sync
                def _(sync):
                    wait_all_stores(sync)
                    for kc in range(DC):
                        if kc >= 2:
                            sync.wait_ge(sh("VB"), bVB + (kc - 1))
                        LW.issue(sync.dma_start(
                            qt[:, kc % 2, :],
                            w1_in[kc * 128:(kc + 1) * 128, :]))
                    for kc in range(DC):
                        sync.wait_ge(sh("VB"), bVB + DC + kc - 1)
                        LW.issue(sync.dma_start(
                            qt[:, kc % 2, :],
                            w2_in[kc * 128:(kc + 1) * 128, :]))
                    ko = 0
                    for ni, (nt0, nts) in enumerate(NT):
                        if ni >= 3:
                            sync.wait_ge(sh("TA"), bTA + (ni - 2) * DC)
                            sync.wait_ge(sh("VC"), bVC + (ni - 2))
                        for hc in range(DC):
                            TR.issue(sync.dma_start(
                                x2f_v[:, ni % 3, hc, 0:nts],
                                qa_h[nt0:nt0 + nts, hc * 128:(hc + 1) * 128],
                                transpose=True))
                        for c in range(DC):
                            gstep = ni * DC + c
                            if gstep >= 2:
                                sync.wait_ge(sh("TB"), bTB + gstep - 1)
                            LD.issue(sync.dma_start(
                                diagc[:, gstep % 2, :, :],
                                bass.AP(diag_h.tensor,
                                        c * 128 * NTAP * 128,
                                        [[NTAP * 128, 128],
                                         [1, NTAP * 128]])))
                        for ms in OMS[ni]:
                            if ko >= 2:
                                skew = 2 if ko - 2 >= oms_before[nnt - 1] \
                                    else 0
                                sync.wait_ge(sh("VA"),
                                             bVA + (ko - 1) + skew)
                            XB.issue(sync.dma_start(
                                xb[:, ko % 2, :],
                                xp_h[nt0 + ms * 128:nt0 + ms * 128 + 128, :]))
                            ko += 1

                @blk.vector
                def _(vec):
                    eg(vec)
                    for kc in range(DC):
                        LW.wait(vec, kc)
                        vec.tensor_copy(w1_v[:, kc, :], qt[:, kc % 2, :])
                        vec.drain()
                        inc(vec.tensor_copy(t1[:, 0:1], t1[:, 0:1]), "VB", 1)
                    for kc in range(DC):
                        LW.wait(vec, DC + kc)
                        vec.tensor_copy(w2_v[:, kc, :], qt[:, kc % 2, :])
                        vec.drain()
                        inc(vec.tensor_copy(t1[:, 0:1], t1[:, 0:1]), "VB", 1)
                    ko = 0
                    for ni, (nt0, nts) in enumerate(NT):
                        lo = HALO if ni == 0 else 0
                        TR.wait(vec, ni)
                        vec.tensor_reduce(t2[:, 40:40 + DC],
                                          x2f_v[:, ni % 3, :, lo:nts],
                                          axis=AX.X, op=AO.add)
                        vec.drain()
                        inc(vec.tensor_tensor(tot8[:, :], tot8[:, :],
                                              t2[:, 40:40 + DC], op=AO.add),
                            "VC", 1)
                        vec.drain()
                        if ni == nnt - 1:
                            for b in range(4):
                                vec.tensor_scalar(
                                    ctr32[:, b * DC:(b + 1) * DC],
                                    tot8[:, :], bsel_b[:, b:b + 1],
                                    None, op0=AO.mult)
                            vec.drain()
                            inc(vec.tensor_copy(t1[:, 0:1], t1[:, 0:1]),
                                "VA", 1)
                            vec.wait_ge(sh("TA"), bTA + nnt * DC + 1)
                            inc(vec.tensor_copy(ar32[:, :],
                                                ps[6][0:32, 0:128]),
                                "VA", 1)
                            vec.drain()
                        if ni == 0:
                            vec.wait_ge(sh("AA"), bAA + DC)
                            for c in range(DC):
                                vec.tensor_scalar(pr_v[:, c, 128:256],
                                                  pr_v[:, c, 128:256],
                                                  sc_c(HMASK), None,
                                                  op0=AO.mult)
                            vec.drain()
                        for c in range(DC):
                            gstep = ni * DC + c
                            inc(vec.tensor_copy(t1[:, 0:1], t1[:, 0:1]),
                                "DG", 1)
                            if ni >= 1 and c == 0:
                                vec.wait_ge(sh("TC"), bTC + oms_before[ni])
                            vec.wait_ge(sh("TB"), bTB + gstep + 1)
                            vec.tensor_copy(z_v[:, c, 0:nts],
                                            ps[4 + (gstep % 2)][:, :nts])
                            vec.drain()
                            if ni < nnt - 1:
                                vec.tensor_copy(pr_v[:, c, 0:128],
                                                pr_v[:, c, nts:nts + 128])
                                vec.drain()
                            inc(vec.tensor_copy(t1[:, 1:2], t1[:, 1:2]),
                                "DG", 1)
                        for oi, ms in enumerate(OMS[ni]):
                            mts = min(128, nts - ms * 128)
                            po0 = ps[2 * (ko % 2)]
                            po1 = ps[2 * (ko % 2) + 1]
                            vec.wait_ge(sh("TC"), bTC + ko + 1)
                            XB.wait(vec, ko)
                            if ko >= 2:
                                XO.wait(vec, ko - 2)
                            for half, po in ((0, po0), (1, po1)):
                                vec.tensor_tensor(
                                    ob[:mts, ko % 2,
                                       half * 512:(half + 1) * 512],
                                    po[:mts, :],
                                    xb[:mts, ko % 2,
                                       half * 512:(half + 1) * 512],
                                    op=AO.add)
                            vec.drain()
                            inc(vec.tensor_copy(t1[:, 0:1], t1[:, 0:1]),
                                "PFA", 2)
                            inc(vec.tensor_copy(t1[:, 1:2], t1[:, 1:2]),
                                "VA", 1)
                            ko += 1

                @blk.scalar
                def _(s):
                    for ni, (nt0, nts) in enumerate(NT):
                        for mc in range(DC):
                            s.wait_ge(sh("TA"), bTA + ni * DC + mc + 1)
                            if ni >= 1:
                                s.wait_ge(sh("DG"), bDG + 2 * ((ni - 1) * DC
                                                               + mc + 1))
                            inc(s.activation(pr_v[:, mc, 128:128 + nts],
                                             ps[6 + (mc % 2)][:, :nts],
                                             AF.Identity,
                                             bias=b1f[:, mc:mc + 1],
                                             scale=1.0), "AA", 1)

                @blk.tensor
                def _(t):
                    for ni, (nt0, nts) in enumerate(NT):
                        t.wait_ge(sh("VB"), bVB + DC)
                        TR.wait(t, ni)
                        for mc in range(DC):
                            if ni * DC + mc >= 2:
                                t.wait_ge(sh("AA"), bAA + ni * DC + mc - 1)
                            inst = None
                            for kc in range(DC):
                                inst = t.matmul(
                                    ps[6 + (mc % 2)][:, :nts],
                                    w1_v[:, kc, mc * 128:(mc + 1) * 128],
                                    x2f_v[:, ni % 3, kc, 0:nts],
                                    start=(kc == 0), stop=(kc == DC - 1))
                            inc(inst, "TA", 1)
                        if ni == nnt - 1:
                            t.wait_ge(sh("VA"),
                                      bVA + oms_before[nnt - 1] + 1)
                            t.wait_ge(sh("AA"), bAA + nnt * DC)
                            inc(t.matmul(ps[6][0:32, 0:128], ctr32[:, :],
                                         idf[:, :], start=True, stop=True),
                                "TA", 1)
                        for c in range(DC):
                            gstep = ni * DC + c
                            LD.wait(t, gstep)
                            t.wait_ge(sh("DG"), bDG + 2 * gstep + 1)
                            t.wait_ge(sh("AA"), bAA + ni * DC + c + 1)
                            if ni == 0 and c == 0:
                                t.wait_ge(sh("GI"), 1)
                            if gstep >= 2:
                                t.wait_ge(sh("DG"), bDG + 2 * (gstep - 1))
                            pb = ps[4 + (gstep % 2)]
                            inst = None
                            for j in range(NTAP):
                                inst = t.matmul(
                                    pb[:, :nts], diagc[:, gstep % 2, j, :],
                                    pr_v[:, c, 128 - j:128 - j + nts],
                                    start=(j == 0), stop=(j == NTAP - 1))
                            inc(inst, "TB", 1)
                        t.wait_ge(sh("VB"), bVB + 2 * DC)
                        t.wait_ge(sh("DG"), bDG + 2 * (ni * DC + DC))
                        for oi, ms in enumerate(OMS[ni]):
                            mts = min(128, nts - ms * 128)
                            ko = oms_before[ni] + oi
                            po0 = ps[2 * (ko % 2)]
                            po1 = ps[2 * (ko % 2) + 1]
                            if ko >= 2:
                                t.wait_ge(sh("PFA"), bPFA + 2 * (ko - 1))
                            inst = None
                            for kc in range(DC):
                                t.matmul(po0[:mts, :],
                                         z_v[:, kc, ms * 128:ms * 128 + mts],
                                         w2_v[:, kc, 0:512],
                                         start=(kc == 0), stop=(kc == DC - 1))
                                inst = t.matmul(
                                    po1[:mts, :],
                                    z_v[:, kc, ms * 128:ms * 128 + mts],
                                    w2_v[:, kc, 512:1024],
                                    start=(kc == 0), stop=(kc == DC - 1))
                            inc(inst, "TC", 1)

                @blk.gpsimd
                def _(g):
                    kom1 = oms_before[nnt - 1]
                    for ko in range(oms_before[nnt]):
                        if ko == kom1:
                            g.wait_ge(sh("VA"), bVA + kom1 + 2)
                            CCQ.issue(g.dma_start(cc_t["i3"].ap()[:, :],
                                                  ar32[:, :]))
                            CCQ.wait(g, 0)
                            inc(g.collective_compute(
                                "AllReduce", AO.add,
                                replica_groups=[list(range(8))],
                                ins=[cc_t["i3"].ap().opt()],
                                outs=[cc_t["o3"].ap().opt()]), "CC", 1)
                            g.wait_ge(sh("CC"), bCC + 1)
                            CCQ.issue(g.dma_start(ar32[:, :],
                                                  cc_t["o3"].ap()[:, :]))
                        ni, ms, nt0 = ko_of(ko)
                        skew = 2 if ko >= kom1 else 0
                        g.wait_ge(sh("VA"), bVA + (ko + 1) + skew)
                        row = nt0 + ms * 128 - HALO
                        XO.issue(g.dma_start(xpp_h[row:row + 128, :],
                                             ob[:, ko % 2, :]))

        def ar3_vrow():
            with nc.Block() as blk:
                bTA, bVA = sv("TA"), sv("VA")
                eg = entry_guard()

                @blk.vector
                def _(vec):
                    eg(vec)
                    vec.wait_ge(sh("TA"), bTA + 1)
                    inc(vec.tensor_copy(allT[:, :], ps[7][:, 0:32]), "VA", 1)
                    vec.drain()
                    vec.tensor_scalar(t2[:, 0:DC], allT[:, 0:DC],
                                      bsel_b[:, 0:1], None, op0=AO.mult)
                    vec.drain()
                    for b in range(1, 4):
                        vec.scalar_tensor_tensor(
                            t2[:, 0:DC], allT[:, b * DC:(b + 1) * DC],
                            bsel_b[:, b:b + 1], t2[:, 0:DC],
                            op0=AO.mult, op1=AO.add)
                        vec.drain()
                    vec.tensor_copy(tot8[:, :], t2[:, 0:DC])
                    vec.drain()
                    inc(vec.tensor_copy(t8b[:, :], tot8[:, :]), "VA", 1)
                    vec.drain()
                    for mc in range(DC):
                        vec.wait_ge(sh("TA"), bTA + 2 + mc)
                        vec.tensor_tensor(totf[:, mc:mc + 1],
                                          ps[mc % 2][:, 0:1],
                                          b1T[:, mc:mc + 1], op=AO.add)
                        vec.drain()
                        inc(vec.tensor_copy(t1[:, 0:1], t1[:, 0:1]), "VA", 1)
                    vec.tensor_tensor(u8[:, :], totf[:, :], mwf[:, :, 0],
                                      op=AO.mult)
                    vec.drain()
                    inc(vec.tensor_copy(u8b[:, :], u8[:, :]), "VA", 1)
                    vec.drain()
                    vec.wait_ge(sh("TA"), bTA + DC + 3)
                    vec.tensor_copy(rowD[0:1, 0:512], ps[6][0:1, :])
                    vec.tensor_copy(rowD[0:1, 512:1024], ps[7][0:1, :])
                    vec.drain()
                    vec.tensor_tensor(rowD[0:1, :], rowD[0:1, :],
                                      rowD2[0:1, :], op=AO.add)
                    vec.drain()
                    inc(vec.tensor_copy(t1[:, 0:1], t1[:, 0:1]), "VA", 1)
                    for half in range(2):
                        vec.wait_ge(sh("TA"), bTA + DC + 4 + half)
                        vec.tensor_copy(vrow[:, half * 512:(half + 1) * 512],
                                        ps[4 + half][:, 0:512])
                        vec.drain()

                @blk.tensor
                def _(t):
                    for s_ in ("CQ0", "CQ1"):
                        t.wait_ge(sh(s_), sv(s_))
                    inc(t.matmul(ps[7][:, 0:32], ar32[:, :], idf[0:32, 0:32],
                                 start=True, stop=True), "TA", 1)
                    t.wait_ge(sh("VA"), bVA + 2)
                    for mc in range(DC):
                        if mc >= 2:
                            t.wait_ge(sh("VA"), bVA + 2 + (mc - 1))
                        inst = None
                        for kc in range(DC):
                            inst = t.matmul(ps[mc % 2][:, 0:1],
                                            w1_v[:, kc,
                                                 mc * 128:(mc + 1) * 128],
                                            t8b[:, kc:kc + 1],
                                            start=(kc == 0),
                                            stop=(kc == DC - 1))
                        inc(inst, "TA", 1)
                    t.wait_ge(sh("VA"), bVA + DC + 3)
                    for half, bank in ((0, 6), (1, 7)):
                        inst = None
                        for kc in range(DC):
                            inst = t.matmul(ps[bank][0:1, :],
                                            u8b[:, kc:kc + 1],
                                            w2_v[:, kc,
                                                 half * 512:(half + 1) * 512],
                                            start=(kc == 0),
                                            stop=(kc == DC - 1))
                        inc(inst, "TA", 1)
                    t.wait_ge(sh("VA"), bVA + DC + 4)
                    for half, bank in ((0, 4), (1, 5)):
                        inc(t.matmul(ps[bank][:, 0:512], onesf[:, :],
                                     rowD[0:1, half * 512:(half + 1) * 512],
                                     start=True, stop=True), "TA", 1)

                @blk.gpsimd
                def _(g):
                    g.memset(gy[:, :], 0.0)
                    g.memset(gh[:, :], 0.0)

        def _go():
            armax(gy, cc_t["i1"], cc_t["o1"], s1r[0:1, 0:1], sc_c(CSX),
                  sc_c(SHS))
            quant_apply(x_in, n128, 0, 1, sc_c(CSX), False, True,
                        qxt_tok=t_tot, ph0d=True)
            if stop_after == "quant1":
                return
            ffn_gateup(t_tot)
            if stop_after == "gateup1":
                return
            armax(gh, cc_t["i2"], cc_t["o2"], s1r[0:1, 1:2], sc_c(CSH),
                  sc_c(SHS))
            ffn_down(t_tot, x_in, xp_h, True, False, 0)
            if stop_after == "ffn1":
                return
            ln_batch_mr(n128, 2, 3)
            quant_apply(xp_h, n128, 2, 3, None, False, False)
            if stop_after == "x2q":
                return
            attn_main()
            if stop_after == "attn":
                return
            ar3_vrow()
            if stop_after == "ar3":
                return
            ln_stats_ymax(xpp_h, n128o, 4, 5, True, gy)
            armax(gy, cc_t["i4"], cc_t["o4"], s1r[0:1, 0:1], sc_c(CSX),
                  sc_c(SHS))
            quant_apply(xpp_h, n128o, 4, 5, sc_c(CSX), True, True,
                        qxt_tok=t_own)
            if stop_after == "quant2":
                return
            ffn_gateup(t_own)
            if stop_after == "gateup2":
                return
            armax(gh, cc_t["i5"], cc_t["o5"], s1r[0:1, 1:2], sc_c(CSH),
                  sc_c(SHS))
            ffn_down(t_own, xpp_h, out_ext, False, True, 0)

        _go()

        if dbg:
            with nc.Block() as blk:
                DQ = Stm("SQ")

                @blk.scalar
                def _(s):
                    DQ.issue(s.dma_start(dbg_h[:, 0:8], scal[:, :]))
                    DQ.issue(s.dma_start(dbg_h[:, 8:16], vrow[:, 0:8]))
                    DQ.issue(s.dma_start(dbg_h[:, 16:24], vrow[:, 8:16]))
                    DQ.issue(s.dma_start(dbg_h[:, 24:32], tot8[:, :]))
                    DQ.issue(s.dma_start(dbg_h[0:1, 32:64], s1r[:, :]))
                    DQ.wait(s, 4)

        with nc.Block() as blk:
            @blk.gpsimd
            def _(g):
                g.wait_ge(sh("SO0"), sv("SO0"))
                g.wait_ge(sh("SO1"), sv("SO1"))

    return nc


def kernel(**inputs):
    x = np.ascontiguousarray(np.asarray(inputs["L0_attn_input_data"],
                                        dtype=np.float32))
    B, S, _ = x.shape
    t_own = S // 2
    nc = build_nc(t_own)

    w1 = np.ascontiguousarray(inputs["L0_attn_W1"], dtype=np.float32)
    b1 = np.ascontiguousarray(inputs["L0_attn_bias1"], dtype=np.float32)
    w2 = np.ascontiguousarray(inputs["L0_attn_W2"], dtype=np.float32)
    b2 = np.ascontiguousarray(inputs["L0_attn_bias2"], dtype=np.float32)
    mw = np.ascontiguousarray(inputs["L0_attn_memory_weights"],
                              dtype=np.float32)
    cx = np.asarray(inputs["L0_ffn_clip_x"], dtype=np.float32).reshape(1, 1)
    cd = np.asarray(inputs["L0_ffn_clip_down"],
                    dtype=np.float32).reshape(1, 1)
    wg = np.ascontiguousarray(inputs["L0_ffn_w_gate"], dtype=np.float32)
    wu = np.ascontiguousarray(inputs["L0_ffn_w_up"], dtype=np.float32)
    wd = np.ascontiguousarray(inputs["L0_ffn_w_down"], dtype=np.float32)

    in_maps = []
    for c in range(8):
        b, h = c // 2, c % 2
        if h == 0:
            halo = np.zeros((HALO, D), dtype=np.float32)
            hmv = 0.0
        else:
            halo = x[b, t_own - HALO:t_own]
            hmv = 1.0
        shard = np.concatenate([halo, x[b, h * t_own:(h + 1) * t_own]],
                               axis=0)
        bsel = np.zeros((1, 4), dtype=np.float32)
        bsel[0, b] = 1.0
        in_maps.append({
            "x": np.ascontiguousarray(shard),
            "hmask": np.array([[hmv]], dtype=np.float32),
            "bsel": bsel,
            "W1": w1, "b1": b1, "W2": w2, "b2": b2, "mw": mw,
            "clip_x": cx, "clip_down": cd,
            "w_gate": wg, "w_up": wu, "w_down": wd,
        })

    res = run_bass_kernel_spmd(nc, in_maps, core_ids=list(range(8)))
    out = np.empty((B, S, D), dtype=np.float32)
    for c in range(8):
        b, h = c // 2, c % 2
        out[b, h * t_own:(h + 1) * t_own] = res.results[c]["out"]
    return out


# version 11
# speedup vs baseline: 1.0189x; 1.0189x over previous
import contextlib

import numpy as np

import concourse.bass as bass
import concourse.mybir as mybir
from concourse.alu_op_type import AluOpType as AO
from concourse.bass_utils import run_bass_kernel_spmd

F32 = mybir.dt.float32
BF16 = mybir.dt.bfloat16
I32 = mybir.dt.int32
AF = mybir.ActivationFunctionType
AX = mybir.AxisListType

D = 1024
IFF = 2816
DC = D // 128
ICN = IFF // 128
NTAP = 40
QMAX = 127.0
EPS = 1e-5
HALO = 128
CLIP_LO = -(QMAX + 1.49)
CLIP_HI = QMAX + 0.49


def _tiles(total, step=512):
    out, t = [], 0
    while t < total:
        out.append((t, min(step, total - t)))
        t += step
    return out


def build_nc(t_own, dbg=False, stop_after=None):
    nc = bass.Bass("TRN2", target_bir_lowering=False)
    t_tot = t_own + HALO
    n128 = t_tot // 128
    n128o = t_own // 128
    NT = _tiles(t_tot)
    TSEQ = 2 * t_own

    x_in = nc.dram_tensor("x", [t_tot, D], F32, kind="ExternalInput").ap()
    hm_in = nc.dram_tensor("hmask", [1, 1], F32, kind="ExternalInput").ap()
    bsel_in = nc.dram_tensor("bsel", [1, 4], F32, kind="ExternalInput").ap()
    w1_in = nc.dram_tensor("W1", [D, D], F32, kind="ExternalInput").ap()
    b1_in = nc.dram_tensor("b1", [D], F32, kind="ExternalInput").ap()
    w2_in = nc.dram_tensor("W2", [D, D], F32, kind="ExternalInput").ap()
    b2_in = nc.dram_tensor("b2", [D], F32, kind="ExternalInput").ap()
    mw_in = nc.dram_tensor("mw", [NTAP + 1, D], F32, kind="ExternalInput").ap()
    cx_in = nc.dram_tensor("clip_x", [1, 1], F32, kind="ExternalInput").ap()
    cd_in = nc.dram_tensor("clip_down", [1, 1], F32, kind="ExternalInput").ap()
    wg_in = nc.dram_tensor("w_gate", [IFF, D], F32, kind="ExternalInput").ap()
    wu_in = nc.dram_tensor("w_up", [IFF, D], F32, kind="ExternalInput").ap()
    wd_in = nc.dram_tensor("w_down", [D, IFF], F32, kind="ExternalInput").ap()
    out_ext = nc.dram_tensor("out", [t_own, D], F32, kind="ExternalOutput").ap()

    KI = "ExternalOutput" if dbg else "Internal"
    xp_h = nc.dram_tensor("xp_h", [t_tot, D], F32, kind=KI).ap()
    xpp_h = nc.dram_tensor("xpp_h", [t_own, D], F32, kind=KI).ap()
    qa_h = nc.dram_tensor("qa_h", [t_tot, D], BF16, kind=KI).ap()
    h_h = nc.dram_tensor("h_h", [IFF, t_tot], F32, kind=KI).ap()
    dbg_h = nc.dram_tensor("dbg_h", [128, 64], F32, kind=KI).ap() if dbg \
        else None
    qwg_h = nc.dram_tensor("qwg_h", [IFF, D], BF16, kind="Internal").ap()
    qwu_h = nc.dram_tensor("qwu_h", [IFF, D], BF16, kind="Internal").ap()
    qwd_h = nc.dram_tensor("qwd_h", [D, IFF], BF16, kind="Internal").ap()
    wgT_h = nc.dram_tensor("wgT_h", [D, IFF], BF16, kind="Internal").ap()
    wuT_h = nc.dram_tensor("wuT_h", [D, IFF], BF16, kind="Internal").ap()
    wdT_h = nc.dram_tensor("wdT_h", [IFF, D], BF16, kind="Internal").ap()
    misc_h = nc.dram_tensor("misc_h", [4096], F32, kind="Internal").ap()
    diag_h = nc.dram_tensor("diag_h", [DC * 128 * NTAP * 128], BF16,
                            kind="Internal").ap()
    cc_t = {}
    for i, shp in (("1", [1, 1]), ("2", [1, 1]), ("3", [32, 128]),
                   ("4", [1, 1]), ("5", [1, 1])):
        cc_t["i" + i] = nc.dram_tensor("cc%si" % i, shp, F32, kind="Internal")
        cc_t["o" + i] = nc.dram_tensor("cc%so" % i, shp, F32, kind="Internal")

    ctx = contextlib.ExitStack()
    with ctx:
        sbuf = lambda name, shape, dt: ctx.enter_context(
            nc.sbuf_tensor(name, shape, dt))

        AR_N = max(DC * t_tot, 2 * ICN * 512 + ICN * 1024,
                   2 * DC * 1024 + 2 * DC * 512 + DC * 640 + DC * 512)
        arena = sbuf("arena", [128, AR_N], BF16)
        qx_fm = arena[:, :DC * t_tot].rearrange("p (c t) -> p c t", c=DC)
        qh_v = arena[:, :2 * ICN * 512].rearrange(
            "p (b c t) -> p b c t", b=2, c=ICN)
        wdT_v = arena[:, 2 * ICN * 512:2 * ICN * 512 + ICN * 1024].rearrange(
            "p (c t) -> p c t", c=ICN)
        _o = 0
        w1_v = arena[:, _o:_o + DC * 1024].rearrange("p (c t) -> p c t", c=DC)
        _o += DC * 1024
        w2_v = arena[:, _o:_o + DC * 1024].rearrange("p (c t) -> p c t", c=DC)
        _o += DC * 1024
        x2f_v = arena[:, _o:_o + 2 * DC * 512].rearrange(
            "p (b c t) -> p b c t", b=2, c=DC)
        _o += 2 * DC * 512
        pr_v = arena[:, _o:_o + DC * 640].rearrange("p (c t) -> p c t", c=DC)
        _o += DC * 640
        z_v = arena[:, _o:_o + DC * 512].rearrange("p (c t) -> p c t", c=DC)
        _o += DC * 512
        stg_v = arena[:, :IFF]
        stg2_v = arena[:, IFF:IFF + 1024]

        xa = sbuf("xa", [128, 2, D], F32)
        xb = sbuf("xb", [128, 2, D], F32)
        qt = sbuf("qt", [128, 2, D], F32)
        qb = sbuf("qb", [128, 2, D], BF16)
        ab = sbuf("ab", [128, 2, D], F32)
        ob = sbuf("ob", [128, 2, D], F32)
        wgc = sbuf("wgc", [128, 2, DC, 128], BF16)
        wuc = sbuf("wuc", [128, 2, DC, 128], BF16)
        st_t = sbuf("st_t", [128, 4, 512], F32)
        hlu = sbuf("hlu", [128, 2, 2, 512], F32)
        ht = hlu[:, 0, :, :]
        hl = hlu
        hq_ti = sbuf("hq_ti", [128, 2, 2, 512], I32)
        qi = hq_ti[:, 0].rearrange("p a b -> p (a b)")
        diagc = sbuf("diagc", [128, 2, NTAP, 128], BF16)
        idf = sbuf("idf", [128, 128], F32)
        idb = sbuf("idb", [128, 128], BF16)
        onesf = sbuf("onesf", [1, 128], F32)
        io1 = sbuf("io1", [128, 128], I32)
        io2 = sbuf("io2", [128, 128], I32)

        mstat = sbuf("mstat", [128, 6 * n128], F32)
        mr = lambda w, t: mstat[:, w * n128 + t:w * n128 + t + 1]
        mrr = lambda w, n: mstat[:, w * n128:w * n128 + n]
        sums = sbuf("sums", [128, 2 * n128], F32)
        s_sum = lambda t: sums[:, t:t + 1]
        s_ssq = lambda t: sums[:, n128 + t:n128 + t + 1]
        sgu = sbuf("sgu", [128, 2 * ICN + DC], F32)
        sgc = lambda i: sgu[:, i:i + 1]
        suc = lambda i: sgu[:, ICN + i:ICN + i + 1]
        sdc = lambda i: sgu[:, 2 * ICN + i:2 * ICN + i + 1]
        cgu = sbuf("cgu", [128, 2 * ICN], F32)
        cgc = lambda i: cgu[:, i:i + 1]
        cuc = lambda i: cgu[:, ICN + i:ICN + i + 1]
        mwf = sbuf("mwf", [128, DC, NTAP + 1], F32)
        mwe = sbuf("mwe", [128, DC, NTAP], F32)
        b8 = sbuf("b8", [8, 2 * 128], F32)
        b1f = sbuf("b1f", [128, DC], F32)
        b2f = sbuf("b2f", [128, DC], F32)
        b1T = sbuf("b1T", [128, DC], F32)
        scal = sbuf("scal", [128, 8], F32)
        CSX, CSH, HMASK, SHS = 0, 1, 2, 3
        sc_c = lambda i: scal[:, i:i + 1]
        s1r = sbuf("s1r", [1, 32], F32)
        vrow = sbuf("vrow", [128, D], F32)
        tot8 = sbuf("tot8", [128, DC], F32)
        ctr32 = sbuf("ctr32", [128, 32], F32)
        ar32 = sbuf("ar32", [32, 128], F32)
        allT = sbuf("allT", [128, 32], F32)
        bsel_b = sbuf("bsel_b", [128, 4], F32)
        rowD = sbuf("rowD", [1, D], F32)
        rowD2 = sbuf("rowD2", [1, D], F32)
        t1 = sbuf("t1", [128, 64], F32)
        t2 = sbuf("t2", [128, 64], F32)
        t3 = sbuf("t3", [128, 64], F32)
        t4 = sbuf("t4", [128, 64], F32)
        gy = sbuf("gy", [128, 1], F32)
        gh = sbuf("gh", [128, 1], F32)
        u8 = sbuf("u8", [128, DC], F32)
        u8b = sbuf("u8b", [128, DC], BF16)
        t8b = sbuf("t8b", [128, DC], BF16)
        totf = sbuf("totf", [128, DC], F32)

        ps = [ctx.enter_context(nc.psum_tensor("ps%d" % i, [128, 512], F32))
              for i in range(8)]

        sems = {}
        for s in ["VA", "VB", "VC", "AA", "AB", "TA", "TB", "TC",
                  "PFA", "DG", "CC", "GI"]:
            sems[s] = [ctx.enter_context(nc.semaphore(s)), 0]
        for s in ["LA", "LB", "LW", "TR", "SQ", "SG", "SO", "LH", "CQ"]:
            sems[s + "0"] = [ctx.enter_context(nc.semaphore(s + "0")), 0]
            sems[s + "1"] = [ctx.enter_context(nc.semaphore(s + "1")), 0]

        def inc(inst, s, n):
            sems[s][1] += n
            inst.then_inc(sems[s][0], n)
            return sems[s][1]

        def sh(s):
            return sems[s][0]

        def sv(s):
            return sems[s][1]

        class Stm:

            def __init__(self, name, per=1):
                self.k = [name + "0", name + "1"]
                self.b = [sv(self.k[0]), sv(self.k[1])]
                self.per = per
                self.d = 0

            def issue(self, inst):
                step = self.d // self.per
                inc(inst, self.k[step % 2], 16)
                self.d += 1

            def wait(self, eng, step):
                eng.wait_ge(sh(self.k[step % 2]),
                            self.b[step % 2] + 16 * self.per * (step // 2 + 1))

            def wait_all(self, eng):
                eng.wait_ge(sh(self.k[0]), sv(self.k[0]))
                eng.wait_ge(sh(self.k[1]), sv(self.k[1]))

        def wait_all_stores(eng):
            for s in ("SQ0", "SQ1", "SG0", "SG1", "SO0", "SO1"):
                eng.wait_ge(sh(s), sv(s))

        def entry_guard():
            vals = {s: sv(s) for s in ("SQ0", "SQ1", "SG0", "SG1",
                                       "SO0", "SO1")}

            def g(eng):
                for s, v in vals.items():
                    eng.wait_ge(sh(s), v)
            return g

        with nc.Block() as blk:
            L0a = Stm("LW")

            @blk.sync
            def _(sync):
                L0a.issue(sync.dma_start(qt[0:NTAP + 1, 0, :], mw_in[:, :]))
                L0a.issue(sync.dma_start(
                    b8[0:8, 0:128], b1_in.rearrange("(c p) -> c p", p=128)))
                L0a.issue(sync.dma_start(
                    b8[0:8, 128:256], b2_in.rearrange("(c p) -> c p", p=128)))
                sync.wait_ge(sh("GI"), 1)
                L0a.issue(sync.dma_start(s1r[0:1, 0:1], cx_in[:, :]))
                L0a.issue(sync.dma_start(s1r[0:1, 1:2], cd_in[:, :]))
                L0a.issue(sync.dma_start(s1r[0:1, 2:3], hm_in[:, :]))
                L0a.issue(sync.dma_start(s1r[0:1, 4:8], bsel_in[:, :]))

            @blk.gpsimd
            def _(g):
                g.iota(io1[:, :], [[1, 128]], channel_multiplier=0)
                g.iota(io2[:, :], [[0, 128]], channel_multiplier=1)
                g.memset(onesf[:, :], 1.0)
                g.memset(gy[:, :], 0.0)
                g.memset(gh[:, :], 0.0)
                g.memset(tot8[:, :], 0.0)
                g.memset(s1r[:, :], 0.0)
                g.memset(t1[:, :], 0.0)
                g.memset(scal[:, :], 0.0)
                g.memset(vrow[:, :], 0.0)
                inc(g.memset(pr_v[:, :, :], 0.0), "GI", 1)

            @blk.vector
            def _(vec):
                vec.wait_ge(sh("GI"), 1)
                vec.tensor_tensor(idf[:, :], io1[:, :], io2[:, :],
                                  op=AO.is_equal)
                vec.drain()
                vec.tensor_copy(idb[:, :], idf[:, :])
                vec.drain()

        with nc.Block() as blk:
            bTA, bVA = sv("TA"), sv("VA")
            SQb = Stm("SQ")

            @blk.tensor
            def _(t):
                L0a.wait_all(t)
                for c in range(DC):
                    inc(t.matmul(ps[c][:, 0:NTAP + 1],
                                 qt[0:NTAP + 1, 0, c * 128:(c + 1) * 128],
                                 idf[0:NTAP + 1, 0:NTAP + 1],
                                 start=True, stop=True), "TA", 1)
                t.wait_ge(sh("VA"), bVA + 1)
                inc(t.matmul(ps[0][:, 0:DC], b8[0:8, 0:128], idf[0:8, 0:8],
                             start=True, stop=True), "TA", 1)
                inc(t.matmul(ps[1][:, 0:DC], b8[0:8, 128:256], idf[0:8, 0:8],
                             start=True, stop=True), "TA", 1)
                inc(t.matmul(ps[2][:, 0:1], onesf[:, :], s1r[0:1, 2:3],
                             start=True, stop=True), "TA", 1)
                inc(t.matmul(ps[3][:, 0:4], onesf[:, :], s1r[0:1, 4:8],
                             start=True, stop=True), "TA", 1)
                t.wait_ge(sh("VA"), bVA + 2)
                inc(t.matmul(ps[4][0:DC, 0:128], b2f[:, :], idf[:, :],
                             start=True, stop=True), "TA", 1)

            @blk.vector
            def _(vec):
                vec.wait_ge(sh("TA"), bTA + DC)
                for c in range(DC):
                    vec.tensor_copy(mwf[:, c, :], ps[c][:, 0:NTAP + 1])
                vec.drain()
                inc(vec.tensor_copy(t1[:, 0:1], mwf[:, 0, 0:1]), "VA", 1)
                vec.wait_ge(sh("TA"), bTA + DC + 4)
                vec.tensor_copy(b1f[:, :], ps[0][:, 0:DC])
                vec.tensor_copy(b2f[:, :], ps[1][:, 0:DC])
                vec.tensor_copy(sc_c(HMASK), ps[2][:, 0:1])
                vec.tensor_copy(bsel_b[:, :], ps[3][:, 0:4])
                vec.drain()
                vec.tensor_scalar(b1T[:, :], b1f[:, :], float(TSEQ), None,
                                  op0=AO.mult)
                for c in range(DC):
                    vec.tensor_scalar(mwe[:, c, :], mwf[:, c, 1:],
                                      mwf[:, c, 0:1], None, op0=AO.subtract)
                vec.drain()
                inc(vec.tensor_copy(t1[:, 0:1], b1f[:, 0:1]), "VA", 1)
                vec.wait_ge(sh("TA"), bTA + DC + 5)
                inc(vec.tensor_copy(ar32[0:DC, :], ps[4][0:DC, 0:128]),
                    "VA", 1)
                vec.drain()

            @blk.scalar
            def _(s):
                s.wait_ge(sh("VA"), bVA + 3)
                SQb.issue(s.dma_start(misc_h[1024:1024 + D].rearrange(
                    "(c p) -> c p", p=128), ar32[0:DC, :]))
                SQb.wait(s, 0)
                SQb.issue(s.dma_start(rowD2[0:1, :],
                                      misc_h[None, 1024:1024 + D]))
                SQb.wait(s, 1)

        with nc.Block() as blk:
            bVA = sv("VA")
            SQp = Stm("SQ")

            @blk.vector
            def _(vec):
                for c in range(DC):
                    if c >= 2:
                        SQp.wait(vec, c - 2)
                    for j in range(NTAP):
                        vec.tensor_scalar(diagc[:, c % 2, j, :], idb[:, :],
                                          mwe[:, c, j:j + 1], None,
                                          op0=AO.mult)
                    vec.drain()
                    inc(vec.tensor_copy(t1[:, 0:1], t1[:, 0:1]), "VA", 1)
                    vec.drain()

            @blk.scalar
            def _(s):
                for c in range(DC):
                    s.wait_ge(sh("VA"), bVA + c + 1)
                    SQp.issue(s.dma_start(
                        bass.AP(diag_h.tensor, c * 128 * NTAP * 128,
                                [[NTAP * 128, 128], [1, NTAP * 128]]),
                        diagc[:, c % 2, :, :]))

        def head_merged():
            PCS = [(0, 1024), (1024, 1024), (2048, 768)]
            qunits = ([('g', t) for t in range(ICN)] +
                      [('u', t) for t in range(ICN)] +
                      [('d', t) for t in range(DC)])
            U = []
            li = qi_ = 0
            nqu = len(qunits)
            while li < n128 or qi_ < nqu:
                take_q = 1 if qi_ < nqu else 0
                if take_q and (li >= n128 or qi_ * n128 <= li * nqu):
                    U.append(qunits[qi_])
                    qi_ += 1
                else:
                    U.append(('l', li))
                    li += 1
            la_steps = {}
            qstep_of = {}
            la = 0
            qs = 0
            for u in U:
                kind, i = u
                if kind in ('g', 'u'):
                    la_steps[u] = [la]
                    la += 1
                    qstep_of[u] = [qs]
                    qs += 1
                elif kind == 'd':
                    qstep_of[u] = [qs, qs + 1, qs + 2]
                    qs += 3
            nq = qs
            wd_stage = [(qt[:, 0, :], qt[:, 1, :]), None]

            with nc.Block() as blk:
                bVA, bVB, bVC = sv("VA"), sv("VB"), sv("VC")
                bAA, bAB, bTB, bTC = sv("AA"), sv("AB"), sv("TB"), sv("TC")
                LA, SQ = Stm("LA"), Stm("SQ")
                LB = Stm("CQ")
                LDW = Stm("LH", per=3)
                eg = entry_guard()

                @blk.sync
                def _(sync):
                    wait_all_stores(sync)
                    for u in U:
                        kind, i = u
                        if kind in ('g', 'u'):
                            q = qstep_of[u][0]
                            if q >= 2:
                                sync.wait_ge(sh("VA"), bVA + (q - 1))
                            wap = wg_in if kind == 'g' else wu_in
                            LA.issue(sync.dma_start(
                                xa[:, q % 2, :],
                                wap[i * 128:(i + 1) * 128, :]))
                        elif kind == 'd':
                            if i >= 1:
                                sync.wait_ge(sh("VB"), bVB + i)
                            for p3, (o3, n3) in enumerate(PCS):
                                dst = (qt[:, 0, :n3], qt[:, 1, :n3],
                                       ob[:, 0, :n3])[p3]
                                LDW.issue(sync.dma_start(
                                    dst,
                                    wd_in[i * 128:(i + 1) * 128,
                                          o3:o3 + n3]))

                @blk.gpsimd
                def _(g):
                    for u in U:
                        kind, i = u
                        if kind == 'l':
                            t = i
                            if t >= 2:
                                g.wait_ge(sh("AA"), bAA + 2 * t - 2)
                            LB.issue(g.dma_start(
                                xb[:, t % 2, :],
                                x_in[t * 128:(t + 1) * 128, :]))

                @blk.vector
                def _(vec):
                    eg(vec)
                    for u in U:
                        kind, i = u
                        if kind in ('g', 'u'):
                            q = qstep_of[u][0]
                            LA.wait(vec, la_steps[u][0])
                            if q >= 2:
                                SQ.wait(vec, q - 2)
                            w_t = xa[:, q % 2, :]
                            scol = sgc if kind == 'g' else suc
                            vec.tensor_reduce(t1[:, 0:1], w_t, axis=AX.X,
                                              op=AO.max,
                                              apply_absolute_value=True)
                            vec.drain()
                            vec.reciprocal(t1[:, 1:2], t1[:, 0:1])
                            vec.drain()
                            vec.tensor_scalar(scol(i), t1[:, 0:1],
                                              1.0 / QMAX, None, op0=AO.mult)
                            vec.tensor_scalar(t1[:, 2:3], t1[:, 1:2], QMAX,
                                              None, op0=AO.mult)
                            vec.drain()
                            vec.tensor_scalar(qi[:, :], w_t, t1[:, 2:3],
                                              None, op0=AO.mult)
                            vec.drain()
                            inc(vec.tensor_copy(qb[:, q % 2, :], qi[:, :]),
                                "VA", 1)
                            vec.drain()
                        elif kind == 'd':
                            LDW.wait(vec, i)
                            srcs = (qt[:, 0, :], qt[:, 1, :], ob[:, 0, :])
                            for p3, (o3, n3) in enumerate(PCS):
                                vec.tensor_reduce(t1[:, 3 + p3:4 + p3],
                                                  srcs[p3][:, :n3], axis=AX.X,
                                                  op=AO.max,
                                                  apply_absolute_value=True)
                                vec.drain()
                            vec.tensor_reduce(t1[:, 6:7], t1[:, 3:6],
                                              axis=AX.X, op=AO.max)
                            vec.drain()
                            vec.reciprocal(t1[:, 7:8], t1[:, 6:7])
                            vec.drain()
                            vec.tensor_scalar(sdc(i), t1[:, 6:7], 1.0 / QMAX,
                                              None, op0=AO.mult)
                            vec.tensor_scalar(t1[:, 8:9], t1[:, 7:8], QMAX,
                                              None, op0=AO.mult)
                            vec.drain()
                            vec.tensor_scalar(t1[:, 9:10], sdc(i), 0.5, None,
                                              op0=AO.mult)
                            vec.drain()
                            for p3, (o3, n3) in enumerate(PCS):
                                q = qstep_of[u][p3]
                                if q >= 2:
                                    SQ.wait(vec, q - 2)
                                vec.tensor_scalar(qi[:, :n3],
                                                  srcs[p3][:, :n3],
                                                  t1[:, 8:9], None,
                                                  op0=AO.mult)
                                vec.drain()
                                inc(vec.tensor_scalar(qb[:, q % 2, :n3],
                                                      qi[:, :n3],
                                                      t1[:, 9:10], None,
                                                      op0=AO.mult), "VA", 1)
                                vec.drain()
                            inc(vec.tensor_copy(t1[:, 0:1], t1[:, 0:1]),
                                "VB", 1)
                        else:
                            t = i
                            c = 10 + 8 * (t % 2)
                            LB.wait(vec, t)
                            inc(vec.tensor_reduce(s_sum(t), xb[:, t % 2, :],
                                                  axis=AX.X, op=AO.add),
                                "VC", 1)
                            vec.wait_ge(sh("AA"), bAA + 2 * t + 1)
                            vec.tensor_scalar(mr(0, t), s_sum(t), 1.0 / D,
                                              None, op0=AO.mult)
                            vec.tensor_scalar(t1[:, c + 1:c + 2], s_ssq(t),
                                              1.0 / D, EPS, op0=AO.mult,
                                              op1=AO.add)
                            vec.drain()
                            vec.tensor_tensor(t1[:, c:c + 1], mr(0, t),
                                              mr(0, t), op=AO.mult)
                            vec.drain()
                            vec.tensor_tensor(t1[:, c + 2:c + 3],
                                              t1[:, c + 1:c + 2],
                                              t1[:, c:c + 1],
                                              op=AO.subtract)
                            vec.drain()
                            inc(vec.reciprocal(t1[:, c + 3:c + 4],
                                               t1[:, c + 2:c + 3]), "TB", 1)
                            vec.drain()
                            vec.wait_ge(sh("AB"), bAB + t + 1)
                            vec.reciprocal(t1[:, c + 5:c + 6],
                                           t1[:, c + 4:c + 5])
                            vec.drain()
                            vec.tensor_tensor(t1[:, c + 5:c + 6],
                                              t1[:, c + 5:c + 6],
                                              t1[:, c + 3:c + 4],
                                              op=AO.mult)
                            vec.drain()
                            vec.tensor_tensor(t1[:, c + 5:c + 6],
                                              t1[:, c + 5:c + 6],
                                              t1[:, c + 4:c + 5], op=AO.add)
                            vec.drain()
                            vec.tensor_scalar(mr(1, t), t1[:, c + 5:c + 6],
                                              0.5, None, op0=AO.mult)
                            vec.drain()
                            inc(vec.scalar_tensor_tensor(
                                t1[:, c + 6:c + 7], mr(0, t), -1.0,
                                mr(1, t), op0=AO.mult, op1=AO.mult),
                                "TC", 1)
                            vec.wait_ge(sh("AA"), bAA + 2 * t + 2)
                            vec.tensor_reduce(t1[:, c + 7:c + 8],
                                              ab[:, t % 2, :],
                                              axis=AX.X, op=AO.max)
                            vec.drain()
                            inc(vec.tensor_tensor(gy[:, :], gy[:, :],
                                                  t1[:, c + 7:c + 8],
                                                  op=AO.max), "VC", 1)
                            vec.drain()

                @blk.scalar
                def _(s):
                    eg(s)
                    for u in U:
                        kind, i = u
                        if kind in ('g', 'u'):
                            q = qstep_of[u][0]
                            s.wait_ge(sh("VA"), bVA + q + 1)
                            q_dst = qwg_h if kind == 'g' else qwu_h
                            SQ.issue(s.dma_start(
                                q_dst[i * 128:(i + 1) * 128, :],
                                qb[:, q % 2, :]))
                        elif kind == 'd':
                            for p3, (o3, n3) in enumerate(PCS):
                                q = qstep_of[u][p3]
                                s.wait_ge(sh("VA"), bVA + q + 1)
                                SQ.issue(s.dma_start(
                                    qwd_h[i * 128:(i + 1) * 128,
                                          o3:o3 + n3],
                                    qb[:, q % 2, :n3]))
                        else:
                            t = i
                            c = 10 + 8 * (t % 2)
                            s.wait_ge(sh("VC"), bVC + 2 * t + 1)
                            inc(s.activation(ab[:, t % 2, :], xb[:, t % 2, :],
                                             AF.Square, accum_out=s_ssq(t)),
                                "AA", 1)
                            s.wait_ge(sh("TB"), bTB + t + 1)
                            inc(s.activation(t1[:, c + 4:c + 5],
                                             t1[:, c + 3:c + 4], AF.Sqrt),
                                "AB", 1)
                            s.wait_ge(sh("TC"), bTC + t + 1)
                            inc(s.activation(ab[:, t % 2, :], xb[:, t % 2, :],
                                             AF.Abs,
                                             bias=t1[:, c + 6:c + 7],
                                             scale=mr(1, t)), "AA", 1)

        def quant_weight(wap, nchunk, scol, q_dst):
            with nc.Block() as blk:
                bVA = sv("VA")
                LA, SQ = Stm("LA"), Stm("SQ")
                eg = entry_guard()

                @blk.sync
                def _(sync):
                    for i in range(nchunk):
                        if i >= 2:
                            sync.wait_ge(sh("VA"), bVA + (i - 1))
                        LA.issue(sync.dma_start(
                            xa[:, i % 2, :], wap[i * 128:(i + 1) * 128, :]))

                @blk.vector
                def _(vec):
                    eg(vec)
                    for i in range(nchunk):
                        LA.wait(vec, i)
                        if i >= 2:
                            SQ.wait(vec, i - 2)
                        w_t = xa[:, i % 2, :]
                        vec.tensor_reduce(t1[:, 0:1], w_t, axis=AX.X,
                                          op=AO.max, apply_absolute_value=True)
                        vec.drain()
                        vec.reciprocal(t1[:, 1:2], t1[:, 0:1])
                        vec.drain()
                        vec.tensor_scalar(scol(i), t1[:, 0:1], 1.0 / QMAX,
                                          None, op0=AO.mult)
                        vec.tensor_scalar(t1[:, 2:3], t1[:, 1:2], QMAX, None,
                                          op0=AO.mult)
                        vec.drain()
                        vec.tensor_scalar(qi[:, :], w_t, t1[:, 2:3],
                                          None, op0=AO.mult)
                        vec.drain()
                        inc(vec.tensor_copy(qb[:, i % 2, :], qi[:, :]),
                            "VA", 1)
                        vec.drain()

                @blk.scalar
                def _(s):
                    for i in range(nchunk):
                        s.wait_ge(sh("VA"), bVA + (i + 1))
                        SQ.issue(s.dma_start(q_dst[i * 128:(i + 1) * 128, :],
                                             qb[:, i % 2, :]))

        head_merged()

        def quant_weight_wide():
            PCS = [(0, 1024), (1024, 1024), (2048, 768)]
            np3 = len(PCS)
            with nc.Block() as blk:
                bVA, bVB = sv("VA"), sv("VB")
                LA, SQ = Stm("LA"), Stm("SQ")
                eg = entry_guard()

                @blk.sync
                def _(sync):
                    j = 0
                    for i in range(DC):
                        for p3, (o3, n3) in enumerate(PCS):
                            if j >= 2:
                                sync.wait_ge(sh("VB"), bVB + (j - 1))
                            LA.issue(sync.dma_start(
                                xa[:, j % 2, :n3],
                                wd_in[i * 128:(i + 1) * 128, o3:o3 + n3]))
                            j += 1
                        for p3, (o3, n3) in enumerate(PCS):
                            sync.wait_ge(sh("VB"), bVB + (j - 1))
                            LA.issue(sync.dma_start(
                                xa[:, j % 2, :n3],
                                wd_in[i * 128:(i + 1) * 128, o3:o3 + n3]))
                            j += 1

                @blk.vector
                def _(vec):
                    eg(vec)
                    j = 0
                    for i in range(DC):
                        for p3, (o3, n3) in enumerate(PCS):
                            LA.wait(vec, j)
                            vec.tensor_reduce(t1[:, p3:p3 + 1],
                                              xa[:, j % 2, :n3], axis=AX.X,
                                              op=AO.max,
                                              apply_absolute_value=True)
                            vec.drain()
                            inc(vec.tensor_copy(t1[:, 8:9], t1[:, 0:1]),
                                "VB", 1)
                            j += 1
                        vec.tensor_reduce(t1[:, 4:5], t1[:, 0:np3], axis=AX.X,
                                          op=AO.max)
                        vec.drain()
                        vec.reciprocal(t1[:, 5:6], t1[:, 4:5])
                        vec.drain()
                        vec.tensor_scalar(sdc(i), t1[:, 4:5], 1.0 / QMAX,
                                          None, op0=AO.mult)
                        vec.tensor_scalar(t1[:, 6:7], t1[:, 5:6], QMAX, None,
                                          op0=AO.mult)
                        vec.drain()
                        vec.tensor_scalar(t1[:, 7:8], sdc(i), 0.5, None,
                                          op0=AO.mult)
                        vec.drain()
                        for p3, (o3, n3) in enumerate(PCS):
                            LA.wait(vec, j)
                            jq = i * np3 + p3
                            if jq >= 2:
                                SQ.wait(vec, jq - 2)
                            vec.tensor_scalar(qi[:, :n3],
                                              xa[:, j % 2, :n3], t1[:, 6:7],
                                              None, op0=AO.mult)
                            vec.drain()
                            inc(vec.tensor_scalar(qb[:, jq % 2, :n3],
                                                  qi[:, :n3], t1[:, 7:8],
                                                  None, op0=AO.mult),
                                "VA", 1)
                            inc(vec.tensor_copy(t1[:, 8:9], t1[:, 0:1]),
                                "VB", 1)
                            vec.drain()
                            j += 1

                @blk.scalar
                def _(s):
                    for i in range(DC):
                        for p3, (o3, n3) in enumerate(PCS):
                            jq = i * np3 + p3
                            s.wait_ge(sh("VA"), bVA + (jq + 1))
                            SQ.issue(s.dma_start(
                                qwd_h[i * 128:(i + 1) * 128, o3:o3 + n3],
                                qb[:, jq % 2, :n3]))


        def ln_batch_mr(ntile, m_which, r_which):
            with nc.Block() as blk:
                bVA, bAA = sv("VA"), sv("AA")
                n = ntile

                @blk.vector
                def _(vec):
                    vec.tensor_scalar(mrr(m_which, n), sums[:, 0:n], 1.0 / D,
                                      None, op0=AO.mult)
                    vec.drain()
                    vec.tensor_tensor(t1[:, :n], mrr(m_which, n),
                                      mrr(m_which, n), op=AO.mult)
                    vec.drain()
                    vec.scalar_tensor_tensor(t2[:, :n], sums[:, n128:n128 + n],
                                             1.0 / D, t1[:, :n], op0=AO.mult,
                                             op1=AO.subtract)
                    vec.drain()
                    vec.tensor_scalar(t2[:, :n], t2[:, :n], EPS, None,
                                      op0=AO.add)
                    vec.drain()
                    inc(vec.reciprocal(t3[:, :n], t2[:, :n]), "VA", 1)
                    vec.drain()
                    vec.wait_ge(sh("AA"), bAA + 1)
                    vec.reciprocal(t2[:, :n], t4[:, :n])
                    vec.drain()
                    vec.tensor_tensor(t2[:, :n], t2[:, :n], t3[:, :n],
                                      op=AO.mult)
                    vec.drain()
                    vec.tensor_tensor(t2[:, :n], t2[:, :n], t4[:, :n],
                                      op=AO.add)
                    vec.drain()
                    vec.tensor_scalar(mrr(r_which, n), t2[:, :n], 0.5, None,
                                      op0=AO.mult)
                    vec.drain()

                @blk.scalar
                def _(s):
                    s.wait_ge(sh("VA"), bVA + 1)
                    inc(s.activation(t4[:, :n], t3[:, :n], AF.Sqrt), "AA", 1)

        def ln_stats_ymax(src_h, ntile, m_which, r_which, vrow_add, gdst):
            assert ntile % 2 == 0
            npair = ntile // 2

            def mp(w, t0):
                return mstat[:, w * n128 + t0:w * n128 + t0 + 2]

            with nc.Block() as blk:
                bVA, bVB, bVC = sv("VA"), sv("VB"), sv("VC")
                bAA, bAB = sv("AA"), sv("AB")
                LA = Stm("LA")

                @blk.sync
                def _(sync):
                    wait_all_stores(sync)
                    for t in range(ntile):
                        if t >= 2:
                            tp, tr = (t - 2) // 2, (t - 2) % 2
                            sync.wait_ge(sh("AA"), bAA + 4 * tp + 3 + tr)
                        LA.issue(sync.dma_start(
                            xa[:, t % 2, :], src_h[t * 128:(t + 1) * 128, :]))

                @blk.vector
                def _(vec):
                    for p in range(npair):
                        t0 = 2 * p
                        c = 10 + 16 * (p % 2)
                        for tr in range(2):
                            LA.wait(vec, t0 + tr)
                            if vrow_add:
                                vec.tensor_tensor(xa[:, (t0 + tr) % 2, :],
                                                  xa[:, (t0 + tr) % 2, :],
                                                  vrow[:, :], op=AO.add)
                                vec.drain()
                            inc(vec.tensor_reduce(s_sum(t0 + tr),
                                                  xa[:, (t0 + tr) % 2, :],
                                                  axis=AX.X, op=AO.add),
                                "VA", 1)
                        vec.wait_ge(sh("AA"), bAA + 4 * p + 2)
                        vec.tensor_scalar(mp(m_which, t0),
                                          sums[:, t0:t0 + 2], 1.0 / D,
                                          None, op0=AO.mult)
                        vec.tensor_scalar(t1[:, c + 2:c + 4],
                                          sums[:, n128 + t0:n128 + t0 + 2],
                                          1.0 / D, EPS, op0=AO.mult,
                                          op1=AO.add)
                        vec.drain()
                        vec.tensor_tensor(t1[:, c:c + 2], mp(m_which, t0),
                                          mp(m_which, t0), op=AO.mult)
                        vec.drain()
                        vec.tensor_tensor(t1[:, c + 4:c + 6],
                                          t1[:, c + 2:c + 4],
                                          t1[:, c:c + 2], op=AO.subtract)
                        vec.drain()
                        inc(vec.reciprocal(t1[:, c + 6:c + 8],
                                           t1[:, c + 4:c + 6]), "VB", 1)
                        vec.drain()
                        vec.wait_ge(sh("AB"), bAB + p + 1)
                        vec.reciprocal(t1[:, c + 10:c + 12],
                                       t1[:, c + 8:c + 10])
                        vec.drain()
                        vec.tensor_tensor(t1[:, c + 10:c + 12],
                                          t1[:, c + 10:c + 12],
                                          t1[:, c + 6:c + 8], op=AO.mult)
                        vec.drain()
                        vec.tensor_tensor(t1[:, c + 10:c + 12],
                                          t1[:, c + 10:c + 12],
                                          t1[:, c + 8:c + 10], op=AO.add)
                        vec.drain()
                        vec.tensor_scalar(mp(r_which, t0),
                                          t1[:, c + 10:c + 12], 0.5, None,
                                          op0=AO.mult)
                        vec.drain()
                        inc(vec.scalar_tensor_tensor(
                            t1[:, c + 12:c + 14], mp(m_which, t0), -1.0,
                            mp(r_which, t0), op0=AO.mult, op1=AO.mult),
                            "VC", 1)
                        for tr in range(2):
                            vec.wait_ge(sh("AA"), bAA + 4 * p + 3 + tr)
                            vec.tensor_reduce(t1[:, c + 14 + tr:c + 15 + tr],
                                              ab[:, (t0 + tr) % 2, :],
                                              axis=AX.X, op=AO.max)
                            vec.drain()
                            inc(vec.tensor_tensor(
                                gdst[:, :], gdst[:, :],
                                t1[:, c + 14 + tr:c + 15 + tr],
                                op=AO.max), "VA", 1)
                            vec.drain()

                @blk.scalar
                def _(s):
                    for p in range(npair):
                        t0 = 2 * p
                        c = 10 + 16 * (p % 2)
                        for tr in range(2):
                            s.wait_ge(sh("VA"), bVA + 4 * p + 1 + tr)
                            inc(s.activation(ab[:, (t0 + tr) % 2, :],
                                             xa[:, (t0 + tr) % 2, :],
                                             AF.Square,
                                             accum_out=s_ssq(t0 + tr)),
                                "AA", 1)
                        s.wait_ge(sh("VB"), bVB + p + 1)
                        inc(s.activation(t1[:, c + 8:c + 10],
                                         t1[:, c + 6:c + 8], AF.Sqrt),
                            "AB", 1)
                        s.wait_ge(sh("VC"), bVC + p + 1)
                        for tr in range(2):
                            inc(s.activation(
                                ab[:, (t0 + tr) % 2, :],
                                xa[:, (t0 + tr) % 2, :], AF.Abs,
                                bias=t1[:, c + 12 + tr:c + 13 + tr],
                                scale=mr(r_which, t0 + tr)), "AA", 1)

        def ln_stats(src_h, ntile, m_which, r_which, vrow_add):
            with nc.Block() as blk:
                bVA, bAA = sv("VA"), sv("AA")
                LA = Stm("LA")

                @blk.sync
                def _(sync):
                    wait_all_stores(sync)
                    for t in range(ntile):
                        if t >= 2:
                            sync.wait_ge(sh("AA"), bAA + (t - 1))
                        LA.issue(sync.dma_start(
                            xa[:, t % 2, :], src_h[t * 128:(t + 1) * 128, :]))

                @blk.vector
                def _(vec):
                    for t in range(ntile):
                        LA.wait(vec, t)
                        if vrow_add:
                            vec.tensor_tensor(xa[:, t % 2, :], xa[:, t % 2, :],
                                              vrow[:, :], op=AO.add)
                            vec.drain()
                        inc(vec.tensor_reduce(s_sum(t), xa[:, t % 2, :],
                                              axis=AX.X, op=AO.add), "VA", 1)

                @blk.scalar
                def _(s):
                    for t in range(ntile):
                        s.wait_ge(sh("VA"), bVA + t + 1)
                        inc(s.activation(ab[:, t % 2, :], xa[:, t % 2, :],
                                         AF.Square, accum_out=s_ssq(t)),
                            "AA", 1)

            ln_batch_mr(ntile, m_which, r_which)

        def ymax(src_h, ntile, m_which, r_which, vrow_add, gdst):
            with nc.Block() as blk:
                bVA, bVB, bAA = sv("VA"), sv("VB"), sv("AA")
                LB = Stm("LB")

                @blk.sync
                def _(sync):
                    wait_all_stores(sync)
                    for t in range(ntile):
                        if t >= 2:
                            sync.wait_ge(sh("AA"), bAA + (t - 1))
                        LB.issue(sync.dma_start(
                            xb[:, t % 2, :], src_h[t * 128:(t + 1) * 128, :]))

                @blk.vector
                def _(vec):
                    for t in range(ntile):
                        LB.wait(vec, t)
                        if vrow_add:
                            vec.tensor_tensor(xb[:, t % 2, :], xb[:, t % 2, :],
                                              vrow[:, :], op=AO.add)
                            vec.drain()
                        inc(vec.scalar_tensor_tensor(
                            t1[:, 32 + (t % 2):33 + (t % 2)], mr(m_which, t),
                            -1.0, mr(r_which, t), op0=AO.mult, op1=AO.mult),
                            "VB", 1)
                        vec.wait_ge(sh("AA"), bAA + t + 1)
                        vec.tensor_reduce(t1[:, 34:35], ab[:, t % 2, :],
                                          axis=AX.X, op=AO.max)
                        vec.drain()
                        inc(vec.tensor_tensor(gdst[:, :], gdst[:, :],
                                              t1[:, 34:35], op=AO.max),
                            "VA", 1)
                        vec.drain()

                @blk.scalar
                def _(s):
                    for t in range(ntile):
                        s.wait_ge(sh("VB"), bVB + t + 1)
                        inc(s.activation(ab[:, t % 2, :], xb[:, t % 2, :],
                                         AF.Abs,
                                         bias=t1[:, 32 + (t % 2):33 + (t % 2)],
                                         scale=mr(r_which, t)), "AA", 1)

        def armax(gsrc, cci, cco, clip_slot, csx_dst, s_dst):
            with nc.Block() as blk:
                bTA, bAA, bVA, bCC = sv("TA"), sv("AA"), sv("VA"), sv("CC")
                SG = Stm("SG")

                @blk.tensor
                def _(t):
                    inc(t.matmul(ps[6][0:1, 0:128], gsrc[:, :], idf[:, :],
                                 start=True, stop=True), "TA", 1)
                    t.wait_ge(sh("VA"), bVA + 2)
                    inc(t.matmul(ps[7][0:128, 0:2], onesf[:, :],
                                 s1r[0:1, 16:18], start=True, stop=True),
                        "TA", 1)

                @blk.vector
                def _(vec):
                    vec.wait_ge(sh("TA"), bTA + 1)
                    inc(vec.tensor_reduce(s1r[0:1, 9:10], ps[6][0:1, 0:128],
                                          axis=AX.X, op=AO.max), "VA", 1)
                    vec.wait_ge(sh("CC"), bCC + 1)
                    SG.wait(vec, 1)
                    vec.wait_ge(sh("AA"), bAA + 1)
                    vec.tensor_tensor(s1r[0:1, 12:13], s1r[0:1, 11:12],
                                      s1r[0:1, 14:15], op=AO.mult)
                    vec.drain()
                    vec.reciprocal(s1r[0:1, 13:14], s1r[0:1, 12:13])
                    vec.drain()
                    vec.tensor_scalar(s1r[0:1, 16:17], s1r[0:1, 13:14], QMAX,
                                      None, op0=AO.mult)
                    inc(vec.tensor_scalar(s1r[0:1, 17:18], s1r[0:1, 12:13],
                                          1.0 / QMAX, None, op0=AO.mult),
                        "VA", 1)
                    vec.drain()
                    vec.wait_ge(sh("TA"), bTA + 2)
                    vec.tensor_copy(csx_dst, ps[7][0:128, 0:1])
                    inc(vec.tensor_copy(s_dst, ps[7][0:128, 1:2]), "VA", 1)
                    vec.drain()

                @blk.scalar
                def _(s):
                    inc(s.activation(s1r[0:1, 11:12], clip_slot, AF.Sigmoid),
                        "AA", 1)

                @blk.gpsimd
                def _(g):
                    g.wait_ge(sh("VA"), bVA + 1)
                    SG.issue(g.dma_start(cci.ap()[:, :], s1r[0:1, 9:10]))
                    SG.wait(g, 0)
                    inc(g.collective_compute(
                        "AllReduce", AO.max, replica_groups=[list(range(8))],
                        ins=[cci.ap().opt()], outs=[cco.ap().opt()]), "CC", 1)
                    g.wait_ge(sh("CC"), bCC + 1)
                    SG.issue(g.dma_start(s1r[0:1, 14:15], cco.ap()[:, :]))

        def quant_apply(src_h, ntile, m_which, r_which, csx_col, vrow_add,
                        do_round, qxt_tok=None, ph0d=False):
            with nc.Block() as blk:
                bVA, bVB, bAA = sv("VA"), sv("VB"), sv("AA")
                LA, SQ = Stm("LA"), Stm("SQ")
                if qxt_tok is not None:
                    TRq = Stm("TR", per=DC)
                    sq0_done = sv("SQ0") + 16 * ((ntile + 1) // 2)
                    sq1_done = sv("SQ1") + 16 * (ntile // 2)
                if ph0d:
                    LWp, SOp = Stm("LW"), Stm("SO")
                    strips = ([('g', hc) for hc in range(DC)] +
                              [('u', hc) for hc in range(DC)] +
                              [('d', ic) for ic in range(ICN)])
                eg = entry_guard()

                @blk.sync
                def _(sync):
                    wait_all_stores(sync)
                    for t in range(ntile):
                        if t >= 2:
                            sync.wait_ge(sh("AA"), bAA + (t - 1))
                        LA.issue(sync.dma_start(
                            xa[:, t % 2, :], src_h[t * 128:(t + 1) * 128, :]))
                    if qxt_tok is not None:
                        sync.wait_ge(sh("SQ0"), sq0_done)
                        sync.wait_ge(sh("SQ1"), sq1_done)
                        for hc in range(DC):
                            TRq.issue(sync.dma_start(
                                qx_fm[:, hc, 0:qxt_tok],
                                qa_h[0:qxt_tok, hc * 128:(hc + 1) * 128],
                                transpose=True))
                        TRq.wait(sync, 0)

                dflat = diagc[:, :, :, :].rearrange("p b j q -> p b (j q)")

                def strip_info(k):
                    kind, j = strips[k]
                    if kind == 'g':
                        return (qwg_h[:, j * 128:(j + 1) * 128],
                                wgT_h[j * 128:(j + 1) * 128, :], IFF)
                    if kind == 'u':
                        return (qwu_h[:, j * 128:(j + 1) * 128],
                                wuT_h[j * 128:(j + 1) * 128, :], IFF)
                    return (qwd_h[:, j * 128:(j + 1) * 128],
                            wdT_h[j * 128:(j + 1) * 128, :], 1024)

                @blk.gpsimd
                def _(g):
                    if not ph0d:
                        return
                    for k in range(len(strips)):
                        _, pdst, pw = strip_info(k)
                        LWp.wait(g, k)
                        SOp.issue(g.dma_start(pdst, dflat[:, k % 2, 0:pw]))

                @blk.vector
                def _(vec):
                    eg(vec)
                    for t in range(ntile):
                        LA.wait(vec, t)
                        if vrow_add:
                            vec.tensor_tensor(xa[:, t % 2, :], xa[:, t % 2, :],
                                              vrow[:, :], op=AO.add)
                            vec.drain()
                        if do_round:
                            vec.tensor_tensor(t2[:, 32 + (t % 2):33 + (t % 2)],
                                              mr(r_which, t), csx_col,
                                              op=AO.mult)
                        else:
                            vec.tensor_copy(t2[:, 32 + (t % 2):33 + (t % 2)],
                                            mr(r_which, t))
                        vec.drain()
                        inc(vec.scalar_tensor_tensor(
                            t3[:, 32 + (t % 2):33 + (t % 2)], mr(m_which, t),
                            -1.0, t2[:, 32 + (t % 2):33 + (t % 2)],
                            op0=AO.mult, op1=AO.mult), "VB", 1)
                        if do_round:
                            vec.wait_ge(sh("AA"), bAA + (t + 1))
                            if t >= 2:
                                SQ.wait(vec, t - 2)
                            vec.tensor_scalar(qi[:, :], qt[:, t % 2, :],
                                              CLIP_LO, CLIP_HI, op0=AO.max,
                                              op1=AO.min)
                            vec.drain()
                            inc(vec.tensor_copy(qb[:, t % 2, :], qi[:, :]),
                                "VA", 1)
                            vec.drain()

                @blk.scalar
                def _(s):
                    eg(s)
                    kph = [0]

                    def ph0d_loads(upto):
                        while kph[0] < min(upto, len(strips)):
                            k = kph[0]
                            src, _, w = strip_info(k)
                            if k >= 2:
                                SOp.wait(s, k - 2)
                            LWp.issue(s.dma_start(dflat[:, k % 2, 0:w], src,
                                                  transpose=True))
                            kph[0] += 1

                    if ph0d:
                        for s_ in ("SQ0", "SQ1"):
                            s.wait_ge(sh(s_), sv(s_))
                    for t in range(ntile):
                        if ph0d:
                            ph0d_loads((t + 1) * len(strips) // ntile)
                        s.wait_ge(sh("VB"), bVB + t + 1)
                        if do_round:
                            if t >= 2:
                                s.wait_ge(sh("VA"), bVA + (t - 1))
                            inc(s.activation(
                                qt[:, t % 2, :], xa[:, t % 2, :], AF.Identity,
                                bias=t3[:, 32 + (t % 2):33 + (t % 2)],
                                scale=t2[:, 32 + (t % 2):33 + (t % 2)]),
                                "AA", 1)
                            s.wait_ge(sh("VA"), bVA + (t + 1))
                        else:
                            if t >= 2:
                                SQ.wait(s, t - 2)
                            aa_val = inc(s.activation(
                                qb[:, t % 2, :], xa[:, t % 2, :], AF.Identity,
                                bias=t3[:, 32 + (t % 2):33 + (t % 2)],
                                scale=t2[:, 32 + (t % 2):33 + (t % 2)]),
                                "AA", 1)
                            s.wait_ge(sh("AA"), aa_val)
                        SQ.issue(s.dma_start(qa_h[t * 128:(t + 1) * 128, :],
                                             qb[:, t % 2, :]))

        def qx_transpose(ttok):
            with nc.Block() as blk:
                TRq = Stm("TR", per=DC)

                @blk.sync
                def _(sync):
                    wait_all_stores(sync)
                    for hc in range(DC):
                        TRq.issue(sync.dma_start(
                            qx_fm[:, hc, 0:ttok],
                            qa_h[0:ttok, hc * 128:(hc + 1) * 128],
                            transpose=True))
                    TRq.wait(sync, 0)

        def ffn_gateup(ttot):
            NTl = _tiles(ttot)
            nnt = len(NTl)
            with nc.Block() as blk:
                bTA = sv("TA")
                bVA, bVB, bPFA = sv("VA"), sv("VB"), sv("PFA")
                LW = Stm("LW", per=2)
                HS = Stm("SG")
                eg = entry_guard()

                @blk.sync
                def _(sync):
                    for ic in range(ICN):
                        if ic >= 2:
                            sync.wait_ge(sh("TA"), bTA + (ic - 1) * nnt)
                        LW.issue(sync.dma_start(
                            wgc[:, ic % 2, :, :],
                            bass.AP(wgT_h.tensor, ic * 128,
                                    [[IFF, 128], [128 * IFF, DC], [1, 128]])))
                        LW.issue(sync.dma_start(
                            wuc[:, ic % 2, :, :],
                            bass.AP(wuT_h.tensor, ic * 128,
                                    [[IFF, 128], [128 * IFF, DC], [1, 128]])))

                @blk.vector
                def _(vec):
                    eg(vec)
                    for i in range(ICN):
                        vec.tensor_tensor(cgc(i), sgc(i), sc_c(SHS),
                                          op=AO.mult)
                        vec.tensor_tensor(cuc(i), suc(i), sc_c(SHS),
                                          op=AO.mult)
                    vec.drain()
                    inc(vec.tensor_copy(t1[:, 0:1], cgc(0)), "VB", 1)
                    k = 0
                    for ic in range(ICN):
                        for (nt0, nts) in NTl:
                            vec.wait_ge(sh("PFA"), bPFA + 2 * (k + 1))
                            if k >= 2:
                                HS.wait(vec, k - 2)
                            vec.tensor_tensor(ht[:, k % 2, :nts],
                                              st_t[:, 2 * (k % 2), :nts],
                                              st_t[:, 2 * (k % 2) + 1, :nts],
                                              op=AO.mult)
                            vec.drain()
                            vec.tensor_reduce(t1[:, 34:35], ht[:, k % 2, :nts],
                                              axis=AX.X, op=AO.max,
                                              apply_absolute_value=True)
                            vec.drain()
                            inc(vec.tensor_tensor(gh[:, :], gh[:, :],
                                                  t1[:, 34:35], op=AO.max),
                                "VA", 1)
                            vec.drain()
                            k += 1

                @blk.tensor
                def _(t):
                    k = 0
                    for ic in range(ICN):
                        LW.wait(t, ic)
                        for (nt0, nts) in NTl:
                            pg = ps[2 * (k % 2)]
                            pu = ps[2 * (k % 2) + 1]
                            if k >= 2:
                                t.wait_ge(sh("PFA"), bPFA + 2 * (k - 1))
                            for hc in range(DC):
                                t.matmul(pg[:, :nts], wgc[:, ic % 2, hc, :],
                                         qx_fm[:, hc, nt0:nt0 + nts],
                                         start=(hc == 0), stop=(hc == DC - 1))
                            inst = None
                            for hc in range(DC):
                                inst = t.matmul(pu[:, :nts],
                                                wuc[:, ic % 2, hc, :],
                                                qx_fm[:, hc, nt0:nt0 + nts],
                                                start=(hc == 0),
                                                stop=(hc == DC - 1))
                            inc(inst, "TA", 1)
                            k += 1

                @blk.scalar
                def _(s):
                    s.wait_ge(sh("VB"), bVB + 1)
                    k = 0
                    for ic in range(ICN):
                        for (nt0, nts) in NTl:
                            pg = ps[2 * (k % 2)]
                            pu = ps[2 * (k % 2) + 1]
                            s.wait_ge(sh("TA"), bTA + k + 1)
                            if k >= 2:
                                s.wait_ge(sh("VA"), bVA + (k - 1))
                            inc(s.activation(st_t[:, 2 * (k % 2), :nts],
                                             pg[:, :nts], AF.Silu,
                                             scale=cgc(ic)), "PFA", 1)
                            inc(s.activation(st_t[:, 2 * (k % 2) + 1, :nts],
                                             pu[:, :nts], AF.Copy,
                                             scale=cuc(ic)), "PFA", 1)
                            k += 1

                @blk.gpsimd
                def _(g):
                    k = 0
                    for ic in range(ICN):
                        for (nt0, nts) in NTl:
                            g.wait_ge(sh("VA"), bVA + k + 1)
                            HS.issue(g.dma_start(
                                bass.AP(h_h.tensor, ic * 128 * t_tot + nt0,
                                        [[t_tot, 128], [1, nts]]),
                                ht[:, k % 2, :nts]))
                            k += 1

        def srow_build():
            with nc.Block() as blk:
                bTA, bVA = sv("TA"), sv("VA")
                SQ = Stm("SQ")
                eg = entry_guard()

                @blk.vector
                def _(vec):
                    eg(vec)
                    for dc in range(DC):
                        vec.tensor_scalar(t1[:, 40 + dc:41 + dc], sdc(dc),
                                          sc_c(SHS), 0.5, op0=AO.mult,
                                          op1=AO.mult)
                    vec.drain()
                    inc(vec.tensor_copy(t1[:, 48:49], t1[:, 40:41]), "VA", 1)
                    vec.wait_ge(sh("TA"), bTA + 1)
                    inc(vec.tensor_copy(ar32[0:DC, :], ps[6][0:DC, 0:128]),
                        "VA", 1)
                    vec.drain()
                    for half in range(2):
                        vec.wait_ge(sh("TA"), bTA + 2 + half)
                        vec.tensor_copy(srow[:, half * 512:(half + 1) * 512],
                                        ps[7 - half][:, 0:512])
                        vec.drain()
                        inc(vec.tensor_copy(t1[:, 49:50], t1[:, 40:41]),
                            "VA", 1)

                @blk.tensor
                def _(t):
                    t.wait_ge(sh("VA"), bVA + 1)
                    inc(t.matmul(ps[6][0:DC, 0:128], t1[:, 40:40 + DC],
                                 idf[:, :], start=True, stop=True), "TA", 1)
                    SQ.wait(t, 1)
                    for half in range(2):
                        if half == 1:
                            t.wait_ge(sh("VA"), bVA + 3)
                        inc(t.matmul(ps[7 - half][:, 0:512], onesf[:, :],
                                     rowD[0:1, half * 512:(half + 1) * 512],
                                     start=True, stop=True), "TA", 1)

                @blk.scalar
                def _(s):
                    s.wait_ge(sh("VA"), bVA + 2)
                    SQ.issue(s.dma_start(
                        misc_h[0:D].rearrange("(c p) -> c p", p=128),
                        ar32[0:DC, :]))
                    SQ.wait(s, 0)
                    SQ.issue(s.dma_start(rowD[0:1, :], misc_h[None, 0:D]))

        def ffn_down(ttot, res_h, out_h, next_stats, final, tok_shift):
            NTl = _tiles(ttot)
            nnt = len(NTl)
            KS = []
            for ni, (nt0, nts) in enumerate(NTl):
                for ms in range((nts + 127) // 128):
                    KS.append((ni, nt0, nts, ms))
            ks_before = [0] * (nnt + 1)
            for ni in range(nnt):
                ks_before[ni + 1] = ks_before[ni] + (NTl[ni][1] + 127) // 128
            with nc.Block() as blk:
                bTA, bAA, bVA = sv("TA"), sv("AA"), sv("VA")
                bVB, bPFA, bAB = sv("VB"), sv("PFA"), sv("AB")
                bDGd = sv("DG")
                LW = Stm("LW")
                LH = Stm("LH")
                XB = Stm("LB")
                ST = Stm("SO" if final else "SG")
                eg = entry_guard()

                ICN2 = ICN // 2

                @blk.sync
                def _(sync):
                    wait_all_stores(sync)
                    LW.issue(sync.dma_start(
                        wdT_v[:, :, :],
                        bass.AP(wdT_h.tensor, 0,
                                [[1024, 128], [128 * 1024, ICN], [1, 1024]])))
                    def xb_issue(k):
                        kni, knt0, knts, kms = KS[k]
                        if k >= 2:
                            sync.wait_ge(sh("DG"), bDGd + (k - 1))
                        XB.issue(sync.dma_start(
                            xb[:, k % 2, :],
                            res_h[knt0 + kms * 128:knt0 + kms * 128 + 128,
                                  :]))

                    kk = 0
                    for ni, (nt0, nts) in enumerate(NTl):
                        for icp in range(ICN2):
                            if kk >= 2:
                                sync.wait_ge(sh("AA"), bAA + (kk - 1))
                            LH.issue(sync.dma_start(
                                hl[:, kk % 2, :, :nts],
                                bass.AP(h_h.tensor,
                                        (2 * icp) * 128 * t_tot + nt0,
                                        [[t_tot, 128], [128 * t_tot, 2],
                                         [1, nts]])))
                            kk += 1
                        if ni >= 1:
                            for k in range(ks_before[ni - 1], ks_before[ni]):
                                xb_issue(k)
                    for k in range(ks_before[nnt - 1], ks_before[nnt]):
                        xb_issue(k)

                @blk.gpsimd
                def _(g):
                    for s in ("SG0", "SG1"):
                        g.wait_ge(sh(s), sv(s))
                    for k, (ni, nt0, nts, ms) in enumerate(KS):
                        g.wait_ge(sh("DG"), bDGd + (k + 1))
                        row = nt0 + ms * 128 - tok_shift
                        ST.issue(g.dma_start(out_h[row:row + 128, :],
                                             ob[:, k % 2, :]))

                @blk.scalar
                def _(s):
                    def emit_square(k):
                        ni, nt0, nts, ms = KS[k]
                        s.wait_ge(sh("DG"), bDGd + (k + 1))
                        tt = (nt0 + ms * 128) // 128
                        inc(s.activation(ab[:, k % 2, :], ob[:, k % 2, :],
                                         AF.Square, accum_out=s_ssq(tt)),
                            "AB", 1)

                    kk = 0
                    for ni, (nt0, nts) in enumerate(NTl):
                        for icp in range(ICN2):
                            LH.wait(s, kk)
                            if kk >= 2:
                                s.wait_ge(sh("VB"), bVB + (kk - 1))
                            inc(s.activation(hq_ti[:, kk % 2, :, :nts],
                                             hl[:, kk % 2, :, :nts], AF.Copy,
                                             scale=sc_c(CSH)), "AA", 1)
                            kk += 1
                        if next_stats and ni >= 1:
                            for k in range(ks_before[ni - 1], ks_before[ni]):
                                emit_square(k)
                    if next_stats:
                        for k in range(ks_before[nnt - 1], ks_before[nnt]):
                            emit_square(k)

                @blk.vector
                def _(vec):
                    eg(vec)

                    def emit_evac(k):
                        ni, nt0, nts, ms = KS[k]
                        mts = min(128, nts - ms * 128)
                        pd0 = ps[2 * (k % 4)]
                        pd1 = ps[2 * (k % 4) + 1]
                        vec.wait_ge(sh("TA"), bTA + k + 1)
                        XB.wait(vec, k)
                        if k >= 2:
                            ST.wait(vec, k - 2)
                            if next_stats:
                                vec.wait_ge(sh("AB"), bAB + (k - 1))
                        for half, pd in ((0, pd0), (1, pd1)):
                            vec.scalar_tensor_tensor(
                                ob[:mts, k % 2, half * 512:(half + 1) * 512],
                                pd[:mts, :],
                                scal[:mts, SHS:SHS + 1],
                                xb[:mts, k % 2, half * 512:(half + 1) * 512],
                                op0=AO.mult, op1=AO.add)
                        vec.drain()
                        inc(vec.tensor_copy(t1[:, 50:51], t1[:, 50:51]),
                            "PFA", 2)
                        if final:
                            vec.tensor_tensor(ob[:mts, k % 2, :],
                                              ob[:mts, k % 2, :],
                                              vrow[:mts, :], op=AO.add)
                            vec.drain()
                        inc(vec.tensor_copy(t1[:, 51:52], t1[:, 51:52]),
                            "DG", 1)
                        if next_stats:
                            tt = (nt0 + ms * 128) // 128
                            vec.tensor_reduce(s_sum(tt), ob[:, k % 2, :],
                                              axis=AX.X, op=AO.add)
                            vec.drain()

                    kk = 0
                    for ni, (nt0, nts) in enumerate(NTl):
                        for icp in range(ICN2):
                            vec.wait_ge(sh("AA"), bAA + kk + 1)
                            if ni >= 2 and icp == 0:
                                vec.wait_ge(sh("TA"), bTA + ks_before[ni - 1])
                            inc(vec.tensor_scalar(
                                qh_v[:, ni % 2, 2 * icp:2 * icp + 2, 0:nts],
                                hq_ti[:, kk % 2, :, :nts],
                                -(QMAX + 1.0), QMAX, op0=AO.max, op1=AO.min),
                                "VB", 1)
                            vec.drain()
                            kk += 1
                        if ni >= 1:
                            for k in range(ks_before[ni - 1], ks_before[ni]):
                                emit_evac(k)
                    for k in range(ks_before[nnt - 1], ks_before[nnt]):
                        emit_evac(k)

                @blk.tensor
                def _(t):
                    LW.wait(t, 0)
                    for k, (ni, nt0, nts, ms) in enumerate(KS):
                        mts = min(128, nts - ms * 128)
                        pd0 = ps[2 * (k % 4)]
                        pd1 = ps[2 * (k % 4) + 1]
                        if k >= 4:
                            t.wait_ge(sh("PFA"), bPFA + 2 * (k - 3))
                        t.wait_ge(sh("VB"), bVB + (ni + 1) * ICN2)
                        inst = None
                        for ic in range(ICN):
                            t.matmul(pd0[:mts, :],
                                     qh_v[:, ni % 2, ic,
                                          ms * 128:ms * 128 + mts],
                                     wdT_v[:, ic, 0:512],
                                     start=(ic == 0), stop=(ic == ICN - 1))
                            inst = t.matmul(
                                pd1[:mts, :],
                                qh_v[:, ni % 2, ic,
                                     ms * 128:ms * 128 + mts],
                                wdT_v[:, ic, 512:1024],
                                start=(ic == 0), stop=(ic == ICN - 1))
                        inc(inst, "TA", 1)

        def attn_main():
            nnt = len(NT)
            OMS = []
            for ni, (nt0, nts) in enumerate(NT):
                OMS.append([ms for ms in range((nts + 127) // 128)
                            if nt0 + ms * 128 >= HALO])
            oms_before = [0] * (nnt + 1)
            for ni in range(nnt):
                oms_before[ni + 1] = oms_before[ni] + len(OMS[ni])

            def ko_of(ko):
                ni = next(i for i in range(nnt) if oms_before[i + 1] > ko)
                return ni, OMS[ni][ko - oms_before[ni]], NT[ni][0]

            with nc.Block() as blk:
                bTA, bTB, bTC = sv("TA"), sv("TB"), sv("TC")
                bAA, bVB, bVC = sv("AA"), sv("VB"), sv("VC")
                bPFA, bDG, bVA = sv("PFA"), sv("DG"), sv("VA")
                LW = Stm("LW")
                TR = Stm("TR", per=DC)
                XB = Stm("LB")
                XO = Stm("SG")
                LD = Stm("LH")
                CCQ = Stm("CQ")
                bCC = sv("CC")
                eg = entry_guard()

                @blk.sync
                def _(sync):
                    wait_all_stores(sync)
                    for kc in range(DC):
                        if kc >= 2:
                            sync.wait_ge(sh("VB"), bVB + (kc - 1))
                        LW.issue(sync.dma_start(
                            qt[:, kc % 2, :],
                            w1_in[kc * 128:(kc + 1) * 128, :]))
                    for kc in range(DC):
                        sync.wait_ge(sh("VB"), bVB + DC + kc - 1)
                        LW.issue(sync.dma_start(
                            qt[:, kc % 2, :],
                            w2_in[kc * 128:(kc + 1) * 128, :]))
                    ko = 0
                    for ni, (nt0, nts) in enumerate(NT):
                        if ni >= 2:
                            sync.wait_ge(sh("TA"), bTA + (ni - 1) * DC)
                            sync.wait_ge(sh("VC"), bVC + (ni - 1))
                        for hc in range(DC):
                            TR.issue(sync.dma_start(
                                x2f_v[:, ni % 2, hc, 0:nts],
                                qa_h[nt0:nt0 + nts, hc * 128:(hc + 1) * 128],
                                transpose=True))
                        for c in range(DC):
                            gstep = ni * DC + c
                            if gstep >= 2:
                                sync.wait_ge(sh("TB"), bTB + gstep - 1)
                            LD.issue(sync.dma_start(
                                diagc[:, gstep % 2, :, :],
                                bass.AP(diag_h.tensor,
                                        c * 128 * NTAP * 128,
                                        [[NTAP * 128, 128],
                                         [1, NTAP * 128]])))
                        for ms in OMS[ni]:
                            if ko >= 2:
                                skew = 2 if ko - 2 >= oms_before[nnt - 1] \
                                    else 0
                                sync.wait_ge(sh("VA"),
                                             bVA + (ko - 1) + skew)
                            XB.issue(sync.dma_start(
                                xb[:, ko % 2, :],
                                xp_h[nt0 + ms * 128:nt0 + ms * 128 + 128, :]))
                            ko += 1

                @blk.vector
                def _(vec):
                    eg(vec)
                    for kc in range(DC):
                        LW.wait(vec, kc)
                        vec.tensor_copy(w1_v[:, kc, :], qt[:, kc % 2, :])
                        vec.drain()
                        inc(vec.tensor_copy(t1[:, 0:1], t1[:, 0:1]), "VB", 1)
                    for kc in range(DC):
                        LW.wait(vec, DC + kc)
                        vec.tensor_copy(w2_v[:, kc, :], qt[:, kc % 2, :])
                        vec.drain()
                        inc(vec.tensor_copy(t1[:, 0:1], t1[:, 0:1]), "VB", 1)
                    ko = 0
                    for ni, (nt0, nts) in enumerate(NT):
                        lo = HALO if ni == 0 else 0
                        TR.wait(vec, ni)
                        vec.tensor_reduce(t2[:, 40:40 + DC],
                                          x2f_v[:, ni % 2, :, lo:nts],
                                          axis=AX.X, op=AO.add)
                        vec.drain()
                        inc(vec.tensor_tensor(tot8[:, :], tot8[:, :],
                                              t2[:, 40:40 + DC], op=AO.add),
                            "VC", 1)
                        vec.drain()
                        if ni == nnt - 1:
                            for b in range(4):
                                vec.tensor_scalar(
                                    ctr32[:, b * DC:(b + 1) * DC],
                                    tot8[:, :], bsel_b[:, b:b + 1],
                                    None, op0=AO.mult)
                            vec.drain()
                            inc(vec.tensor_copy(t1[:, 0:1], t1[:, 0:1]),
                                "VA", 1)
                            vec.wait_ge(sh("TA"), bTA + nnt * DC + 1)
                            inc(vec.tensor_copy(ar32[:, :],
                                                ps[6][0:32, 0:128]),
                                "VA", 1)
                            vec.drain()
                        if ni == 0:
                            vec.wait_ge(sh("AA"), bAA + DC)
                            for c in range(DC):
                                vec.tensor_scalar(pr_v[:, c, 128:256],
                                                  pr_v[:, c, 128:256],
                                                  sc_c(HMASK), None,
                                                  op0=AO.mult)
                            vec.drain()
                        for c in range(DC):
                            gstep = ni * DC + c
                            inc(vec.tensor_copy(t1[:, 0:1], t1[:, 0:1]),
                                "DG", 1)
                            if ni >= 1 and c == 0:
                                vec.wait_ge(sh("TC"), bTC + oms_before[ni])
                            vec.wait_ge(sh("TB"), bTB + gstep + 1)
                            vec.tensor_copy(z_v[:, c, 0:nts],
                                            ps[4 + (gstep % 2)][:, :nts])
                            vec.drain()
                            if ni < nnt - 1:
                                vec.tensor_copy(pr_v[:, c, 0:128],
                                                pr_v[:, c, nts:nts + 128])
                                vec.drain()
                            inc(vec.tensor_copy(t1[:, 1:2], t1[:, 1:2]),
                                "DG", 1)
                        for oi, ms in enumerate(OMS[ni]):
                            mts = min(128, nts - ms * 128)
                            po0 = ps[2 * (ko % 2)]
                            po1 = ps[2 * (ko % 2) + 1]
                            vec.wait_ge(sh("TC"), bTC + ko + 1)
                            XB.wait(vec, ko)
                            if ko >= 2:
                                XO.wait(vec, ko - 2)
                            for half, po in ((0, po0), (1, po1)):
                                vec.tensor_tensor(
                                    ob[:mts, ko % 2,
                                       half * 512:(half + 1) * 512],
                                    po[:mts, :],
                                    xb[:mts, ko % 2,
                                       half * 512:(half + 1) * 512],
                                    op=AO.add)
                            vec.drain()
                            inc(vec.tensor_copy(t1[:, 0:1], t1[:, 0:1]),
                                "PFA", 2)
                            inc(vec.tensor_copy(t1[:, 1:2], t1[:, 1:2]),
                                "VA", 1)
                            ko += 1

                @blk.scalar
                def _(s):
                    for ni, (nt0, nts) in enumerate(NT):
                        for mc in range(DC):
                            s.wait_ge(sh("TA"), bTA + ni * DC + mc + 1)
                            if ni >= 1:
                                s.wait_ge(sh("DG"), bDG + 2 * ((ni - 1) * DC
                                                               + mc + 1))
                            inc(s.activation(pr_v[:, mc, 128:128 + nts],
                                             ps[6 + (mc % 2)][:, :nts],
                                             AF.Identity,
                                             bias=b1f[:, mc:mc + 1],
                                             scale=1.0), "AA", 1)

                @blk.tensor
                def _(t):
                    for ni, (nt0, nts) in enumerate(NT):
                        t.wait_ge(sh("VB"), bVB + DC)
                        TR.wait(t, ni)
                        for mc in range(DC):
                            if ni * DC + mc >= 2:
                                t.wait_ge(sh("AA"), bAA + ni * DC + mc - 1)
                            inst = None
                            for kc in range(DC):
                                inst = t.matmul(
                                    ps[6 + (mc % 2)][:, :nts],
                                    w1_v[:, kc, mc * 128:(mc + 1) * 128],
                                    x2f_v[:, ni % 2, kc, 0:nts],
                                    start=(kc == 0), stop=(kc == DC - 1))
                            inc(inst, "TA", 1)
                        if ni == nnt - 1:
                            t.wait_ge(sh("VA"),
                                      bVA + oms_before[nnt - 1] + 1)
                            t.wait_ge(sh("AA"), bAA + nnt * DC)
                            inc(t.matmul(ps[6][0:32, 0:128], ctr32[:, :],
                                         idf[:, :], start=True, stop=True),
                                "TA", 1)
                        for c in range(DC):
                            gstep = ni * DC + c
                            LD.wait(t, gstep)
                            t.wait_ge(sh("DG"), bDG + 2 * gstep + 1)
                            t.wait_ge(sh("AA"), bAA + ni * DC + c + 1)
                            if ni == 0 and c == 0:
                                t.wait_ge(sh("GI"), 1)
                            if gstep >= 2:
                                t.wait_ge(sh("DG"), bDG + 2 * (gstep - 1))
                            pb = ps[4 + (gstep % 2)]
                            inst = None
                            for j in range(NTAP):
                                inst = t.matmul(
                                    pb[:, :nts], diagc[:, gstep % 2, j, :],
                                    pr_v[:, c, 128 - j:128 - j + nts],
                                    start=(j == 0), stop=(j == NTAP - 1))
                            inc(inst, "TB", 1)
                        t.wait_ge(sh("VB"), bVB + 2 * DC)
                        t.wait_ge(sh("DG"), bDG + 2 * (ni * DC + DC))
                        for oi, ms in enumerate(OMS[ni]):
                            mts = min(128, nts - ms * 128)
                            ko = oms_before[ni] + oi
                            po0 = ps[2 * (ko % 2)]
                            po1 = ps[2 * (ko % 2) + 1]
                            if ko >= 2:
                                t.wait_ge(sh("PFA"), bPFA + 2 * (ko - 1))
                            inst = None
                            for kc in range(DC):
                                t.matmul(po0[:mts, :],
                                         z_v[:, kc, ms * 128:ms * 128 + mts],
                                         w2_v[:, kc, 0:512],
                                         start=(kc == 0), stop=(kc == DC - 1))
                                inst = t.matmul(
                                    po1[:mts, :],
                                    z_v[:, kc, ms * 128:ms * 128 + mts],
                                    w2_v[:, kc, 512:1024],
                                    start=(kc == 0), stop=(kc == DC - 1))
                            inc(inst, "TC", 1)

                @blk.gpsimd
                def _(g):
                    kom1 = oms_before[nnt - 1]
                    for ko in range(oms_before[nnt]):
                        if ko == kom1:
                            g.wait_ge(sh("VA"), bVA + kom1 + 2)
                            CCQ.issue(g.dma_start(cc_t["i3"].ap()[:, :],
                                                  ar32[:, :]))
                            CCQ.wait(g, 0)
                            inc(g.collective_compute(
                                "AllReduce", AO.add,
                                replica_groups=[list(range(8))],
                                ins=[cc_t["i3"].ap().opt()],
                                outs=[cc_t["o3"].ap().opt()]), "CC", 1)
                            g.wait_ge(sh("CC"), bCC + 1)
                            CCQ.issue(g.dma_start(ar32[:, :],
                                                  cc_t["o3"].ap()[:, :]))
                        ni, ms, nt0 = ko_of(ko)
                        skew = 2 if ko >= kom1 else 0
                        g.wait_ge(sh("VA"), bVA + (ko + 1) + skew)
                        row = nt0 + ms * 128 - HALO
                        XO.issue(g.dma_start(xpp_h[row:row + 128, :],
                                             ob[:, ko % 2, :]))

        def ar3_vrow():
            with nc.Block() as blk:
                bTA, bVA = sv("TA"), sv("VA")
                eg = entry_guard()

                @blk.vector
                def _(vec):
                    eg(vec)
                    vec.wait_ge(sh("TA"), bTA + 1)
                    inc(vec.tensor_copy(allT[:, :], ps[7][:, 0:32]), "VA", 1)
                    vec.drain()
                    vec.tensor_scalar(t2[:, 0:DC], allT[:, 0:DC],
                                      bsel_b[:, 0:1], None, op0=AO.mult)
                    vec.drain()
                    for b in range(1, 4):
                        vec.scalar_tensor_tensor(
                            t2[:, 0:DC], allT[:, b * DC:(b + 1) * DC],
                            bsel_b[:, b:b + 1], t2[:, 0:DC],
                            op0=AO.mult, op1=AO.add)
                        vec.drain()
                    vec.tensor_copy(tot8[:, :], t2[:, 0:DC])
                    vec.drain()
                    inc(vec.tensor_copy(t8b[:, :], tot8[:, :]), "VA", 1)
                    vec.drain()
                    for mc in range(DC):
                        vec.wait_ge(sh("TA"), bTA + 2 + mc)
                        vec.tensor_tensor(totf[:, mc:mc + 1],
                                          ps[mc % 2][:, 0:1],
                                          b1T[:, mc:mc + 1], op=AO.add)
                        vec.drain()
                        inc(vec.tensor_copy(t1[:, 0:1], t1[:, 0:1]), "VA", 1)
                    vec.tensor_tensor(u8[:, :], totf[:, :], mwf[:, :, 0],
                                      op=AO.mult)
                    vec.drain()
                    inc(vec.tensor_copy(u8b[:, :], u8[:, :]), "VA", 1)
                    vec.drain()
                    vec.wait_ge(sh("TA"), bTA + DC + 3)
                    vec.tensor_copy(rowD[0:1, 0:512], ps[6][0:1, :])
                    vec.tensor_copy(rowD[0:1, 512:1024], ps[7][0:1, :])
                    vec.drain()
                    vec.tensor_tensor(rowD[0:1, :], rowD[0:1, :],
                                      rowD2[0:1, :], op=AO.add)
                    vec.drain()
                    inc(vec.tensor_copy(t1[:, 0:1], t1[:, 0:1]), "VA", 1)
                    for half in range(2):
                        vec.wait_ge(sh("TA"), bTA + DC + 4 + half)
                        vec.tensor_copy(vrow[:, half * 512:(half + 1) * 512],
                                        ps[4 + half][:, 0:512])
                        vec.drain()

                @blk.tensor
                def _(t):
                    for s_ in ("CQ0", "CQ1"):
                        t.wait_ge(sh(s_), sv(s_))
                    inc(t.matmul(ps[7][:, 0:32], ar32[:, :], idf[0:32, 0:32],
                                 start=True, stop=True), "TA", 1)
                    t.wait_ge(sh("VA"), bVA + 2)
                    for mc in range(DC):
                        if mc >= 2:
                            t.wait_ge(sh("VA"), bVA + 2 + (mc - 1))
                        inst = None
                        for kc in range(DC):
                            inst = t.matmul(ps[mc % 2][:, 0:1],
                                            w1_v[:, kc,
                                                 mc * 128:(mc + 1) * 128],
                                            t8b[:, kc:kc + 1],
                                            start=(kc == 0),
                                            stop=(kc == DC - 1))
                        inc(inst, "TA", 1)
                    t.wait_ge(sh("VA"), bVA + DC + 3)
                    for half, bank in ((0, 6), (1, 7)):
                        inst = None
                        for kc in range(DC):
                            inst = t.matmul(ps[bank][0:1, :],
                                            u8b[:, kc:kc + 1],
                                            w2_v[:, kc,
                                                 half * 512:(half + 1) * 512],
                                            start=(kc == 0),
                                            stop=(kc == DC - 1))
                        inc(inst, "TA", 1)
                    t.wait_ge(sh("VA"), bVA + DC + 4)
                    for half, bank in ((0, 4), (1, 5)):
                        inc(t.matmul(ps[bank][:, 0:512], onesf[:, :],
                                     rowD[0:1, half * 512:(half + 1) * 512],
                                     start=True, stop=True), "TA", 1)

                @blk.gpsimd
                def _(g):
                    g.memset(gy[:, :], 0.0)
                    g.memset(gh[:, :], 0.0)

        def _go():
            armax(gy, cc_t["i1"], cc_t["o1"], s1r[0:1, 0:1], sc_c(CSX),
                  sc_c(SHS))
            quant_apply(x_in, n128, 0, 1, sc_c(CSX), False, True,
                        qxt_tok=t_tot, ph0d=True)
            if stop_after == "quant1":
                return
            ffn_gateup(t_tot)
            if stop_after == "gateup1":
                return
            armax(gh, cc_t["i2"], cc_t["o2"], s1r[0:1, 1:2], sc_c(CSH),
                  sc_c(SHS))
            ffn_down(t_tot, x_in, xp_h, True, False, 0)
            if stop_after == "ffn1":
                return
            ln_batch_mr(n128, 2, 3)
            quant_apply(xp_h, n128, 2, 3, None, False, False)
            if stop_after == "x2q":
                return
            attn_main()
            if stop_after == "attn":
                return
            ar3_vrow()
            if stop_after == "ar3":
                return
            ln_stats_ymax(xpp_h, n128o, 4, 5, True, gy)
            armax(gy, cc_t["i4"], cc_t["o4"], s1r[0:1, 0:1], sc_c(CSX),
                  sc_c(SHS))
            quant_apply(xpp_h, n128o, 4, 5, sc_c(CSX), True, True,
                        qxt_tok=t_own)
            if stop_after == "quant2":
                return
            ffn_gateup(t_own)
            if stop_after == "gateup2":
                return
            armax(gh, cc_t["i5"], cc_t["o5"], s1r[0:1, 1:2], sc_c(CSH),
                  sc_c(SHS))
            ffn_down(t_own, xpp_h, out_ext, False, True, 0)

        _go()

        if dbg:
            with nc.Block() as blk:
                DQ = Stm("SQ")

                @blk.scalar
                def _(s):
                    DQ.issue(s.dma_start(dbg_h[:, 0:8], scal[:, :]))
                    DQ.issue(s.dma_start(dbg_h[:, 8:16], vrow[:, 0:8]))
                    DQ.issue(s.dma_start(dbg_h[:, 16:24], vrow[:, 8:16]))
                    DQ.issue(s.dma_start(dbg_h[:, 24:32], tot8[:, :]))
                    DQ.issue(s.dma_start(dbg_h[0:1, 32:64], s1r[:, :]))
                    DQ.wait(s, 4)

        with nc.Block() as blk:
            @blk.gpsimd
            def _(g):
                g.wait_ge(sh("SO0"), sv("SO0"))
                g.wait_ge(sh("SO1"), sv("SO1"))

    return nc


def kernel(**inputs):
    x = np.ascontiguousarray(np.asarray(inputs["L0_attn_input_data"],
                                        dtype=np.float32))
    B, S, _ = x.shape
    t_own = S // 2
    nc = build_nc(t_own)

    w1 = np.ascontiguousarray(inputs["L0_attn_W1"], dtype=np.float32)
    b1 = np.ascontiguousarray(inputs["L0_attn_bias1"], dtype=np.float32)
    w2 = np.ascontiguousarray(inputs["L0_attn_W2"], dtype=np.float32)
    b2 = np.ascontiguousarray(inputs["L0_attn_bias2"], dtype=np.float32)
    mw = np.ascontiguousarray(inputs["L0_attn_memory_weights"],
                              dtype=np.float32)
    cx = np.asarray(inputs["L0_ffn_clip_x"], dtype=np.float32).reshape(1, 1)
    cd = np.asarray(inputs["L0_ffn_clip_down"],
                    dtype=np.float32).reshape(1, 1)
    wg = np.ascontiguousarray(inputs["L0_ffn_w_gate"], dtype=np.float32)
    wu = np.ascontiguousarray(inputs["L0_ffn_w_up"], dtype=np.float32)
    wd = np.ascontiguousarray(inputs["L0_ffn_w_down"], dtype=np.float32)

    in_maps = []
    for c in range(8):
        b, h = c // 2, c % 2
        if h == 0:
            halo = np.zeros((HALO, D), dtype=np.float32)
            hmv = 0.0
        else:
            halo = x[b, t_own - HALO:t_own]
            hmv = 1.0
        shard = np.concatenate([halo, x[b, h * t_own:(h + 1) * t_own]],
                               axis=0)
        bsel = np.zeros((1, 4), dtype=np.float32)
        bsel[0, b] = 1.0
        in_maps.append({
            "x": np.ascontiguousarray(shard),
            "hmask": np.array([[hmv]], dtype=np.float32),
            "bsel": bsel,
            "W1": w1, "b1": b1, "W2": w2, "b2": b2, "mw": mw,
            "clip_x": cx, "clip_down": cd,
            "w_gate": wg, "w_up": wu, "w_down": wd,
        })

    res = run_bass_kernel_spmd(nc, in_maps, core_ids=list(range(8)))
    out = np.empty((B, S, D), dtype=np.float32)
    for c in range(8):
        b, h = c // 2, c % 2
        out[b, h * t_own:(h + 1) * t_own] = res.results[c]["out"]
    return out


# version 12
# speedup vs baseline: 1.0611x; 1.0415x over previous
import contextlib

import numpy as np

import concourse.bass as bass
import concourse.mybir as mybir
from concourse.alu_op_type import AluOpType as AO
from concourse.bass_utils import run_bass_kernel_spmd

F32 = mybir.dt.float32
BF16 = mybir.dt.bfloat16
I32 = mybir.dt.int32
AF = mybir.ActivationFunctionType
AX = mybir.AxisListType

D = 1024
IFF = 2816
DC = D // 128
ICN = IFF // 128
NTAP = 40
QMAX = 127.0
EPS = 1e-5
HALO = 128
CLIP_LO = -(QMAX + 1.49)
CLIP_HI = QMAX + 0.49


def _tiles(total, step=512):
    out, t = [], 0
    while t < total:
        out.append((t, min(step, total - t)))
        t += step
    return out


def build_nc(t_own, dbg=False, stop_after=None):
    nc = bass.Bass("TRN2", target_bir_lowering=False)
    t_tot = t_own + HALO
    n128 = t_tot // 128
    n128o = t_own // 128
    NT = _tiles(t_tot)
    TSEQ = 2 * t_own

    x_in = nc.dram_tensor("x", [t_tot, D], F32, kind="ExternalInput").ap()
    hm_in = nc.dram_tensor("hmask", [1, 1], F32, kind="ExternalInput").ap()
    bsel_in = nc.dram_tensor("bsel", [1, 4], F32, kind="ExternalInput").ap()
    w1_in = nc.dram_tensor("W1", [D, D], F32, kind="ExternalInput").ap()
    b1_in = nc.dram_tensor("b1", [D], F32, kind="ExternalInput").ap()
    w2_in = nc.dram_tensor("W2", [D, D], F32, kind="ExternalInput").ap()
    b2_in = nc.dram_tensor("b2", [D], F32, kind="ExternalInput").ap()
    mw_in = nc.dram_tensor("mw", [NTAP + 1, D], F32, kind="ExternalInput").ap()
    cx_in = nc.dram_tensor("clip_x", [1, 1], F32, kind="ExternalInput").ap()
    cd_in = nc.dram_tensor("clip_down", [1, 1], F32, kind="ExternalInput").ap()
    wg_in = nc.dram_tensor("w_gate", [IFF, D], F32, kind="ExternalInput").ap()
    wu_in = nc.dram_tensor("w_up", [IFF, D], F32, kind="ExternalInput").ap()
    wd_in = nc.dram_tensor("w_down", [D, IFF], F32, kind="ExternalInput").ap()
    out_ext = nc.dram_tensor("out", [t_own, D], F32, kind="ExternalOutput").ap()

    KI = "ExternalOutput" if dbg else "Internal"
    xp_h = nc.dram_tensor("xp_h", [t_tot, D], F32, kind=KI).ap()
    xpp_h = nc.dram_tensor("xpp_h", [t_own, D], F32, kind=KI).ap()
    qa_h = nc.dram_tensor("qa_h", [t_tot, D], BF16, kind=KI).ap()
    h_h = nc.dram_tensor("h_h", [IFF, t_tot], F32, kind=KI).ap()
    dbg_h = nc.dram_tensor("dbg_h", [128, 64], F32, kind=KI).ap() if dbg \
        else None
    qwg_h = nc.dram_tensor("qwg_h", [IFF, D], BF16, kind="Internal").ap()
    qwu_h = nc.dram_tensor("qwu_h", [IFF, D], BF16, kind="Internal").ap()
    qwd_h = nc.dram_tensor("qwd_h", [D, IFF], BF16, kind="Internal").ap()
    wgT_h = nc.dram_tensor("wgT_h", [D, IFF], BF16, kind="Internal").ap()
    wuT_h = nc.dram_tensor("wuT_h", [D, IFF], BF16, kind="Internal").ap()
    wdT_h = nc.dram_tensor("wdT_h", [IFF, D], BF16, kind="Internal").ap()
    misc_h = nc.dram_tensor("misc_h", [4096], F32, kind="Internal").ap()
    diag_h = nc.dram_tensor("diag_h", [DC * 128 * NTAP * 128], BF16,
                            kind="Internal").ap()
    cc_t = {}
    for i, shp in (("1", [1, 1]), ("2", [1, 1]), ("3", [32, 128]),
                   ("4", [1, 1]), ("5", [1, 1])):
        cc_t["i" + i] = nc.dram_tensor("cc%si" % i, shp, F32, kind="Internal")
        cc_t["o" + i] = nc.dram_tensor("cc%so" % i, shp, F32, kind="Internal")

    ctx = contextlib.ExitStack()
    with ctx:
        sbuf = lambda name, shape, dt: ctx.enter_context(
            nc.sbuf_tensor(name, shape, dt))

        AR_N = max(DC * t_tot, 2 * ICN * 512 + ICN * 1024,
                   2 * DC * 1024 + 2 * DC * 512 + DC * 640 + DC * 512)
        arena = sbuf("arena", [128, AR_N], BF16)
        qx_fm = arena[:, :DC * t_tot].rearrange("p (c t) -> p c t", c=DC)
        qh_v = arena[:, :2 * ICN * 512].rearrange(
            "p (b c t) -> p b c t", b=2, c=ICN)
        wdT_v = arena[:, 2 * ICN * 512:2 * ICN * 512 + ICN * 1024].rearrange(
            "p (c t) -> p c t", c=ICN)
        _o = 0
        w1_v = arena[:, _o:_o + DC * 1024].rearrange("p (c t) -> p c t", c=DC)
        _o += DC * 1024
        w2_v = arena[:, _o:_o + DC * 1024].rearrange("p (c t) -> p c t", c=DC)
        _o += DC * 1024
        x2f_v = arena[:, _o:_o + 2 * DC * 512].rearrange(
            "p (b c t) -> p b c t", b=2, c=DC)
        _o += 2 * DC * 512
        pr_v = arena[:, _o:_o + DC * 640].rearrange("p (c t) -> p c t", c=DC)
        _o += DC * 640
        z_v = arena[:, _o:_o + DC * 512].rearrange("p (c t) -> p c t", c=DC)
        _o += DC * 512
        stg_v = arena[:, :IFF]
        stg2_v = arena[:, IFF:IFF + 1024]

        xa = sbuf("xa", [128, 2, D], F32)
        xb = sbuf("xb", [128, 2, D], F32)
        qt = sbuf("qt", [128, 2, D], F32)
        qb = sbuf("qb", [128, 2, D], BF16)
        ab = sbuf("ab", [128, 2, D], F32)
        ob = sbuf("ob", [128, 2, D], F32)
        wgc = sbuf("wgc", [128, 2, DC, 128], BF16)
        wuc = sbuf("wuc", [128, 2, DC, 128], BF16)
        st_t = sbuf("st_t", [128, 4, 512], F32)
        hlu = sbuf("hlu", [128, 2, 2, 512], F32)
        ht = hlu[:, 0, :, :]
        hl = hlu
        hq_ti = sbuf("hq_ti", [128, 2, 2, 512], I32)
        qi = hq_ti[:, 0].rearrange("p a b -> p (a b)")
        diagc = sbuf("diagc", [128, 2, NTAP, 128], BF16)
        idf = sbuf("idf", [128, 128], F32)
        idb = sbuf("idb", [128, 128], BF16)
        onesf = sbuf("onesf", [1, 128], F32)
        io1 = sbuf("io1", [128, 128], I32)
        io2 = sbuf("io2", [128, 128], I32)

        mstat = sbuf("mstat", [128, 6 * n128], F32)
        mr = lambda w, t: mstat[:, w * n128 + t:w * n128 + t + 1]
        mrr = lambda w, n: mstat[:, w * n128:w * n128 + n]
        sums = sbuf("sums", [128, 2 * n128], F32)
        s_sum = lambda t: sums[:, t:t + 1]
        s_ssq = lambda t: sums[:, n128 + t:n128 + t + 1]
        sgu = sbuf("sgu", [128, 2 * ICN + DC], F32)
        sgc = lambda i: sgu[:, i:i + 1]
        suc = lambda i: sgu[:, ICN + i:ICN + i + 1]
        sdc = lambda i: sgu[:, 2 * ICN + i:2 * ICN + i + 1]
        cgu = sbuf("cgu", [128, 2 * ICN], F32)
        cgc = lambda i: cgu[:, i:i + 1]
        cuc = lambda i: cgu[:, ICN + i:ICN + i + 1]
        mwf = sbuf("mwf", [128, DC, NTAP + 1], F32)
        mwe = sbuf("mwe", [128, DC, NTAP], F32)
        b8 = sbuf("b8", [8, 2 * 128], F32)
        b1f = sbuf("b1f", [128, DC], F32)
        b2f = sbuf("b2f", [128, DC], F32)
        b1T = sbuf("b1T", [128, DC], F32)
        scal = sbuf("scal", [128, 8], F32)
        CSX, CSH, HMASK, SHS = 0, 1, 2, 3
        sc_c = lambda i: scal[:, i:i + 1]
        s1r = sbuf("s1r", [1, 32], F32)
        vrow = sbuf("vrow", [128, D], F32)
        tot8 = sbuf("tot8", [128, DC], F32)
        ctr32 = sbuf("ctr32", [128, 32], F32)
        ar32 = sbuf("ar32", [32, 128], F32)
        allT = sbuf("allT", [128, 32], F32)
        bsel_b = sbuf("bsel_b", [128, 4], F32)
        rowD = sbuf("rowD", [1, D], F32)
        rowD2 = sbuf("rowD2", [1, D], F32)
        t1 = sbuf("t1", [128, 64], F32)
        t2 = sbuf("t2", [128, 64], F32)
        t3 = sbuf("t3", [128, 64], F32)
        t4 = sbuf("t4", [128, 64], F32)
        gy = sbuf("gy", [128, 1], F32)
        gh = sbuf("gh", [128, 1], F32)
        u8 = sbuf("u8", [128, DC], F32)
        u8b = sbuf("u8b", [128, DC], BF16)
        t8b = sbuf("t8b", [128, DC], BF16)
        totf = sbuf("totf", [128, DC], F32)

        ps = [ctx.enter_context(nc.psum_tensor("ps%d" % i, [128, 512], F32))
              for i in range(8)]

        sems = {}
        for s in ["VA", "VB", "VC", "AA", "AB", "TA", "TB", "TC",
                  "PFA", "DG", "CC", "GI"]:
            sems[s] = [ctx.enter_context(nc.semaphore(s)), 0]
        for s in ["LA", "LB", "LW", "TR", "SQ", "SG", "SO", "LH", "CQ"]:
            sems[s + "0"] = [ctx.enter_context(nc.semaphore(s + "0")), 0]
            sems[s + "1"] = [ctx.enter_context(nc.semaphore(s + "1")), 0]

        def inc(inst, s, n):
            sems[s][1] += n
            inst.then_inc(sems[s][0], n)
            return sems[s][1]

        def sh(s):
            return sems[s][0]

        def sv(s):
            return sems[s][1]

        class Stm:

            def __init__(self, name, per=1):
                self.k = [name + "0", name + "1"]
                self.b = [sv(self.k[0]), sv(self.k[1])]
                self.per = per
                self.d = 0

            def issue(self, inst):
                step = self.d // self.per
                inc(inst, self.k[step % 2], 16)
                self.d += 1

            def wait(self, eng, step):
                eng.wait_ge(sh(self.k[step % 2]),
                            self.b[step % 2] + 16 * self.per * (step // 2 + 1))

            def wait_all(self, eng):
                eng.wait_ge(sh(self.k[0]), sv(self.k[0]))
                eng.wait_ge(sh(self.k[1]), sv(self.k[1]))

        def wait_all_stores(eng):
            for s in ("SQ0", "SQ1", "SG0", "SG1", "SO0", "SO1"):
                eng.wait_ge(sh(s), sv(s))

        def entry_guard():
            vals = {s: sv(s) for s in ("SQ0", "SQ1", "SG0", "SG1",
                                       "SO0", "SO1")}

            def g(eng):
                for s, v in vals.items():
                    eng.wait_ge(sh(s), v)
            return g

        with nc.Block() as blk:
            L0a = Stm("LW")

            @blk.sync
            def _(sync):
                L0a.issue(sync.dma_start(qt[0:NTAP + 1, 0, :], mw_in[:, :]))
                L0a.issue(sync.dma_start(
                    b8[0:8, 0:128], b1_in.rearrange("(c p) -> c p", p=128)))
                L0a.issue(sync.dma_start(
                    b8[0:8, 128:256], b2_in.rearrange("(c p) -> c p", p=128)))
                sync.wait_ge(sh("GI"), 1)
                L0a.issue(sync.dma_start(s1r[0:1, 0:1], cx_in[:, :]))
                L0a.issue(sync.dma_start(s1r[0:1, 1:2], cd_in[:, :]))
                L0a.issue(sync.dma_start(s1r[0:1, 2:3], hm_in[:, :]))
                L0a.issue(sync.dma_start(s1r[0:1, 4:8], bsel_in[:, :]))

            @blk.gpsimd
            def _(g):
                g.iota(io1[:, :], [[1, 128]], channel_multiplier=0)
                g.iota(io2[:, :], [[0, 128]], channel_multiplier=1)
                g.memset(onesf[:, :], 1.0)
                g.memset(gy[:, :], 0.0)
                g.memset(gh[:, :], 0.0)
                g.memset(tot8[:, :], 0.0)
                g.memset(s1r[:, :], 0.0)
                g.memset(t1[:, :], 0.0)
                g.memset(scal[:, :], 0.0)
                g.memset(vrow[:, :], 0.0)
                inc(g.memset(pr_v[:, :, :], 0.0), "GI", 1)

            @blk.vector
            def _(vec):
                vec.wait_ge(sh("GI"), 1)
                vec.tensor_tensor(idf[:, :], io1[:, :], io2[:, :],
                                  op=AO.is_equal)
                vec.drain()
                vec.tensor_copy(idb[:, :], idf[:, :])
                vec.drain()

        with nc.Block() as blk:
            bTA, bVA = sv("TA"), sv("VA")
            SQb = Stm("SQ")

            @blk.tensor
            def _(t):
                L0a.wait_all(t)
                for c in range(DC):
                    inc(t.matmul(ps[c][:, 0:NTAP + 1],
                                 qt[0:NTAP + 1, 0, c * 128:(c + 1) * 128],
                                 idf[0:NTAP + 1, 0:NTAP + 1],
                                 start=True, stop=True), "TA", 1)
                t.wait_ge(sh("VA"), bVA + 1)
                inc(t.matmul(ps[0][:, 0:DC], b8[0:8, 0:128], idf[0:8, 0:8],
                             start=True, stop=True), "TA", 1)
                inc(t.matmul(ps[1][:, 0:DC], b8[0:8, 128:256], idf[0:8, 0:8],
                             start=True, stop=True), "TA", 1)
                inc(t.matmul(ps[2][:, 0:1], onesf[:, :], s1r[0:1, 2:3],
                             start=True, stop=True), "TA", 1)
                inc(t.matmul(ps[3][:, 0:4], onesf[:, :], s1r[0:1, 4:8],
                             start=True, stop=True), "TA", 1)
                t.wait_ge(sh("VA"), bVA + 2)
                inc(t.matmul(ps[4][0:DC, 0:128], b2f[:, :], idf[:, :],
                             start=True, stop=True), "TA", 1)

            @blk.vector
            def _(vec):
                vec.wait_ge(sh("TA"), bTA + DC)
                for c in range(DC):
                    vec.tensor_copy(mwf[:, c, :], ps[c][:, 0:NTAP + 1])
                vec.drain()
                inc(vec.tensor_copy(t1[:, 0:1], mwf[:, 0, 0:1]), "VA", 1)
                vec.wait_ge(sh("TA"), bTA + DC + 4)
                vec.tensor_copy(b1f[:, :], ps[0][:, 0:DC])
                vec.tensor_copy(b2f[:, :], ps[1][:, 0:DC])
                vec.tensor_copy(sc_c(HMASK), ps[2][:, 0:1])
                vec.tensor_copy(bsel_b[:, :], ps[3][:, 0:4])
                vec.drain()
                vec.tensor_scalar(b1T[:, :], b1f[:, :], float(TSEQ), None,
                                  op0=AO.mult)
                for c in range(DC):
                    vec.tensor_scalar(mwe[:, c, :], mwf[:, c, 1:],
                                      mwf[:, c, 0:1], None, op0=AO.subtract)
                vec.drain()
                inc(vec.tensor_copy(t1[:, 0:1], b1f[:, 0:1]), "VA", 1)
                vec.wait_ge(sh("TA"), bTA + DC + 5)
                inc(vec.tensor_copy(ar32[0:DC, :], ps[4][0:DC, 0:128]),
                    "VA", 1)
                vec.drain()

            @blk.scalar
            def _(s):
                s.wait_ge(sh("VA"), bVA + 3)
                SQb.issue(s.dma_start(misc_h[1024:1024 + D].rearrange(
                    "(c p) -> c p", p=128), ar32[0:DC, :]))
                SQb.wait(s, 0)
                SQb.issue(s.dma_start(rowD2[0:1, :],
                                      misc_h[None, 1024:1024 + D]))
                SQb.wait(s, 1)

        with nc.Block() as blk:
            bVA = sv("VA")
            SQp = Stm("SQ")

            @blk.vector
            def _(vec):
                for c in range(DC):
                    if c >= 2:
                        SQp.wait(vec, c - 2)
                    for j in range(NTAP):
                        vec.tensor_scalar(diagc[:, c % 2, j, :], idb[:, :],
                                          mwe[:, c, j:j + 1], None,
                                          op0=AO.mult)
                    vec.drain()
                    inc(vec.tensor_copy(t1[:, 0:1], t1[:, 0:1]), "VA", 1)
                    vec.drain()

            @blk.scalar
            def _(s):
                for c in range(DC):
                    s.wait_ge(sh("VA"), bVA + c + 1)
                    SQp.issue(s.dma_start(
                        bass.AP(diag_h.tensor, c * 128 * NTAP * 128,
                                [[NTAP * 128, 128], [1, NTAP * 128]]),
                        diagc[:, c % 2, :, :]))

        def head_merged():
            PCS = [(0, 1024), (1024, 1024), (2048, 768)]
            qunits = ([('g', t) for t in range(ICN)] +
                      [('u', t) for t in range(ICN)] +
                      [('d', t) for t in range(DC)])
            npl = n128 // 2
            lunits = [('l', p) for p in range(npl)]
            if n128 % 2:
                lunits.append(('ls', 2 * npl))

            def abs_pos(t):
                if t < 2 * npl:
                    return 4 * (t // 2) + 3 + (t % 2)
                return 4 * npl + 2

            U = []
            li = qi_ = 0
            nqu, nlu = len(qunits), len(lunits)
            while li < nlu or qi_ < nqu:
                take_q = 1 if qi_ < nqu else 0
                if take_q and (li >= nlu or qi_ * nlu <= li * nqu):
                    U.append(qunits[qi_])
                    qi_ += 1
                else:
                    U.append(lunits[li])
                    li += 1
            la_steps = {}
            qstep_of = {}
            lab_of = {}
            vcr_of = {}
            la = 0
            qs = 0
            nlab = 0
            nvc = 0
            for u in U:
                kind, i = u
                if kind in ('g', 'u'):
                    la_steps[u] = [la]
                    la += 1
                    qstep_of[u] = [qs]
                    qs += 1
                elif kind == 'd':
                    qstep_of[u] = [qs, qs + 1, qs + 2]
                    qs += 3
                else:
                    nlab += 1
                    lab_of[u] = nlab
                    w_ = 2 if kind == 'l' else 1
                    vcr_of[u] = nvc + 1
                    nvc += 2 * w_
            nq = qs
            wd_stage = [(qt[:, 0, :], qt[:, 1, :]), None]

            with nc.Block() as blk:
                bVA, bVB, bVC = sv("VA"), sv("VB"), sv("VC")
                bAA, bAB, bTB, bTC = sv("AA"), sv("AB"), sv("TB"), sv("TC")
                LA, SQ = Stm("LA"), Stm("SQ")
                LB = Stm("CQ")
                LDW = Stm("LH", per=3)
                eg = entry_guard()

                @blk.sync
                def _(sync):
                    wait_all_stores(sync)
                    for u in U:
                        kind, i = u
                        if kind in ('g', 'u'):
                            q = qstep_of[u][0]
                            if q >= 2:
                                sync.wait_ge(sh("VA"), bVA + (q - 1))
                            wap = wg_in if kind == 'g' else wu_in
                            LA.issue(sync.dma_start(
                                xa[:, q % 2, :],
                                wap[i * 128:(i + 1) * 128, :]))
                        elif kind == 'd':
                            if i >= 1:
                                sync.wait_ge(sh("VB"), bVB + i)
                            for p3, (o3, n3) in enumerate(PCS):
                                dst = (qt[:, 0, :n3], qt[:, 1, :n3],
                                       ob[:, 0, :n3])[p3]
                                LDW.issue(sync.dma_start(
                                    dst,
                                    wd_in[i * 128:(i + 1) * 128,
                                          o3:o3 + n3]))

                @blk.gpsimd
                def _(g):
                    for u in U:
                        kind, i = u
                        if kind not in ('l', 'ls'):
                            continue
                        ts_ = [2 * i, 2 * i + 1] if kind == 'l' else [i]
                        for t in ts_:
                            if t >= 2:
                                g.wait_ge(sh("AA"), bAA + abs_pos(t - 2))
                            LB.issue(g.dma_start(
                                xb[:, t % 2, :],
                                x_in[t * 128:(t + 1) * 128, :]))

                @blk.vector
                def _(vec):
                    eg(vec)
                    for u in U:
                        kind, i = u
                        if kind in ('g', 'u'):
                            q = qstep_of[u][0]
                            LA.wait(vec, la_steps[u][0])
                            if q >= 2:
                                SQ.wait(vec, q - 2)
                            w_t = xa[:, q % 2, :]
                            scol = sgc if kind == 'g' else suc
                            vec.tensor_reduce(t1[:, 0:1], w_t, axis=AX.X,
                                              op=AO.max,
                                              apply_absolute_value=True)
                            vec.drain()
                            vec.reciprocal(t1[:, 1:2], t1[:, 0:1])
                            vec.drain()
                            vec.tensor_scalar(scol(i), t1[:, 0:1],
                                              1.0 / QMAX, None, op0=AO.mult)
                            vec.tensor_scalar(t1[:, 2:3], t1[:, 1:2], QMAX,
                                              None, op0=AO.mult)
                            vec.drain()
                            vec.tensor_scalar(qi[:, :], w_t, t1[:, 2:3],
                                              None, op0=AO.mult)
                            vec.drain()
                            inc(vec.tensor_copy(qb[:, q % 2, :], qi[:, :]),
                                "VA", 1)
                            vec.drain()
                        elif kind == 'd':
                            LDW.wait(vec, i)
                            srcs = (qt[:, 0, :], qt[:, 1, :], ob[:, 0, :])
                            for p3, (o3, n3) in enumerate(PCS):
                                vec.tensor_reduce(t1[:, 3 + p3:4 + p3],
                                                  srcs[p3][:, :n3], axis=AX.X,
                                                  op=AO.max,
                                                  apply_absolute_value=True)
                                vec.drain()
                            vec.tensor_reduce(t1[:, 6:7], t1[:, 3:6],
                                              axis=AX.X, op=AO.max)
                            vec.drain()
                            vec.reciprocal(t1[:, 7:8], t1[:, 6:7])
                            vec.drain()
                            vec.tensor_scalar(sdc(i), t1[:, 6:7], 1.0 / QMAX,
                                              None, op0=AO.mult)
                            vec.tensor_scalar(t1[:, 8:9], t1[:, 7:8], QMAX,
                                              None, op0=AO.mult)
                            vec.drain()
                            vec.tensor_scalar(t1[:, 9:10], sdc(i), 0.5, None,
                                              op0=AO.mult)
                            vec.drain()
                            for p3, (o3, n3) in enumerate(PCS):
                                q = qstep_of[u][p3]
                                if q >= 2:
                                    SQ.wait(vec, q - 2)
                                vec.tensor_scalar(qi[:, :n3],
                                                  srcs[p3][:, :n3],
                                                  t1[:, 8:9], None,
                                                  op0=AO.mult)
                                vec.drain()
                                inc(vec.tensor_scalar(qb[:, q % 2, :n3],
                                                      qi[:, :n3],
                                                      t1[:, 9:10], None,
                                                      op0=AO.mult), "VA", 1)
                                vec.drain()
                            inc(vec.tensor_copy(t1[:, 0:1], t1[:, 0:1]),
                                "VB", 1)
                        else:
                            ts_ = ([2 * i, 2 * i + 1] if kind == 'l'
                                   else [i])
                            t0 = ts_[0]
                            w = len(ts_)
                            c = 10 + 16 * (i % 2)
                            for tr, t in enumerate(ts_):
                                LB.wait(vec, t)
                                inc(vec.tensor_reduce(
                                    s_sum(t), xb[:, t % 2, :],
                                    axis=AX.X, op=AO.add), "VC", 1)
                            vec.wait_ge(sh("AA"), bAA + abs_pos(t0) - 1)
                            vec.tensor_scalar(
                                mstat[:, t0:t0 + w], sums[:, t0:t0 + w],
                                1.0 / D, None, op0=AO.mult)
                            vec.tensor_scalar(
                                t1[:, c + 2:c + 2 + w],
                                sums[:, n128 + t0:n128 + t0 + w],
                                1.0 / D, EPS, op0=AO.mult, op1=AO.add)
                            vec.drain()
                            vec.tensor_tensor(t1[:, c:c + w],
                                              mstat[:, t0:t0 + w],
                                              mstat[:, t0:t0 + w],
                                              op=AO.mult)
                            vec.drain()
                            vec.tensor_tensor(t1[:, c + 4:c + 4 + w],
                                              t1[:, c + 2:c + 2 + w],
                                              t1[:, c:c + w],
                                              op=AO.subtract)
                            vec.drain()
                            inc(vec.reciprocal(t1[:, c + 6:c + 6 + w],
                                               t1[:, c + 4:c + 4 + w]),
                                "TB", 1)
                            vec.drain()
                            vec.wait_ge(sh("AB"), bAB + lab_of[u])
                            vec.reciprocal(t1[:, c + 10:c + 10 + w],
                                           t1[:, c + 8:c + 8 + w])
                            vec.drain()
                            vec.tensor_tensor(t1[:, c + 10:c + 10 + w],
                                              t1[:, c + 10:c + 10 + w],
                                              t1[:, c + 6:c + 6 + w],
                                              op=AO.mult)
                            vec.drain()
                            vec.tensor_tensor(t1[:, c + 10:c + 10 + w],
                                              t1[:, c + 10:c + 10 + w],
                                              t1[:, c + 8:c + 8 + w],
                                              op=AO.add)
                            vec.drain()
                            vec.tensor_scalar(
                                mstat[:, n128 + t0:n128 + t0 + w],
                                t1[:, c + 10:c + 10 + w], 0.5, None,
                                op0=AO.mult)
                            vec.drain()
                            inc(vec.scalar_tensor_tensor(
                                t1[:, c + 12:c + 12 + w],
                                mstat[:, t0:t0 + w], -1.0,
                                mstat[:, n128 + t0:n128 + t0 + w],
                                op0=AO.mult, op1=AO.mult), "TC", 1)
                            for tr, t in enumerate(ts_):
                                vec.wait_ge(sh("AA"),
                                            bAA + abs_pos(t0) + tr)
                                vec.tensor_reduce(
                                    t1[:, c + 14 + tr:c + 15 + tr],
                                    ab[:, t % 2, :], axis=AX.X, op=AO.max)
                                vec.drain()
                                inc(vec.tensor_tensor(
                                    gy[:, :], gy[:, :],
                                    t1[:, c + 14 + tr:c + 15 + tr],
                                    op=AO.max), "VC", 1)
                                vec.drain()

                @blk.scalar
                def _(s):
                    eg(s)
                    for u in U:
                        kind, i = u
                        if kind in ('g', 'u'):
                            q = qstep_of[u][0]
                            s.wait_ge(sh("VA"), bVA + q + 1)
                            q_dst = qwg_h if kind == 'g' else qwu_h
                            SQ.issue(s.dma_start(
                                q_dst[i * 128:(i + 1) * 128, :],
                                qb[:, q % 2, :]))
                        elif kind == 'd':
                            for p3, (o3, n3) in enumerate(PCS):
                                q = qstep_of[u][p3]
                                s.wait_ge(sh("VA"), bVA + q + 1)
                                SQ.issue(s.dma_start(
                                    qwd_h[i * 128:(i + 1) * 128,
                                          o3:o3 + n3],
                                    qb[:, q % 2, :n3]))
                        else:
                            ts_ = ([2 * i, 2 * i + 1] if kind == 'l'
                                   else [i])
                            t0 = ts_[0]
                            w = len(ts_)
                            c = 10 + 16 * (i % 2)
                            for tr, t in enumerate(ts_):
                                s.wait_ge(sh("VC"), bVC + vcr_of[u] + tr)
                                inc(s.activation(ab[:, t % 2, :],
                                                 xb[:, t % 2, :], AF.Square,
                                                 accum_out=s_ssq(t)),
                                    "AA", 1)
                            s.wait_ge(sh("TB"), bTB + lab_of[u])
                            inc(s.activation(t1[:, c + 8:c + 8 + w],
                                             t1[:, c + 6:c + 6 + w],
                                             AF.Sqrt), "AB", 1)
                            s.wait_ge(sh("TC"), bTC + lab_of[u])
                            for tr, t in enumerate(ts_):
                                inc(s.activation(
                                    ab[:, t % 2, :], xb[:, t % 2, :],
                                    AF.Abs,
                                    bias=t1[:, c + 12 + tr:c + 13 + tr],
                                    scale=mstat[:, n128 + t:n128 + t + 1]),
                                    "AA", 1)

        def quant_weight(wap, nchunk, scol, q_dst):
            with nc.Block() as blk:
                bVA = sv("VA")
                LA, SQ = Stm("LA"), Stm("SQ")
                eg = entry_guard()

                @blk.sync
                def _(sync):
                    for i in range(nchunk):
                        if i >= 2:
                            sync.wait_ge(sh("VA"), bVA + (i - 1))
                        LA.issue(sync.dma_start(
                            xa[:, i % 2, :], wap[i * 128:(i + 1) * 128, :]))

                @blk.vector
                def _(vec):
                    eg(vec)
                    for i in range(nchunk):
                        LA.wait(vec, i)
                        if i >= 2:
                            SQ.wait(vec, i - 2)
                        w_t = xa[:, i % 2, :]
                        vec.tensor_reduce(t1[:, 0:1], w_t, axis=AX.X,
                                          op=AO.max, apply_absolute_value=True)
                        vec.drain()
                        vec.reciprocal(t1[:, 1:2], t1[:, 0:1])
                        vec.drain()
                        vec.tensor_scalar(scol(i), t1[:, 0:1], 1.0 / QMAX,
                                          None, op0=AO.mult)
                        vec.tensor_scalar(t1[:, 2:3], t1[:, 1:2], QMAX, None,
                                          op0=AO.mult)
                        vec.drain()
                        vec.tensor_scalar(qi[:, :], w_t, t1[:, 2:3],
                                          None, op0=AO.mult)
                        vec.drain()
                        inc(vec.tensor_copy(qb[:, i % 2, :], qi[:, :]),
                            "VA", 1)
                        vec.drain()

                @blk.scalar
                def _(s):
                    for i in range(nchunk):
                        s.wait_ge(sh("VA"), bVA + (i + 1))
                        SQ.issue(s.dma_start(q_dst[i * 128:(i + 1) * 128, :],
                                             qb[:, i % 2, :]))

        head_merged()

        def quant_weight_wide():
            PCS = [(0, 1024), (1024, 1024), (2048, 768)]
            np3 = len(PCS)
            with nc.Block() as blk:
                bVA, bVB = sv("VA"), sv("VB")
                LA, SQ = Stm("LA"), Stm("SQ")
                eg = entry_guard()

                @blk.sync
                def _(sync):
                    j = 0
                    for i in range(DC):
                        for p3, (o3, n3) in enumerate(PCS):
                            if j >= 2:
                                sync.wait_ge(sh("VB"), bVB + (j - 1))
                            LA.issue(sync.dma_start(
                                xa[:, j % 2, :n3],
                                wd_in[i * 128:(i + 1) * 128, o3:o3 + n3]))
                            j += 1
                        for p3, (o3, n3) in enumerate(PCS):
                            sync.wait_ge(sh("VB"), bVB + (j - 1))
                            LA.issue(sync.dma_start(
                                xa[:, j % 2, :n3],
                                wd_in[i * 128:(i + 1) * 128, o3:o3 + n3]))
                            j += 1

                @blk.vector
                def _(vec):
                    eg(vec)
                    j = 0
                    for i in range(DC):
                        for p3, (o3, n3) in enumerate(PCS):
                            LA.wait(vec, j)
                            vec.tensor_reduce(t1[:, p3:p3 + 1],
                                              xa[:, j % 2, :n3], axis=AX.X,
                                              op=AO.max,
                                              apply_absolute_value=True)
                            vec.drain()
                            inc(vec.tensor_copy(t1[:, 8:9], t1[:, 0:1]),
                                "VB", 1)
                            j += 1
                        vec.tensor_reduce(t1[:, 4:5], t1[:, 0:np3], axis=AX.X,
                                          op=AO.max)
                        vec.drain()
                        vec.reciprocal(t1[:, 5:6], t1[:, 4:5])
                        vec.drain()
                        vec.tensor_scalar(sdc(i), t1[:, 4:5], 1.0 / QMAX,
                                          None, op0=AO.mult)
                        vec.tensor_scalar(t1[:, 6:7], t1[:, 5:6], QMAX, None,
                                          op0=AO.mult)
                        vec.drain()
                        vec.tensor_scalar(t1[:, 7:8], sdc(i), 0.5, None,
                                          op0=AO.mult)
                        vec.drain()
                        for p3, (o3, n3) in enumerate(PCS):
                            LA.wait(vec, j)
                            jq = i * np3 + p3
                            if jq >= 2:
                                SQ.wait(vec, jq - 2)
                            vec.tensor_scalar(qi[:, :n3],
                                              xa[:, j % 2, :n3], t1[:, 6:7],
                                              None, op0=AO.mult)
                            vec.drain()
                            inc(vec.tensor_scalar(qb[:, jq % 2, :n3],
                                                  qi[:, :n3], t1[:, 7:8],
                                                  None, op0=AO.mult),
                                "VA", 1)
                            inc(vec.tensor_copy(t1[:, 8:9], t1[:, 0:1]),
                                "VB", 1)
                            vec.drain()
                            j += 1

                @blk.scalar
                def _(s):
                    for i in range(DC):
                        for p3, (o3, n3) in enumerate(PCS):
                            jq = i * np3 + p3
                            s.wait_ge(sh("VA"), bVA + (jq + 1))
                            SQ.issue(s.dma_start(
                                qwd_h[i * 128:(i + 1) * 128, o3:o3 + n3],
                                qb[:, jq % 2, :n3]))


        def ln_batch_mr(ntile, m_which, r_which):
            with nc.Block() as blk:
                bVA, bAA = sv("VA"), sv("AA")
                n = ntile

                @blk.vector
                def _(vec):
                    vec.tensor_scalar(mrr(m_which, n), sums[:, 0:n], 1.0 / D,
                                      None, op0=AO.mult)
                    vec.drain()
                    vec.tensor_tensor(t1[:, :n], mrr(m_which, n),
                                      mrr(m_which, n), op=AO.mult)
                    vec.drain()
                    vec.scalar_tensor_tensor(t2[:, :n], sums[:, n128:n128 + n],
                                             1.0 / D, t1[:, :n], op0=AO.mult,
                                             op1=AO.subtract)
                    vec.drain()
                    vec.tensor_scalar(t2[:, :n], t2[:, :n], EPS, None,
                                      op0=AO.add)
                    vec.drain()
                    inc(vec.reciprocal(t3[:, :n], t2[:, :n]), "VA", 1)
                    vec.drain()
                    vec.wait_ge(sh("AA"), bAA + 1)
                    vec.reciprocal(t2[:, :n], t4[:, :n])
                    vec.drain()
                    vec.tensor_tensor(t2[:, :n], t2[:, :n], t3[:, :n],
                                      op=AO.mult)
                    vec.drain()
                    vec.tensor_tensor(t2[:, :n], t2[:, :n], t4[:, :n],
                                      op=AO.add)
                    vec.drain()
                    vec.tensor_scalar(mrr(r_which, n), t2[:, :n], 0.5, None,
                                      op0=AO.mult)
                    vec.drain()

                @blk.scalar
                def _(s):
                    s.wait_ge(sh("VA"), bVA + 1)
                    inc(s.activation(t4[:, :n], t3[:, :n], AF.Sqrt), "AA", 1)

        def ln_stats_ymax(src_h, ntile, m_which, r_which, vrow_add, gdst):
            assert ntile % 2 == 0
            npair = ntile // 2

            def mp(w, t0):
                return mstat[:, w * n128 + t0:w * n128 + t0 + 2]

            with nc.Block() as blk:
                bVA, bVB, bVC = sv("VA"), sv("VB"), sv("VC")
                bAA, bAB = sv("AA"), sv("AB")
                LA = Stm("LA")

                @blk.sync
                def _(sync):
                    wait_all_stores(sync)
                    for t in range(ntile):
                        if t >= 2:
                            tp, tr = (t - 2) // 2, (t - 2) % 2
                            sync.wait_ge(sh("AA"), bAA + 4 * tp + 3 + tr)
                        LA.issue(sync.dma_start(
                            xa[:, t % 2, :], src_h[t * 128:(t + 1) * 128, :]))

                @blk.vector
                def _(vec):
                    for p in range(npair):
                        t0 = 2 * p
                        c = 10 + 16 * (p % 2)
                        for tr in range(2):
                            LA.wait(vec, t0 + tr)
                            if vrow_add:
                                vec.tensor_tensor(xa[:, (t0 + tr) % 2, :],
                                                  xa[:, (t0 + tr) % 2, :],
                                                  vrow[:, :], op=AO.add)
                                vec.drain()
                            inc(vec.tensor_reduce(s_sum(t0 + tr),
                                                  xa[:, (t0 + tr) % 2, :],
                                                  axis=AX.X, op=AO.add),
                                "VA", 1)
                        vec.wait_ge(sh("AA"), bAA + 4 * p + 2)
                        vec.tensor_scalar(mp(m_which, t0),
                                          sums[:, t0:t0 + 2], 1.0 / D,
                                          None, op0=AO.mult)
                        vec.tensor_scalar(t1[:, c + 2:c + 4],
                                          sums[:, n128 + t0:n128 + t0 + 2],
                                          1.0 / D, EPS, op0=AO.mult,
                                          op1=AO.add)
                        vec.drain()
                        vec.tensor_tensor(t1[:, c:c + 2], mp(m_which, t0),
                                          mp(m_which, t0), op=AO.mult)
                        vec.drain()
                        vec.tensor_tensor(t1[:, c + 4:c + 6],
                                          t1[:, c + 2:c + 4],
                                          t1[:, c:c + 2], op=AO.subtract)
                        vec.drain()
                        inc(vec.reciprocal(t1[:, c + 6:c + 8],
                                           t1[:, c + 4:c + 6]), "VB", 1)
                        vec.drain()
                        vec.wait_ge(sh("AB"), bAB + p + 1)
                        vec.reciprocal(t1[:, c + 10:c + 12],
                                       t1[:, c + 8:c + 10])
                        vec.drain()
                        vec.tensor_tensor(t1[:, c + 10:c + 12],
                                          t1[:, c + 10:c + 12],
                                          t1[:, c + 6:c + 8], op=AO.mult)
                        vec.drain()
                        vec.tensor_tensor(t1[:, c + 10:c + 12],
                                          t1[:, c + 10:c + 12],
                                          t1[:, c + 8:c + 10], op=AO.add)
                        vec.drain()
                        vec.tensor_scalar(mp(r_which, t0),
                                          t1[:, c + 10:c + 12], 0.5, None,
                                          op0=AO.mult)
                        vec.drain()
                        inc(vec.scalar_tensor_tensor(
                            t1[:, c + 12:c + 14], mp(m_which, t0), -1.0,
                            mp(r_which, t0), op0=AO.mult, op1=AO.mult),
                            "VC", 1)
                        for tr in range(2):
                            vec.wait_ge(sh("AA"), bAA + 4 * p + 3 + tr)
                            vec.tensor_reduce(t1[:, c + 14 + tr:c + 15 + tr],
                                              ab[:, (t0 + tr) % 2, :],
                                              axis=AX.X, op=AO.max)
                            vec.drain()
                            inc(vec.tensor_tensor(
                                gdst[:, :], gdst[:, :],
                                t1[:, c + 14 + tr:c + 15 + tr],
                                op=AO.max), "VA", 1)
                            vec.drain()

                @blk.scalar
                def _(s):
                    for p in range(npair):
                        t0 = 2 * p
                        c = 10 + 16 * (p % 2)
                        for tr in range(2):
                            s.wait_ge(sh("VA"), bVA + 4 * p + 1 + tr)
                            inc(s.activation(ab[:, (t0 + tr) % 2, :],
                                             xa[:, (t0 + tr) % 2, :],
                                             AF.Square,
                                             accum_out=s_ssq(t0 + tr)),
                                "AA", 1)
                        s.wait_ge(sh("VB"), bVB + p + 1)
                        inc(s.activation(t1[:, c + 8:c + 10],
                                         t1[:, c + 6:c + 8], AF.Sqrt),
                            "AB", 1)
                        s.wait_ge(sh("VC"), bVC + p + 1)
                        for tr in range(2):
                            inc(s.activation(
                                ab[:, (t0 + tr) % 2, :],
                                xa[:, (t0 + tr) % 2, :], AF.Abs,
                                bias=t1[:, c + 12 + tr:c + 13 + tr],
                                scale=mr(r_which, t0 + tr)), "AA", 1)

        def ln_stats(src_h, ntile, m_which, r_which, vrow_add):
            with nc.Block() as blk:
                bVA, bAA = sv("VA"), sv("AA")
                LA = Stm("LA")

                @blk.sync
                def _(sync):
                    wait_all_stores(sync)
                    for t in range(ntile):
                        if t >= 2:
                            sync.wait_ge(sh("AA"), bAA + (t - 1))
                        LA.issue(sync.dma_start(
                            xa[:, t % 2, :], src_h[t * 128:(t + 1) * 128, :]))

                @blk.vector
                def _(vec):
                    for t in range(ntile):
                        LA.wait(vec, t)
                        if vrow_add:
                            vec.tensor_tensor(xa[:, t % 2, :], xa[:, t % 2, :],
                                              vrow[:, :], op=AO.add)
                            vec.drain()
                        inc(vec.tensor_reduce(s_sum(t), xa[:, t % 2, :],
                                              axis=AX.X, op=AO.add), "VA", 1)

                @blk.scalar
                def _(s):
                    for t in range(ntile):
                        s.wait_ge(sh("VA"), bVA + t + 1)
                        inc(s.activation(ab[:, t % 2, :], xa[:, t % 2, :],
                                         AF.Square, accum_out=s_ssq(t)),
                            "AA", 1)

            ln_batch_mr(ntile, m_which, r_which)

        def ymax(src_h, ntile, m_which, r_which, vrow_add, gdst):
            with nc.Block() as blk:
                bVA, bVB, bAA = sv("VA"), sv("VB"), sv("AA")
                LB = Stm("LB")

                @blk.sync
                def _(sync):
                    wait_all_stores(sync)
                    for t in range(ntile):
                        if t >= 2:
                            sync.wait_ge(sh("AA"), bAA + (t - 1))
                        LB.issue(sync.dma_start(
                            xb[:, t % 2, :], src_h[t * 128:(t + 1) * 128, :]))

                @blk.vector
                def _(vec):
                    for t in range(ntile):
                        LB.wait(vec, t)
                        if vrow_add:
                            vec.tensor_tensor(xb[:, t % 2, :], xb[:, t % 2, :],
                                              vrow[:, :], op=AO.add)
                            vec.drain()
                        inc(vec.scalar_tensor_tensor(
                            t1[:, 32 + (t % 2):33 + (t % 2)], mr(m_which, t),
                            -1.0, mr(r_which, t), op0=AO.mult, op1=AO.mult),
                            "VB", 1)
                        vec.wait_ge(sh("AA"), bAA + t + 1)
                        vec.tensor_reduce(t1[:, 34:35], ab[:, t % 2, :],
                                          axis=AX.X, op=AO.max)
                        vec.drain()
                        inc(vec.tensor_tensor(gdst[:, :], gdst[:, :],
                                              t1[:, 34:35], op=AO.max),
                            "VA", 1)
                        vec.drain()

                @blk.scalar
                def _(s):
                    for t in range(ntile):
                        s.wait_ge(sh("VB"), bVB + t + 1)
                        inc(s.activation(ab[:, t % 2, :], xb[:, t % 2, :],
                                         AF.Abs,
                                         bias=t1[:, 32 + (t % 2):33 + (t % 2)],
                                         scale=mr(r_which, t)), "AA", 1)

        def armax(gsrc, cci, cco, clip_slot, csx_dst, s_dst):
            with nc.Block() as blk:
                bTA, bAA, bVA, bCC = sv("TA"), sv("AA"), sv("VA"), sv("CC")
                SG = Stm("SG")

                @blk.tensor
                def _(t):
                    inc(t.matmul(ps[6][0:1, 0:128], gsrc[:, :], idf[:, :],
                                 start=True, stop=True), "TA", 1)
                    t.wait_ge(sh("VA"), bVA + 2)
                    inc(t.matmul(ps[7][0:128, 0:2], onesf[:, :],
                                 s1r[0:1, 16:18], start=True, stop=True),
                        "TA", 1)

                @blk.vector
                def _(vec):
                    vec.wait_ge(sh("TA"), bTA + 1)
                    inc(vec.tensor_reduce(s1r[0:1, 9:10], ps[6][0:1, 0:128],
                                          axis=AX.X, op=AO.max), "VA", 1)
                    vec.wait_ge(sh("CC"), bCC + 1)
                    SG.wait(vec, 1)
                    vec.wait_ge(sh("AA"), bAA + 1)
                    vec.tensor_tensor(s1r[0:1, 12:13], s1r[0:1, 11:12],
                                      s1r[0:1, 14:15], op=AO.mult)
                    vec.drain()
                    vec.reciprocal(s1r[0:1, 13:14], s1r[0:1, 12:13])
                    vec.drain()
                    vec.tensor_scalar(s1r[0:1, 16:17], s1r[0:1, 13:14], QMAX,
                                      None, op0=AO.mult)
                    inc(vec.tensor_scalar(s1r[0:1, 17:18], s1r[0:1, 12:13],
                                          1.0 / QMAX, None, op0=AO.mult),
                        "VA", 1)
                    vec.drain()
                    vec.wait_ge(sh("TA"), bTA + 2)
                    vec.tensor_copy(csx_dst, ps[7][0:128, 0:1])
                    inc(vec.tensor_copy(s_dst, ps[7][0:128, 1:2]), "VA", 1)
                    vec.drain()

                @blk.scalar
                def _(s):
                    inc(s.activation(s1r[0:1, 11:12], clip_slot, AF.Sigmoid),
                        "AA", 1)

                @blk.gpsimd
                def _(g):
                    g.wait_ge(sh("VA"), bVA + 1)
                    SG.issue(g.dma_start(cci.ap()[:, :], s1r[0:1, 9:10]))
                    SG.wait(g, 0)
                    inc(g.collective_compute(
                        "AllReduce", AO.max, replica_groups=[list(range(8))],
                        ins=[cci.ap().opt()], outs=[cco.ap().opt()]), "CC", 1)
                    g.wait_ge(sh("CC"), bCC + 1)
                    SG.issue(g.dma_start(s1r[0:1, 14:15], cco.ap()[:, :]))

        def quant_apply(src_h, ntile, m_which, r_which, csx_col, vrow_add,
                        do_round, qxt_tok=None, ph0d=False):
            with nc.Block() as blk:
                bVA, bVB, bAA = sv("VA"), sv("VB"), sv("AA")
                LA, SQ = Stm("LA"), Stm("SQ")
                if qxt_tok is not None:
                    TRq = Stm("TR", per=DC)
                    sq0_done = sv("SQ0") + 16 * ((ntile + 1) // 2)
                    sq1_done = sv("SQ1") + 16 * (ntile // 2)
                if ph0d:
                    LWp, SOp = Stm("LW"), Stm("SO")
                    strips = ([('g', hc) for hc in range(DC)] +
                              [('u', hc) for hc in range(DC)] +
                              [('d', ic) for ic in range(ICN)])
                eg = entry_guard()

                @blk.sync
                def _(sync):
                    wait_all_stores(sync)
                    for t in range(ntile):
                        if t >= 2:
                            sync.wait_ge(sh("AA"), bAA + (t - 1))
                        LA.issue(sync.dma_start(
                            xa[:, t % 2, :], src_h[t * 128:(t + 1) * 128, :]))
                    if qxt_tok is not None:
                        sync.wait_ge(sh("SQ0"), sq0_done)
                        sync.wait_ge(sh("SQ1"), sq1_done)
                        for hc in range(DC):
                            TRq.issue(sync.dma_start(
                                qx_fm[:, hc, 0:qxt_tok],
                                qa_h[0:qxt_tok, hc * 128:(hc + 1) * 128],
                                transpose=True))
                        TRq.wait(sync, 0)

                dflat = diagc[:, :, :, :].rearrange("p b j q -> p b (j q)")

                def strip_info(k):
                    kind, j = strips[k]
                    if kind == 'g':
                        return (qwg_h[:, j * 128:(j + 1) * 128],
                                wgT_h[j * 128:(j + 1) * 128, :], IFF)
                    if kind == 'u':
                        return (qwu_h[:, j * 128:(j + 1) * 128],
                                wuT_h[j * 128:(j + 1) * 128, :], IFF)
                    return (qwd_h[:, j * 128:(j + 1) * 128],
                            wdT_h[j * 128:(j + 1) * 128, :], 1024)

                @blk.gpsimd
                def _(g):
                    if not ph0d:
                        return
                    for k in range(len(strips)):
                        _, pdst, pw = strip_info(k)
                        LWp.wait(g, k)
                        SOp.issue(g.dma_start(pdst, dflat[:, k % 2, 0:pw]))

                @blk.vector
                def _(vec):
                    eg(vec)
                    for t in range(ntile):
                        LA.wait(vec, t)
                        if vrow_add:
                            vec.tensor_tensor(xa[:, t % 2, :], xa[:, t % 2, :],
                                              vrow[:, :], op=AO.add)
                            vec.drain()
                        if do_round:
                            vec.tensor_tensor(t2[:, 32 + (t % 2):33 + (t % 2)],
                                              mr(r_which, t), csx_col,
                                              op=AO.mult)
                        else:
                            vec.tensor_copy(t2[:, 32 + (t % 2):33 + (t % 2)],
                                            mr(r_which, t))
                        vec.drain()
                        inc(vec.scalar_tensor_tensor(
                            t3[:, 32 + (t % 2):33 + (t % 2)], mr(m_which, t),
                            -1.0, t2[:, 32 + (t % 2):33 + (t % 2)],
                            op0=AO.mult, op1=AO.mult), "VB", 1)
                        if do_round:
                            vec.wait_ge(sh("AA"), bAA + (t + 1))
                            if t >= 2:
                                SQ.wait(vec, t - 2)
                            vec.tensor_scalar(qi[:, :], qt[:, t % 2, :],
                                              CLIP_LO, CLIP_HI, op0=AO.max,
                                              op1=AO.min)
                            vec.drain()
                            inc(vec.tensor_copy(qb[:, t % 2, :], qi[:, :]),
                                "VA", 1)
                            vec.drain()

                @blk.scalar
                def _(s):
                    eg(s)
                    kph = [0]

                    def ph0d_loads(upto):
                        while kph[0] < min(upto, len(strips)):
                            k = kph[0]
                            src, _, w = strip_info(k)
                            if k >= 2:
                                SOp.wait(s, k - 2)
                            LWp.issue(s.dma_start(dflat[:, k % 2, 0:w], src,
                                                  transpose=True))
                            kph[0] += 1

                    if ph0d:
                        for s_ in ("SQ0", "SQ1"):
                            s.wait_ge(sh(s_), sv(s_))
                    for t in range(ntile):
                        if ph0d:
                            ph0d_loads((t + 1) * len(strips) // ntile)
                        s.wait_ge(sh("VB"), bVB + t + 1)
                        if do_round:
                            if t >= 2:
                                s.wait_ge(sh("VA"), bVA + (t - 1))
                            inc(s.activation(
                                qt[:, t % 2, :], xa[:, t % 2, :], AF.Identity,
                                bias=t3[:, 32 + (t % 2):33 + (t % 2)],
                                scale=t2[:, 32 + (t % 2):33 + (t % 2)]),
                                "AA", 1)
                            s.wait_ge(sh("VA"), bVA + (t + 1))
                        else:
                            if t >= 2:
                                SQ.wait(s, t - 2)
                            aa_val = inc(s.activation(
                                qb[:, t % 2, :], xa[:, t % 2, :], AF.Identity,
                                bias=t3[:, 32 + (t % 2):33 + (t % 2)],
                                scale=t2[:, 32 + (t % 2):33 + (t % 2)]),
                                "AA", 1)
                            s.wait_ge(sh("AA"), aa_val)
                        SQ.issue(s.dma_start(qa_h[t * 128:(t + 1) * 128, :],
                                             qb[:, t % 2, :]))

        def qx_transpose(ttok):
            with nc.Block() as blk:
                TRq = Stm("TR", per=DC)

                @blk.sync
                def _(sync):
                    wait_all_stores(sync)
                    for hc in range(DC):
                        TRq.issue(sync.dma_start(
                            qx_fm[:, hc, 0:ttok],
                            qa_h[0:ttok, hc * 128:(hc + 1) * 128],
                            transpose=True))
                    TRq.wait(sync, 0)

        def ffn_gateup(ttot):
            NTl = _tiles(ttot)
            nnt = len(NTl)
            with nc.Block() as blk:
                bTA = sv("TA")
                bVA, bVB, bPFA = sv("VA"), sv("VB"), sv("PFA")
                LW = Stm("LW", per=2)
                HS = Stm("SG")
                eg = entry_guard()

                @blk.sync
                def _(sync):
                    for ic in range(ICN):
                        if ic >= 2:
                            sync.wait_ge(sh("TA"), bTA + (ic - 1) * nnt)
                        LW.issue(sync.dma_start(
                            wgc[:, ic % 2, :, :],
                            bass.AP(wgT_h.tensor, ic * 128,
                                    [[IFF, 128], [128 * IFF, DC], [1, 128]])))
                        LW.issue(sync.dma_start(
                            wuc[:, ic % 2, :, :],
                            bass.AP(wuT_h.tensor, ic * 128,
                                    [[IFF, 128], [128 * IFF, DC], [1, 128]])))

                @blk.vector
                def _(vec):
                    eg(vec)
                    for i in range(ICN):
                        vec.tensor_tensor(cgc(i), sgc(i), sc_c(SHS),
                                          op=AO.mult)
                        vec.tensor_tensor(cuc(i), suc(i), sc_c(SHS),
                                          op=AO.mult)
                    vec.drain()
                    inc(vec.tensor_copy(t1[:, 0:1], cgc(0)), "VB", 1)
                    k = 0
                    for ic in range(ICN):
                        for (nt0, nts) in NTl:
                            vec.wait_ge(sh("PFA"), bPFA + 2 * (k + 1))
                            if k >= 2:
                                HS.wait(vec, k - 2)
                            vec.tensor_tensor(ht[:, k % 2, :nts],
                                              st_t[:, 2 * (k % 2), :nts],
                                              st_t[:, 2 * (k % 2) + 1, :nts],
                                              op=AO.mult)
                            vec.drain()
                            vec.tensor_reduce(t1[:, 34:35], ht[:, k % 2, :nts],
                                              axis=AX.X, op=AO.max,
                                              apply_absolute_value=True)
                            vec.drain()
                            inc(vec.tensor_tensor(gh[:, :], gh[:, :],
                                                  t1[:, 34:35], op=AO.max),
                                "VA", 1)
                            vec.drain()
                            k += 1

                @blk.tensor
                def _(t):
                    k = 0
                    for ic in range(ICN):
                        LW.wait(t, ic)
                        for (nt0, nts) in NTl:
                            pg = ps[2 * (k % 2)]
                            pu = ps[2 * (k % 2) + 1]
                            if k >= 2:
                                t.wait_ge(sh("PFA"), bPFA + 2 * (k - 1))
                            for hc in range(DC):
                                t.matmul(pg[:, :nts], wgc[:, ic % 2, hc, :],
                                         qx_fm[:, hc, nt0:nt0 + nts],
                                         start=(hc == 0), stop=(hc == DC - 1))
                            inst = None
                            for hc in range(DC):
                                inst = t.matmul(pu[:, :nts],
                                                wuc[:, ic % 2, hc, :],
                                                qx_fm[:, hc, nt0:nt0 + nts],
                                                start=(hc == 0),
                                                stop=(hc == DC - 1))
                            inc(inst, "TA", 1)
                            k += 1

                @blk.scalar
                def _(s):
                    s.wait_ge(sh("VB"), bVB + 1)
                    k = 0
                    for ic in range(ICN):
                        for (nt0, nts) in NTl:
                            pg = ps[2 * (k % 2)]
                            pu = ps[2 * (k % 2) + 1]
                            s.wait_ge(sh("TA"), bTA + k + 1)
                            if k >= 2:
                                s.wait_ge(sh("VA"), bVA + (k - 1))
                            inc(s.activation(st_t[:, 2 * (k % 2), :nts],
                                             pg[:, :nts], AF.Silu,
                                             scale=cgc(ic)), "PFA", 1)
                            inc(s.activation(st_t[:, 2 * (k % 2) + 1, :nts],
                                             pu[:, :nts], AF.Copy,
                                             scale=cuc(ic)), "PFA", 1)
                            k += 1

                @blk.gpsimd
                def _(g):
                    k = 0
                    for ic in range(ICN):
                        for (nt0, nts) in NTl:
                            g.wait_ge(sh("VA"), bVA + k + 1)
                            HS.issue(g.dma_start(
                                bass.AP(h_h.tensor, ic * 128 * t_tot + nt0,
                                        [[t_tot, 128], [1, nts]]),
                                ht[:, k % 2, :nts]))
                            k += 1

        def srow_build():
            with nc.Block() as blk:
                bTA, bVA = sv("TA"), sv("VA")
                SQ = Stm("SQ")
                eg = entry_guard()

                @blk.vector
                def _(vec):
                    eg(vec)
                    for dc in range(DC):
                        vec.tensor_scalar(t1[:, 40 + dc:41 + dc], sdc(dc),
                                          sc_c(SHS), 0.5, op0=AO.mult,
                                          op1=AO.mult)
                    vec.drain()
                    inc(vec.tensor_copy(t1[:, 48:49], t1[:, 40:41]), "VA", 1)
                    vec.wait_ge(sh("TA"), bTA + 1)
                    inc(vec.tensor_copy(ar32[0:DC, :], ps[6][0:DC, 0:128]),
                        "VA", 1)
                    vec.drain()
                    for half in range(2):
                        vec.wait_ge(sh("TA"), bTA + 2 + half)
                        vec.tensor_copy(srow[:, half * 512:(half + 1) * 512],
                                        ps[7 - half][:, 0:512])
                        vec.drain()
                        inc(vec.tensor_copy(t1[:, 49:50], t1[:, 40:41]),
                            "VA", 1)

                @blk.tensor
                def _(t):
                    t.wait_ge(sh("VA"), bVA + 1)
                    inc(t.matmul(ps[6][0:DC, 0:128], t1[:, 40:40 + DC],
                                 idf[:, :], start=True, stop=True), "TA", 1)
                    SQ.wait(t, 1)
                    for half in range(2):
                        if half == 1:
                            t.wait_ge(sh("VA"), bVA + 3)
                        inc(t.matmul(ps[7 - half][:, 0:512], onesf[:, :],
                                     rowD[0:1, half * 512:(half + 1) * 512],
                                     start=True, stop=True), "TA", 1)

                @blk.scalar
                def _(s):
                    s.wait_ge(sh("VA"), bVA + 2)
                    SQ.issue(s.dma_start(
                        misc_h[0:D].rearrange("(c p) -> c p", p=128),
                        ar32[0:DC, :]))
                    SQ.wait(s, 0)
                    SQ.issue(s.dma_start(rowD[0:1, :], misc_h[None, 0:D]))

        def ffn_down(ttot, res_h, out_h, next_stats, final, tok_shift):
            NTl = _tiles(ttot)
            nnt = len(NTl)
            KS = []
            for ni, (nt0, nts) in enumerate(NTl):
                for ms in range((nts + 127) // 128):
                    KS.append((ni, nt0, nts, ms))
            ks_before = [0] * (nnt + 1)
            for ni in range(nnt):
                ks_before[ni + 1] = ks_before[ni] + (NTl[ni][1] + 127) // 128
            with nc.Block() as blk:
                bTA, bAA, bVA = sv("TA"), sv("AA"), sv("VA")
                bVB, bPFA, bAB = sv("VB"), sv("PFA"), sv("AB")
                bDGd = sv("DG")
                LW = Stm("LW")
                LH = Stm("LH")
                XB = Stm("LB")
                ST = Stm("SO" if final else "SG")
                eg = entry_guard()

                ICN2 = ICN // 2

                @blk.sync
                def _(sync):
                    wait_all_stores(sync)
                    LW.issue(sync.dma_start(
                        wdT_v[:, :, :],
                        bass.AP(wdT_h.tensor, 0,
                                [[1024, 128], [128 * 1024, ICN], [1, 1024]])))
                    def xb_issue(k):
                        kni, knt0, knts, kms = KS[k]
                        if k >= 2:
                            sync.wait_ge(sh("DG"), bDGd + (k - 1))
                        XB.issue(sync.dma_start(
                            xb[:, k % 2, :],
                            res_h[knt0 + kms * 128:knt0 + kms * 128 + 128,
                                  :]))

                    kk = 0
                    for ni, (nt0, nts) in enumerate(NTl):
                        for icp in range(ICN2):
                            if kk >= 2:
                                sync.wait_ge(sh("AA"), bAA + (kk - 1))
                            LH.issue(sync.dma_start(
                                hl[:, kk % 2, :, :nts],
                                bass.AP(h_h.tensor,
                                        (2 * icp) * 128 * t_tot + nt0,
                                        [[t_tot, 128], [128 * t_tot, 2],
                                         [1, nts]])))
                            kk += 1
                        if ni >= 1:
                            for k in range(ks_before[ni - 1], ks_before[ni]):
                                xb_issue(k)
                    for k in range(ks_before[nnt - 1], ks_before[nnt]):
                        xb_issue(k)

                @blk.gpsimd
                def _(g):
                    for s in ("SG0", "SG1"):
                        g.wait_ge(sh(s), sv(s))
                    for k, (ni, nt0, nts, ms) in enumerate(KS):
                        g.wait_ge(sh("DG"), bDGd + (k + 1))
                        row = nt0 + ms * 128 - tok_shift
                        ST.issue(g.dma_start(out_h[row:row + 128, :],
                                             ob[:, k % 2, :]))

                @blk.scalar
                def _(s):
                    def emit_square(k):
                        ni, nt0, nts, ms = KS[k]
                        s.wait_ge(sh("DG"), bDGd + (k + 1))
                        tt = (nt0 + ms * 128) // 128
                        inc(s.activation(ab[:, k % 2, :], ob[:, k % 2, :],
                                         AF.Square, accum_out=s_ssq(tt)),
                            "AB", 1)

                    kk = 0
                    for ni, (nt0, nts) in enumerate(NTl):
                        for icp in range(ICN2):
                            LH.wait(s, kk)
                            if kk >= 2:
                                s.wait_ge(sh("VB"), bVB + (kk - 1))
                            inc(s.activation(hq_ti[:, kk % 2, :, :nts],
                                             hl[:, kk % 2, :, :nts], AF.Copy,
                                             scale=sc_c(CSH)), "AA", 1)
                            kk += 1
                        if next_stats and ni >= 1:
                            for k in range(ks_before[ni - 1], ks_before[ni]):
                                emit_square(k)
                    if next_stats:
                        for k in range(ks_before[nnt - 1], ks_before[nnt]):
                            emit_square(k)

                @blk.vector
                def _(vec):
                    eg(vec)

                    def emit_evac(k):
                        ni, nt0, nts, ms = KS[k]
                        mts = min(128, nts - ms * 128)
                        pd0 = ps[2 * (k % 4)]
                        pd1 = ps[2 * (k % 4) + 1]
                        vec.wait_ge(sh("TA"), bTA + k + 1)
                        XB.wait(vec, k)
                        if k >= 2:
                            ST.wait(vec, k - 2)
                            if next_stats:
                                vec.wait_ge(sh("AB"), bAB + (k - 1))
                        for half, pd in ((0, pd0), (1, pd1)):
                            vec.scalar_tensor_tensor(
                                ob[:mts, k % 2, half * 512:(half + 1) * 512],
                                pd[:mts, :],
                                scal[:mts, SHS:SHS + 1],
                                xb[:mts, k % 2, half * 512:(half + 1) * 512],
                                op0=AO.mult, op1=AO.add)
                        vec.drain()
                        inc(vec.tensor_copy(t1[:, 50:51], t1[:, 50:51]),
                            "PFA", 2)
                        if final:
                            vec.tensor_tensor(ob[:mts, k % 2, :],
                                              ob[:mts, k % 2, :],
                                              vrow[:mts, :], op=AO.add)
                            vec.drain()
                        inc(vec.tensor_copy(t1[:, 51:52], t1[:, 51:52]),
                            "DG", 1)
                        if next_stats:
                            tt = (nt0 + ms * 128) // 128
                            vec.tensor_reduce(s_sum(tt), ob[:, k % 2, :],
                                              axis=AX.X, op=AO.add)
                            vec.drain()

                    kk = 0
                    for ni, (nt0, nts) in enumerate(NTl):
                        for icp in range(ICN2):
                            vec.wait_ge(sh("AA"), bAA + kk + 1)
                            if ni >= 2 and icp == 0:
                                vec.wait_ge(sh("TA"), bTA + ks_before[ni - 1])
                            inc(vec.tensor_scalar(
                                qh_v[:, ni % 2, 2 * icp:2 * icp + 2, 0:nts],
                                hq_ti[:, kk % 2, :, :nts],
                                -(QMAX + 1.0), QMAX, op0=AO.max, op1=AO.min),
                                "VB", 1)
                            vec.drain()
                            kk += 1
                        if ni >= 1:
                            for k in range(ks_before[ni - 1], ks_before[ni]):
                                emit_evac(k)
                    for k in range(ks_before[nnt - 1], ks_before[nnt]):
                        emit_evac(k)

                @blk.tensor
                def _(t):
                    LW.wait(t, 0)
                    for k, (ni, nt0, nts, ms) in enumerate(KS):
                        mts = min(128, nts - ms * 128)
                        pd0 = ps[2 * (k % 4)]
                        pd1 = ps[2 * (k % 4) + 1]
                        if k >= 4:
                            t.wait_ge(sh("PFA"), bPFA + 2 * (k - 3))
                        t.wait_ge(sh("VB"), bVB + (ni + 1) * ICN2)
                        inst = None
                        for ic in range(ICN):
                            t.matmul(pd0[:mts, :],
                                     qh_v[:, ni % 2, ic,
                                          ms * 128:ms * 128 + mts],
                                     wdT_v[:, ic, 0:512],
                                     start=(ic == 0), stop=(ic == ICN - 1))
                            inst = t.matmul(
                                pd1[:mts, :],
                                qh_v[:, ni % 2, ic,
                                     ms * 128:ms * 128 + mts],
                                wdT_v[:, ic, 512:1024],
                                start=(ic == 0), stop=(ic == ICN - 1))
                        inc(inst, "TA", 1)

        def attn_main():
            nnt = len(NT)
            OMS = []
            for ni, (nt0, nts) in enumerate(NT):
                OMS.append([ms for ms in range((nts + 127) // 128)
                            if nt0 + ms * 128 >= HALO])
            oms_before = [0] * (nnt + 1)
            for ni in range(nnt):
                oms_before[ni + 1] = oms_before[ni] + len(OMS[ni])

            def ko_of(ko):
                ni = next(i for i in range(nnt) if oms_before[i + 1] > ko)
                return ni, OMS[ni][ko - oms_before[ni]], NT[ni][0]

            with nc.Block() as blk:
                bTA, bTB, bTC = sv("TA"), sv("TB"), sv("TC")
                bAA, bVB, bVC = sv("AA"), sv("VB"), sv("VC")
                bPFA, bDG, bVA = sv("PFA"), sv("DG"), sv("VA")
                LW = Stm("LW")
                TR = Stm("TR", per=DC)
                XB = Stm("LB")
                XO = Stm("SG")
                LD = Stm("LH")
                CCQ = Stm("CQ")
                bCC = sv("CC")
                eg = entry_guard()

                @blk.sync
                def _(sync):
                    wait_all_stores(sync)
                    for kc in range(DC):
                        if kc >= 2:
                            sync.wait_ge(sh("VB"), bVB + (kc - 1))
                        LW.issue(sync.dma_start(
                            qt[:, kc % 2, :],
                            w1_in[kc * 128:(kc + 1) * 128, :]))
                    for kc in range(DC):
                        sync.wait_ge(sh("VB"), bVB + DC + kc - 1)
                        LW.issue(sync.dma_start(
                            qt[:, kc % 2, :],
                            w2_in[kc * 128:(kc + 1) * 128, :]))
                    ko = 0
                    for ni, (nt0, nts) in enumerate(NT):
                        if ni >= 2:
                            sync.wait_ge(sh("TA"), bTA + (ni - 1) * DC)
                            sync.wait_ge(sh("VC"), bVC + (ni - 1))
                        for hc in range(DC):
                            TR.issue(sync.dma_start(
                                x2f_v[:, ni % 2, hc, 0:nts],
                                qa_h[nt0:nt0 + nts, hc * 128:(hc + 1) * 128],
                                transpose=True))
                        for c in range(DC):
                            gstep = ni * DC + c
                            if gstep >= 2:
                                sync.wait_ge(sh("TB"), bTB + gstep - 1)
                            LD.issue(sync.dma_start(
                                diagc[:, gstep % 2, :, :],
                                bass.AP(diag_h.tensor,
                                        c * 128 * NTAP * 128,
                                        [[NTAP * 128, 128],
                                         [1, NTAP * 128]])))
                        for ms in OMS[ni]:
                            if ko >= 2:
                                skew = 2 if ko - 2 >= oms_before[nnt - 1] \
                                    else 0
                                sync.wait_ge(sh("VA"),
                                             bVA + (ko - 1) + skew)
                            XB.issue(sync.dma_start(
                                xb[:, ko % 2, :],
                                xp_h[nt0 + ms * 128:nt0 + ms * 128 + 128, :]))
                            ko += 1

                @blk.vector
                def _(vec):
                    eg(vec)
                    for kc in range(DC):
                        LW.wait(vec, kc)
                        vec.tensor_copy(w1_v[:, kc, :], qt[:, kc % 2, :])
                        vec.drain()
                        inc(vec.tensor_copy(t1[:, 0:1], t1[:, 0:1]), "VB", 1)
                    for kc in range(DC):
                        LW.wait(vec, DC + kc)
                        vec.tensor_copy(w2_v[:, kc, :], qt[:, kc % 2, :])
                        vec.drain()
                        inc(vec.tensor_copy(t1[:, 0:1], t1[:, 0:1]), "VB", 1)
                    ko = 0
                    for ni, (nt0, nts) in enumerate(NT):
                        lo = HALO if ni == 0 else 0
                        TR.wait(vec, ni)
                        vec.tensor_reduce(t2[:, 40:40 + DC],
                                          x2f_v[:, ni % 2, :, lo:nts],
                                          axis=AX.X, op=AO.add)
                        vec.drain()
                        inc(vec.tensor_tensor(tot8[:, :], tot8[:, :],
                                              t2[:, 40:40 + DC], op=AO.add),
                            "VC", 1)
                        vec.drain()
                        if ni == nnt - 1:
                            for b in range(4):
                                vec.tensor_scalar(
                                    ctr32[:, b * DC:(b + 1) * DC],
                                    tot8[:, :], bsel_b[:, b:b + 1],
                                    None, op0=AO.mult)
                            vec.drain()
                            inc(vec.tensor_copy(t1[:, 0:1], t1[:, 0:1]),
                                "VA", 1)
                            vec.wait_ge(sh("TA"), bTA + nnt * DC + 1)
                            inc(vec.tensor_copy(ar32[:, :],
                                                ps[6][0:32, 0:128]),
                                "VA", 1)
                            vec.drain()
                        if ni == 0:
                            vec.wait_ge(sh("AA"), bAA + DC)
                            for c in range(DC):
                                vec.tensor_scalar(pr_v[:, c, 128:256],
                                                  pr_v[:, c, 128:256],
                                                  sc_c(HMASK), None,
                                                  op0=AO.mult)
                            vec.drain()
                        for c in range(DC):
                            gstep = ni * DC + c
                            inc(vec.tensor_copy(t1[:, 0:1], t1[:, 0:1]),
                                "DG", 1)
                            if ni >= 1 and c == 0:
                                vec.wait_ge(sh("TC"), bTC + oms_before[ni])
                            vec.wait_ge(sh("TB"), bTB + gstep + 1)
                            vec.tensor_copy(z_v[:, c, 0:nts],
                                            ps[4 + (gstep % 2)][:, :nts])
                            vec.drain()
                            if ni < nnt - 1:
                                vec.tensor_copy(pr_v[:, c, 0:128],
                                                pr_v[:, c, nts:nts + 128])
                                vec.drain()
                            inc(vec.tensor_copy(t1[:, 1:2], t1[:, 1:2]),
                                "DG", 1)
                        for oi, ms in enumerate(OMS[ni]):
                            mts = min(128, nts - ms * 128)
                            po0 = ps[2 * (ko % 2)]
                            po1 = ps[2 * (ko % 2) + 1]
                            vec.wait_ge(sh("TC"), bTC + ko + 1)
                            XB.wait(vec, ko)
                            if ko >= 2:
                                XO.wait(vec, ko - 2)
                            for half, po in ((0, po0), (1, po1)):
                                vec.tensor_tensor(
                                    ob[:mts, ko % 2,
                                       half * 512:(half + 1) * 512],
                                    po[:mts, :],
                                    xb[:mts, ko % 2,
                                       half * 512:(half + 1) * 512],
                                    op=AO.add)
                            vec.drain()
                            inc(vec.tensor_copy(t1[:, 0:1], t1[:, 0:1]),
                                "PFA", 2)
                            inc(vec.tensor_copy(t1[:, 1:2], t1[:, 1:2]),
                                "VA", 1)
                            ko += 1

                @blk.scalar
                def _(s):
                    for ni, (nt0, nts) in enumerate(NT):
                        for mc in range(DC):
                            s.wait_ge(sh("TA"), bTA + ni * DC + mc + 1)
                            if ni >= 1:
                                s.wait_ge(sh("DG"), bDG + 2 * ((ni - 1) * DC
                                                               + mc + 1))
                            inc(s.activation(pr_v[:, mc, 128:128 + nts],
                                             ps[6 + (mc % 2)][:, :nts],
                                             AF.Identity,
                                             bias=b1f[:, mc:mc + 1],
                                             scale=1.0), "AA", 1)

                @blk.tensor
                def _(t):
                    for ni, (nt0, nts) in enumerate(NT):
                        t.wait_ge(sh("VB"), bVB + DC)
                        TR.wait(t, ni)
                        for mc in range(DC):
                            if ni * DC + mc >= 2:
                                t.wait_ge(sh("AA"), bAA + ni * DC + mc - 1)
                            inst = None
                            for kc in range(DC):
                                inst = t.matmul(
                                    ps[6 + (mc % 2)][:, :nts],
                                    w1_v[:, kc, mc * 128:(mc + 1) * 128],
                                    x2f_v[:, ni % 2, kc, 0:nts],
                                    start=(kc == 0), stop=(kc == DC - 1))
                            inc(inst, "TA", 1)
                        if ni == nnt - 1:
                            t.wait_ge(sh("VA"),
                                      bVA + oms_before[nnt - 1] + 1)
                            t.wait_ge(sh("AA"), bAA + nnt * DC)
                            inc(t.matmul(ps[6][0:32, 0:128], ctr32[:, :],
                                         idf[:, :], start=True, stop=True),
                                "TA", 1)
                        for c in range(DC):
                            gstep = ni * DC + c
                            LD.wait(t, gstep)
                            t.wait_ge(sh("DG"), bDG + 2 * gstep + 1)
                            t.wait_ge(sh("AA"), bAA + ni * DC + c + 1)
                            if ni == 0 and c == 0:
                                t.wait_ge(sh("GI"), 1)
                            if gstep >= 2:
                                t.wait_ge(sh("DG"), bDG + 2 * (gstep - 1))
                            pb = ps[4 + (gstep % 2)]
                            inst = None
                            for j in range(NTAP):
                                inst = t.matmul(
                                    pb[:, :nts], diagc[:, gstep % 2, j, :],
                                    pr_v[:, c, 128 - j:128 - j + nts],
                                    start=(j == 0), stop=(j == NTAP - 1))
                            inc(inst, "TB", 1)
                        t.wait_ge(sh("VB"), bVB + 2 * DC)
                        t.wait_ge(sh("DG"), bDG + 2 * (ni * DC + DC))
                        for oi, ms in enumerate(OMS[ni]):
                            mts = min(128, nts - ms * 128)
                            ko = oms_before[ni] + oi
                            po0 = ps[2 * (ko % 2)]
                            po1 = ps[2 * (ko % 2) + 1]
                            if ko >= 2:
                                t.wait_ge(sh("PFA"), bPFA + 2 * (ko - 1))
                            inst = None
                            for kc in range(DC):
                                t.matmul(po0[:mts, :],
                                         z_v[:, kc, ms * 128:ms * 128 + mts],
                                         w2_v[:, kc, 0:512],
                                         start=(kc == 0), stop=(kc == DC - 1))
                                inst = t.matmul(
                                    po1[:mts, :],
                                    z_v[:, kc, ms * 128:ms * 128 + mts],
                                    w2_v[:, kc, 512:1024],
                                    start=(kc == 0), stop=(kc == DC - 1))
                            inc(inst, "TC", 1)

                @blk.gpsimd
                def _(g):
                    kom1 = oms_before[nnt - 1]
                    for ko in range(oms_before[nnt]):
                        if ko == kom1:
                            g.wait_ge(sh("VA"), bVA + kom1 + 2)
                            CCQ.issue(g.dma_start(cc_t["i3"].ap()[:, :],
                                                  ar32[:, :]))
                            CCQ.wait(g, 0)
                            inc(g.collective_compute(
                                "AllReduce", AO.add,
                                replica_groups=[list(range(8))],
                                ins=[cc_t["i3"].ap().opt()],
                                outs=[cc_t["o3"].ap().opt()]), "CC", 1)
                            g.wait_ge(sh("CC"), bCC + 1)
                            CCQ.issue(g.dma_start(ar32[:, :],
                                                  cc_t["o3"].ap()[:, :]))
                        ni, ms, nt0 = ko_of(ko)
                        skew = 2 if ko >= kom1 else 0
                        g.wait_ge(sh("VA"), bVA + (ko + 1) + skew)
                        row = nt0 + ms * 128 - HALO
                        XO.issue(g.dma_start(xpp_h[row:row + 128, :],
                                             ob[:, ko % 2, :]))

        def ar3_vrow():
            with nc.Block() as blk:
                bTA, bVA = sv("TA"), sv("VA")
                eg = entry_guard()

                @blk.vector
                def _(vec):
                    eg(vec)
                    vec.wait_ge(sh("TA"), bTA + 1)
                    inc(vec.tensor_copy(allT[:, :], ps[7][:, 0:32]), "VA", 1)
                    vec.drain()
                    vec.tensor_scalar(t2[:, 0:DC], allT[:, 0:DC],
                                      bsel_b[:, 0:1], None, op0=AO.mult)
                    vec.drain()
                    for b in range(1, 4):
                        vec.scalar_tensor_tensor(
                            t2[:, 0:DC], allT[:, b * DC:(b + 1) * DC],
                            bsel_b[:, b:b + 1], t2[:, 0:DC],
                            op0=AO.mult, op1=AO.add)
                        vec.drain()
                    vec.tensor_copy(tot8[:, :], t2[:, 0:DC])
                    vec.drain()
                    inc(vec.tensor_copy(t8b[:, :], tot8[:, :]), "VA", 1)
                    vec.drain()
                    for mc in range(DC):
                        vec.wait_ge(sh("TA"), bTA + 2 + mc)
                        vec.tensor_tensor(totf[:, mc:mc + 1],
                                          ps[mc % 2][:, 0:1],
                                          b1T[:, mc:mc + 1], op=AO.add)
                        vec.drain()
                        inc(vec.tensor_copy(t1[:, 0:1], t1[:, 0:1]), "VA", 1)
                    vec.tensor_tensor(u8[:, :], totf[:, :], mwf[:, :, 0],
                                      op=AO.mult)
                    vec.drain()
                    inc(vec.tensor_copy(u8b[:, :], u8[:, :]), "VA", 1)
                    vec.drain()
                    vec.wait_ge(sh("TA"), bTA + DC + 3)
                    vec.tensor_copy(rowD[0:1, 0:512], ps[6][0:1, :])
                    vec.tensor_copy(rowD[0:1, 512:1024], ps[7][0:1, :])
                    vec.drain()
                    vec.tensor_tensor(rowD[0:1, :], rowD[0:1, :],
                                      rowD2[0:1, :], op=AO.add)
                    vec.drain()
                    inc(vec.tensor_copy(t1[:, 0:1], t1[:, 0:1]), "VA", 1)
                    for half in range(2):
                        vec.wait_ge(sh("TA"), bTA + DC + 4 + half)
                        vec.tensor_copy(vrow[:, half * 512:(half + 1) * 512],
                                        ps[4 + half][:, 0:512])
                        vec.drain()

                @blk.tensor
                def _(t):
                    for s_ in ("CQ0", "CQ1"):
                        t.wait_ge(sh(s_), sv(s_))
                    inc(t.matmul(ps[7][:, 0:32], ar32[:, :], idf[0:32, 0:32],
                                 start=True, stop=True), "TA", 1)
                    t.wait_ge(sh("VA"), bVA + 2)
                    for mc in range(DC):
                        if mc >= 2:
                            t.wait_ge(sh("VA"), bVA + 2 + (mc - 1))
                        inst = None
                        for kc in range(DC):
                            inst = t.matmul(ps[mc % 2][:, 0:1],
                                            w1_v[:, kc,
                                                 mc * 128:(mc + 1) * 128],
                                            t8b[:, kc:kc + 1],
                                            start=(kc == 0),
                                            stop=(kc == DC - 1))
                        inc(inst, "TA", 1)
                    t.wait_ge(sh("VA"), bVA + DC + 3)
                    for half, bank in ((0, 6), (1, 7)):
                        inst = None
                        for kc in range(DC):
                            inst = t.matmul(ps[bank][0:1, :],
                                            u8b[:, kc:kc + 1],
                                            w2_v[:, kc,
                                                 half * 512:(half + 1) * 512],
                                            start=(kc == 0),
                                            stop=(kc == DC - 1))
                        inc(inst, "TA", 1)
                    t.wait_ge(sh("VA"), bVA + DC + 4)
                    for half, bank in ((0, 4), (1, 5)):
                        inc(t.matmul(ps[bank][:, 0:512], onesf[:, :],
                                     rowD[0:1, half * 512:(half + 1) * 512],
                                     start=True, stop=True), "TA", 1)

                @blk.gpsimd
                def _(g):
                    g.memset(gy[:, :], 0.0)
                    g.memset(gh[:, :], 0.0)

        def _go():
            armax(gy, cc_t["i1"], cc_t["o1"], s1r[0:1, 0:1], sc_c(CSX),
                  sc_c(SHS))
            quant_apply(x_in, n128, 0, 1, sc_c(CSX), False, True,
                        qxt_tok=t_tot, ph0d=True)
            if stop_after == "quant1":
                return
            ffn_gateup(t_tot)
            if stop_after == "gateup1":
                return
            armax(gh, cc_t["i2"], cc_t["o2"], s1r[0:1, 1:2], sc_c(CSH),
                  sc_c(SHS))
            ffn_down(t_tot, x_in, xp_h, True, False, 0)
            if stop_after == "ffn1":
                return
            ln_batch_mr(n128, 2, 3)
            quant_apply(xp_h, n128, 2, 3, None, False, False)
            if stop_after == "x2q":
                return
            attn_main()
            if stop_after == "attn":
                return
            ar3_vrow()
            if stop_after == "ar3":
                return
            ln_stats_ymax(xpp_h, n128o, 4, 5, True, gy)
            armax(gy, cc_t["i4"], cc_t["o4"], s1r[0:1, 0:1], sc_c(CSX),
                  sc_c(SHS))
            quant_apply(xpp_h, n128o, 4, 5, sc_c(CSX), True, True,
                        qxt_tok=t_own)
            if stop_after == "quant2":
                return
            ffn_gateup(t_own)
            if stop_after == "gateup2":
                return
            armax(gh, cc_t["i5"], cc_t["o5"], s1r[0:1, 1:2], sc_c(CSH),
                  sc_c(SHS))
            ffn_down(t_own, xpp_h, out_ext, False, True, 0)

        _go()

        if dbg:
            with nc.Block() as blk:
                DQ = Stm("SQ")

                @blk.scalar
                def _(s):
                    DQ.issue(s.dma_start(dbg_h[:, 0:8], scal[:, :]))
                    DQ.issue(s.dma_start(dbg_h[:, 8:16], vrow[:, 0:8]))
                    DQ.issue(s.dma_start(dbg_h[:, 16:24], vrow[:, 8:16]))
                    DQ.issue(s.dma_start(dbg_h[:, 24:32], tot8[:, :]))
                    DQ.issue(s.dma_start(dbg_h[0:1, 32:64], s1r[:, :]))
                    DQ.wait(s, 4)

        with nc.Block() as blk:
            @blk.gpsimd
            def _(g):
                g.wait_ge(sh("SO0"), sv("SO0"))
                g.wait_ge(sh("SO1"), sv("SO1"))

    return nc


def kernel(**inputs):
    x = np.ascontiguousarray(np.asarray(inputs["L0_attn_input_data"],
                                        dtype=np.float32))
    B, S, _ = x.shape
    t_own = S // 2
    nc = build_nc(t_own)

    w1 = np.ascontiguousarray(inputs["L0_attn_W1"], dtype=np.float32)
    b1 = np.ascontiguousarray(inputs["L0_attn_bias1"], dtype=np.float32)
    w2 = np.ascontiguousarray(inputs["L0_attn_W2"], dtype=np.float32)
    b2 = np.ascontiguousarray(inputs["L0_attn_bias2"], dtype=np.float32)
    mw = np.ascontiguousarray(inputs["L0_attn_memory_weights"],
                              dtype=np.float32)
    cx = np.asarray(inputs["L0_ffn_clip_x"], dtype=np.float32).reshape(1, 1)
    cd = np.asarray(inputs["L0_ffn_clip_down"],
                    dtype=np.float32).reshape(1, 1)
    wg = np.ascontiguousarray(inputs["L0_ffn_w_gate"], dtype=np.float32)
    wu = np.ascontiguousarray(inputs["L0_ffn_w_up"], dtype=np.float32)
    wd = np.ascontiguousarray(inputs["L0_ffn_w_down"], dtype=np.float32)

    in_maps = []
    for c in range(8):
        b, h = c // 2, c % 2
        if h == 0:
            halo = np.zeros((HALO, D), dtype=np.float32)
            hmv = 0.0
        else:
            halo = x[b, t_own - HALO:t_own]
            hmv = 1.0
        shard = np.concatenate([halo, x[b, h * t_own:(h + 1) * t_own]],
                               axis=0)
        bsel = np.zeros((1, 4), dtype=np.float32)
        bsel[0, b] = 1.0
        in_maps.append({
            "x": np.ascontiguousarray(shard),
            "hmask": np.array([[hmv]], dtype=np.float32),
            "bsel": bsel,
            "W1": w1, "b1": b1, "W2": w2, "b2": b2, "mw": mw,
            "clip_x": cx, "clip_down": cd,
            "w_gate": wg, "w_up": wu, "w_down": wd,
        })

    res = run_bass_kernel_spmd(nc, in_maps, core_ids=list(range(8)))
    out = np.empty((B, S, D), dtype=np.float32)
    for c in range(8):
        b, h = c // 2, c % 2
        out[b, h * t_own:(h + 1) * t_own] = res.results[c]["out"]
    return out


# version 13 (final)
# speedup vs baseline: 1.0643x; 1.0029x over previous
import contextlib

import numpy as np

import concourse.bass as bass
import concourse.mybir as mybir
from concourse.alu_op_type import AluOpType as AO
from concourse.bass_utils import run_bass_kernel_spmd

F32 = mybir.dt.float32
BF16 = mybir.dt.bfloat16
I32 = mybir.dt.int32
AF = mybir.ActivationFunctionType
AX = mybir.AxisListType

D = 1024
IFF = 2816
DC = D // 128
ICN = IFF // 128
NTAP = 40
QMAX = 127.0
EPS = 1e-5
HALO = 128
CLIP_LO = -(QMAX + 1.49)
CLIP_HI = QMAX + 0.49


def _tiles(total, step=512):
    out, t = [], 0
    while t < total:
        out.append((t, min(step, total - t)))
        t += step
    return out


def build_nc(t_own, dbg=False, stop_after=None):
    nc = bass.Bass("TRN2", target_bir_lowering=False)
    t_tot = t_own + HALO
    n128 = t_tot // 128
    n128o = t_own // 128
    NT = _tiles(t_tot)
    TSEQ = 2 * t_own

    x_in = nc.dram_tensor("x", [t_tot, D], F32, kind="ExternalInput").ap()
    hm_in = nc.dram_tensor("hmask", [1, 1], F32, kind="ExternalInput").ap()
    bsel_in = nc.dram_tensor("bsel", [1, 4], F32, kind="ExternalInput").ap()
    w1_in = nc.dram_tensor("W1", [D, D], F32, kind="ExternalInput").ap()
    b1_in = nc.dram_tensor("b1", [D], F32, kind="ExternalInput").ap()
    w2_in = nc.dram_tensor("W2", [D, D], F32, kind="ExternalInput").ap()
    b2_in = nc.dram_tensor("b2", [D], F32, kind="ExternalInput").ap()
    mw_in = nc.dram_tensor("mw", [NTAP + 1, D], F32, kind="ExternalInput").ap()
    cx_in = nc.dram_tensor("clip_x", [1, 1], F32, kind="ExternalInput").ap()
    cd_in = nc.dram_tensor("clip_down", [1, 1], F32, kind="ExternalInput").ap()
    wg_in = nc.dram_tensor("w_gate", [IFF, D], F32, kind="ExternalInput").ap()
    wu_in = nc.dram_tensor("w_up", [IFF, D], F32, kind="ExternalInput").ap()
    wd_in = nc.dram_tensor("w_down", [D, IFF], F32, kind="ExternalInput").ap()
    out_ext = nc.dram_tensor("out", [t_own, D], F32, kind="ExternalOutput").ap()

    KI = "ExternalOutput" if dbg else "Internal"
    xp_h = nc.dram_tensor("xp_h", [t_tot, D], F32, kind=KI).ap()
    xpp_h = nc.dram_tensor("xpp_h", [t_own, D], F32, kind=KI).ap()
    qa_h = nc.dram_tensor("qa_h", [t_tot, D], BF16, kind=KI).ap()
    h_h = nc.dram_tensor("h_h", [IFF, t_tot], F32, kind=KI).ap()
    dbg_h = nc.dram_tensor("dbg_h", [128, 64], F32, kind=KI).ap() if dbg \
        else None
    qwg_h = nc.dram_tensor("qwg_h", [IFF, D], BF16, kind="Internal").ap()
    qwu_h = nc.dram_tensor("qwu_h", [IFF, D], BF16, kind="Internal").ap()
    qwd_h = nc.dram_tensor("qwd_h", [D, IFF], BF16, kind="Internal").ap()
    wgT_h = nc.dram_tensor("wgT_h", [D, IFF], BF16, kind="Internal").ap()
    wuT_h = nc.dram_tensor("wuT_h", [D, IFF], BF16, kind="Internal").ap()
    wdT_h = nc.dram_tensor("wdT_h", [IFF, D], BF16, kind="Internal").ap()
    misc_h = nc.dram_tensor("misc_h", [4096], F32, kind="Internal").ap()
    diag_h = nc.dram_tensor("diag_h", [DC * 128 * NTAP * 128], BF16,
                            kind="Internal").ap()
    cc_t = {}
    for i, shp in (("1", [1, 1]), ("2", [1, 1]), ("3", [32, 128]),
                   ("4", [1, 1]), ("5", [1, 1])):
        cc_t["i" + i] = nc.dram_tensor("cc%si" % i, shp, F32, kind="Internal")
        cc_t["o" + i] = nc.dram_tensor("cc%so" % i, shp, F32, kind="Internal")

    ctx = contextlib.ExitStack()
    with ctx:
        sbuf = lambda name, shape, dt: ctx.enter_context(
            nc.sbuf_tensor(name, shape, dt))

        AR_N = max(DC * t_tot, 2 * ICN * 512 + ICN * 1024,
                   2 * DC * 1024 + 2 * DC * 512 + DC * 640 + DC * 512)
        arena = sbuf("arena", [128, AR_N], BF16)
        qx_fm = arena[:, :DC * t_tot].rearrange("p (c t) -> p c t", c=DC)
        qh_v = arena[:, :2 * ICN * 512].rearrange(
            "p (b c t) -> p b c t", b=2, c=ICN)
        wdT_v = arena[:, 2 * ICN * 512:2 * ICN * 512 + ICN * 1024].rearrange(
            "p (c t) -> p c t", c=ICN)
        _o = 0
        w1_v = arena[:, _o:_o + DC * 1024].rearrange("p (c t) -> p c t", c=DC)
        _o += DC * 1024
        w2_v = arena[:, _o:_o + DC * 1024].rearrange("p (c t) -> p c t", c=DC)
        _o += DC * 1024
        x2f_v = arena[:, _o:_o + 2 * DC * 512].rearrange(
            "p (b c t) -> p b c t", b=2, c=DC)
        _o += 2 * DC * 512
        pr_v = arena[:, _o:_o + DC * 640].rearrange("p (c t) -> p c t", c=DC)
        _o += DC * 640
        z_v = arena[:, _o:_o + DC * 512].rearrange("p (c t) -> p c t", c=DC)
        _o += DC * 512
        stg_v = arena[:, :IFF]
        stg2_v = arena[:, IFF:IFF + 1024]

        xa = sbuf("xa", [128, 2, D], F32)
        xb = sbuf("xb", [128, 2, D], F32)
        qt = sbuf("qt", [128, 2, D], F32)
        qb = sbuf("qb", [128, 2, D], BF16)
        ab = sbuf("ab", [128, 2, D], F32)
        ob = sbuf("ob", [128, 2, D], F32)
        wgc = sbuf("wgc", [128, 2, DC, 128], BF16)
        wuc = sbuf("wuc", [128, 2, DC, 128], BF16)
        st_t = sbuf("st_t", [128, 4, 512], F32)
        hlu = sbuf("hlu", [128, 2, 2, 512], F32)
        ht = hlu[:, 0, :, :]
        hl = hlu
        hq_ti = sbuf("hq_ti", [128, 2, 2, 512], I32)
        qi = hq_ti[:, 0].rearrange("p a b -> p (a b)")
        diagc = sbuf("diagc", [128, 2, NTAP, 128], BF16)
        idf = sbuf("idf", [128, 128], F32)
        idb = sbuf("idb", [128, 128], BF16)
        onesf = sbuf("onesf", [1, 128], F32)
        io1 = sbuf("io1", [128, 128], I32)
        io2 = sbuf("io2", [128, 128], I32)

        mstat = sbuf("mstat", [128, 6 * n128], F32)
        mr = lambda w, t: mstat[:, w * n128 + t:w * n128 + t + 1]
        mrr = lambda w, n: mstat[:, w * n128:w * n128 + n]
        sums = sbuf("sums", [128, 2 * n128], F32)
        s_sum = lambda t: sums[:, t:t + 1]
        s_ssq = lambda t: sums[:, n128 + t:n128 + t + 1]
        sgu = sbuf("sgu", [128, 2 * ICN + DC], F32)
        sgc = lambda i: sgu[:, i:i + 1]
        suc = lambda i: sgu[:, ICN + i:ICN + i + 1]
        sdc = lambda i: sgu[:, 2 * ICN + i:2 * ICN + i + 1]
        cgu = sbuf("cgu", [128, 2 * ICN], F32)
        cgc = lambda i: cgu[:, i:i + 1]
        cuc = lambda i: cgu[:, ICN + i:ICN + i + 1]
        mwf = sbuf("mwf", [128, DC, NTAP + 1], F32)
        mwe = sbuf("mwe", [128, DC, NTAP], F32)
        b8 = sbuf("b8", [8, 2 * 128], F32)
        b1f = sbuf("b1f", [128, DC], F32)
        b2f = sbuf("b2f", [128, DC], F32)
        b1T = sbuf("b1T", [128, DC], F32)
        scal = sbuf("scal", [128, 8], F32)
        CSX, CSH, HMASK, SHS = 0, 1, 2, 3
        sc_c = lambda i: scal[:, i:i + 1]
        s1r = sbuf("s1r", [1, 32], F32)
        vrow = sbuf("vrow", [128, D], F32)
        tot8 = sbuf("tot8", [128, DC], F32)
        ctr32 = sbuf("ctr32", [128, 32], F32)
        ar32 = sbuf("ar32", [32, 128], F32)
        allT = sbuf("allT", [128, 32], F32)
        bsel_b = sbuf("bsel_b", [128, 4], F32)
        rowD = sbuf("rowD", [1, D], F32)
        rowD2 = sbuf("rowD2", [1, D], F32)
        t1 = sbuf("t1", [128, 64], F32)
        t2 = sbuf("t2", [128, 64], F32)
        t3 = sbuf("t3", [128, 64], F32)
        t4 = sbuf("t4", [128, 64], F32)
        gy = sbuf("gy", [128, 1], F32)
        gh = sbuf("gh", [128, 1], F32)
        u8 = sbuf("u8", [128, DC], F32)
        u8b = sbuf("u8b", [128, DC], BF16)
        t8b = sbuf("t8b", [128, DC], BF16)
        totf = sbuf("totf", [128, DC], F32)

        ps = [ctx.enter_context(nc.psum_tensor("ps%d" % i, [128, 512], F32))
              for i in range(8)]

        sems = {}
        for s in ["VA", "VB", "VC", "AA", "AB", "TA", "TB", "TC",
                  "PFA", "DG", "CC", "GI"]:
            sems[s] = [ctx.enter_context(nc.semaphore(s)), 0]
        for s in ["LA", "LB", "LW", "TR", "SQ", "SG", "SO", "LH", "CQ"]:
            sems[s + "0"] = [ctx.enter_context(nc.semaphore(s + "0")), 0]
            sems[s + "1"] = [ctx.enter_context(nc.semaphore(s + "1")), 0]

        def inc(inst, s, n):
            sems[s][1] += n
            inst.then_inc(sems[s][0], n)
            return sems[s][1]

        def sh(s):
            return sems[s][0]

        def sv(s):
            return sems[s][1]

        class Stm:

            def __init__(self, name, per=1):
                self.k = [name + "0", name + "1"]
                self.b = [sv(self.k[0]), sv(self.k[1])]
                self.per = per
                self.d = 0

            def issue(self, inst):
                step = self.d // self.per
                inc(inst, self.k[step % 2], 16)
                self.d += 1

            def wait(self, eng, step):
                eng.wait_ge(sh(self.k[step % 2]),
                            self.b[step % 2] + 16 * self.per * (step // 2 + 1))

            def wait_all(self, eng):
                eng.wait_ge(sh(self.k[0]), sv(self.k[0]))
                eng.wait_ge(sh(self.k[1]), sv(self.k[1]))

        def wait_all_stores(eng):
            for s in ("SQ0", "SQ1", "SG0", "SG1", "SO0", "SO1"):
                eng.wait_ge(sh(s), sv(s))

        def entry_guard():
            vals = {s: sv(s) for s in ("SQ0", "SQ1", "SG0", "SG1",
                                       "SO0", "SO1")}

            def g(eng):
                for s, v in vals.items():
                    eng.wait_ge(sh(s), v)
            return g

        with nc.Block() as blk:
            L0a = Stm("LW")

            @blk.sync
            def _(sync):
                L0a.issue(sync.dma_start(qt[0:NTAP + 1, 0, :], mw_in[:, :]))
                L0a.issue(sync.dma_start(
                    b8[0:8, 0:128], b1_in.rearrange("(c p) -> c p", p=128)))
                L0a.issue(sync.dma_start(
                    b8[0:8, 128:256], b2_in.rearrange("(c p) -> c p", p=128)))
                sync.wait_ge(sh("GI"), 1)
                L0a.issue(sync.dma_start(s1r[0:1, 0:1], cx_in[:, :]))
                L0a.issue(sync.dma_start(s1r[0:1, 1:2], cd_in[:, :]))
                L0a.issue(sync.dma_start(s1r[0:1, 2:3], hm_in[:, :]))
                L0a.issue(sync.dma_start(s1r[0:1, 4:8], bsel_in[:, :]))

            @blk.gpsimd
            def _(g):
                g.iota(io1[:, :], [[1, 128]], channel_multiplier=0)
                g.iota(io2[:, :], [[0, 128]], channel_multiplier=1)
                g.memset(onesf[:, :], 1.0)
                g.memset(gy[:, :], 0.0)
                g.memset(gh[:, :], 0.0)
                g.memset(tot8[:, :], 0.0)
                g.memset(s1r[:, :], 0.0)
                g.memset(t1[:, :], 0.0)
                g.memset(scal[:, :], 0.0)
                g.memset(vrow[:, :], 0.0)
                inc(g.memset(pr_v[:, :, :], 0.0), "GI", 1)

            @blk.vector
            def _(vec):
                vec.wait_ge(sh("GI"), 1)
                vec.tensor_tensor(idf[:, :], io1[:, :], io2[:, :],
                                  op=AO.is_equal)
                vec.drain()
                vec.tensor_copy(idb[:, :], idf[:, :])
                vec.drain()

        with nc.Block() as blk:
            bTA, bVA = sv("TA"), sv("VA")
            SQb = Stm("SQ")

            @blk.tensor
            def _(t):
                L0a.wait_all(t)
                for c in range(DC):
                    inc(t.matmul(ps[c][:, 0:NTAP + 1],
                                 qt[0:NTAP + 1, 0, c * 128:(c + 1) * 128],
                                 idf[0:NTAP + 1, 0:NTAP + 1],
                                 start=True, stop=True), "TA", 1)
                t.wait_ge(sh("VA"), bVA + 1)
                inc(t.matmul(ps[0][:, 0:DC], b8[0:8, 0:128], idf[0:8, 0:8],
                             start=True, stop=True), "TA", 1)
                inc(t.matmul(ps[1][:, 0:DC], b8[0:8, 128:256], idf[0:8, 0:8],
                             start=True, stop=True), "TA", 1)
                inc(t.matmul(ps[2][:, 0:1], onesf[:, :], s1r[0:1, 2:3],
                             start=True, stop=True), "TA", 1)
                inc(t.matmul(ps[3][:, 0:4], onesf[:, :], s1r[0:1, 4:8],
                             start=True, stop=True), "TA", 1)
                t.wait_ge(sh("VA"), bVA + 2)
                inc(t.matmul(ps[4][0:DC, 0:128], b2f[:, :], idf[:, :],
                             start=True, stop=True), "TA", 1)

            @blk.vector
            def _(vec):
                vec.wait_ge(sh("TA"), bTA + DC)
                for c in range(DC):
                    vec.tensor_copy(mwf[:, c, :], ps[c][:, 0:NTAP + 1])
                vec.drain()
                inc(vec.tensor_copy(t1[:, 0:1], mwf[:, 0, 0:1]), "VA", 1)
                vec.wait_ge(sh("TA"), bTA + DC + 4)
                vec.tensor_copy(b1f[:, :], ps[0][:, 0:DC])
                vec.tensor_copy(b2f[:, :], ps[1][:, 0:DC])
                vec.tensor_copy(sc_c(HMASK), ps[2][:, 0:1])
                vec.tensor_copy(bsel_b[:, :], ps[3][:, 0:4])
                vec.drain()
                vec.tensor_scalar(b1T[:, :], b1f[:, :], float(TSEQ), None,
                                  op0=AO.mult)
                for c in range(DC):
                    vec.tensor_scalar(mwe[:, c, :], mwf[:, c, 1:],
                                      mwf[:, c, 0:1], None, op0=AO.subtract)
                vec.drain()
                inc(vec.tensor_copy(t1[:, 0:1], b1f[:, 0:1]), "VA", 1)
                vec.wait_ge(sh("TA"), bTA + DC + 5)
                inc(vec.tensor_copy(ar32[0:DC, :], ps[4][0:DC, 0:128]),
                    "VA", 1)
                vec.drain()

            @blk.scalar
            def _(s):
                s.wait_ge(sh("VA"), bVA + 3)
                SQb.issue(s.dma_start(misc_h[1024:1024 + D].rearrange(
                    "(c p) -> c p", p=128), ar32[0:DC, :]))
                SQb.wait(s, 0)
                SQb.issue(s.dma_start(rowD2[0:1, :],
                                      misc_h[None, 1024:1024 + D]))
                SQb.wait(s, 1)

        with nc.Block() as blk:
            bVA = sv("VA")
            SQp = Stm("SQ")

            @blk.vector
            def _(vec):
                for c in range(DC):
                    if c >= 2:
                        SQp.wait(vec, c - 2)
                    for j in range(NTAP):
                        vec.tensor_scalar(diagc[:, c % 2, j, :], idb[:, :],
                                          mwe[:, c, j:j + 1], None,
                                          op0=AO.mult)
                    vec.drain()
                    inc(vec.tensor_copy(t1[:, 0:1], t1[:, 0:1]), "VA", 1)
                    vec.drain()

            @blk.scalar
            def _(s):
                for c in range(DC):
                    s.wait_ge(sh("VA"), bVA + c + 1)
                    SQp.issue(s.dma_start(
                        bass.AP(diag_h.tensor, c * 128 * NTAP * 128,
                                [[NTAP * 128, 128], [1, NTAP * 128]]),
                        diagc[:, c % 2, :, :]))

        def head_merged():
            PCS = [(0, 1024), (1024, 1024), (2048, 768)]
            qunits = ([('g', t) for t in range(ICN)] +
                      [('u', t) for t in range(ICN)] +
                      [('d', t) for t in range(DC)])
            npl = n128 // 2
            lunits = [('l', p) for p in range(npl)]
            if n128 % 2:
                lunits.append(('ls', 2 * npl))

            def abs_pos(t):
                if t < 2 * npl:
                    return 4 * (t // 2) + 3 + (t % 2)
                return 4 * npl + 2

            U = []
            li = qi_ = 0
            nqu, nlu = len(qunits), len(lunits)
            while li < nlu or qi_ < nqu:
                take_q = 1 if qi_ < nqu else 0
                if take_q and (li >= nlu or qi_ * nlu <= li * nqu):
                    U.append(qunits[qi_])
                    qi_ += 1
                else:
                    U.append(lunits[li])
                    li += 1
            la_steps = {}
            qstep_of = {}
            lab_of = {}
            vcr_of = {}
            la = 0
            qs = 0
            nlab = 0
            nvc = 0
            for u in U:
                kind, i = u
                if kind in ('g', 'u'):
                    la_steps[u] = [la]
                    la += 1
                    qstep_of[u] = [qs]
                    qs += 1
                elif kind == 'd':
                    qstep_of[u] = [qs, qs + 1, qs + 2]
                    qs += 3
                else:
                    nlab += 1
                    lab_of[u] = nlab
                    w_ = 2 if kind == 'l' else 1
                    vcr_of[u] = nvc + 1
                    nvc += 2 * w_
            nq = qs
            wd_stage = [(qt[:, 0, :], qt[:, 1, :]), None]

            with nc.Block() as blk:
                bVA, bVB, bVC = sv("VA"), sv("VB"), sv("VC")
                bAA, bAB, bTB, bTC = sv("AA"), sv("AB"), sv("TB"), sv("TC")
                LA, SQ = Stm("LA"), Stm("SQ")
                LB = Stm("CQ")
                LDW = Stm("LH", per=3)
                eg = entry_guard()

                @blk.sync
                def _(sync):
                    wait_all_stores(sync)
                    for u in U:
                        kind, i = u
                        if kind in ('g', 'u'):
                            q = qstep_of[u][0]
                            if q >= 2:
                                sync.wait_ge(sh("VA"), bVA + (q - 1))
                            wap = wg_in if kind == 'g' else wu_in
                            LA.issue(sync.dma_start(
                                xa[:, q % 2, :],
                                wap[i * 128:(i + 1) * 128, :]))
                        elif kind == 'd':
                            if i >= 1:
                                sync.wait_ge(sh("VB"), bVB + i)
                            for p3, (o3, n3) in enumerate(PCS):
                                dst = (qt[:, 0, :n3], qt[:, 1, :n3],
                                       ob[:, 0, :n3])[p3]
                                LDW.issue(sync.dma_start(
                                    dst,
                                    wd_in[i * 128:(i + 1) * 128,
                                          o3:o3 + n3]))

                @blk.gpsimd
                def _(g):
                    for u in U:
                        kind, i = u
                        if kind not in ('l', 'ls'):
                            continue
                        ts_ = [2 * i, 2 * i + 1] if kind == 'l' else [i]
                        for t in ts_:
                            if t >= 2:
                                g.wait_ge(sh("AA"), bAA + abs_pos(t - 2))
                            LB.issue(g.dma_start(
                                xb[:, t % 2, :],
                                x_in[t * 128:(t + 1) * 128, :]))

                @blk.vector
                def _(vec):
                    eg(vec)
                    for u in U:
                        kind, i = u
                        if kind in ('g', 'u'):
                            q = qstep_of[u][0]
                            LA.wait(vec, la_steps[u][0])
                            if q >= 2:
                                SQ.wait(vec, q - 2)
                            w_t = xa[:, q % 2, :]
                            scol = sgc if kind == 'g' else suc
                            vec.tensor_reduce(t1[:, 0:1], w_t, axis=AX.X,
                                              op=AO.max,
                                              apply_absolute_value=True)
                            vec.drain()
                            vec.reciprocal(t1[:, 1:2], t1[:, 0:1])
                            vec.drain()
                            vec.tensor_scalar(scol(i), t1[:, 0:1],
                                              1.0 / QMAX, None, op0=AO.mult)
                            vec.tensor_scalar(t1[:, 2:3], t1[:, 1:2], QMAX,
                                              None, op0=AO.mult)
                            vec.drain()
                            vec.tensor_scalar(qi[:, :], w_t, t1[:, 2:3],
                                              None, op0=AO.mult)
                            vec.drain()
                            inc(vec.tensor_copy(qb[:, q % 2, :], qi[:, :]),
                                "VA", 1)
                            vec.drain()
                        elif kind == 'd':
                            LDW.wait(vec, i)
                            srcs = (qt[:, 0, :], qt[:, 1, :], ob[:, 0, :])
                            for p3, (o3, n3) in enumerate(PCS):
                                vec.tensor_reduce(t1[:, 3 + p3:4 + p3],
                                                  srcs[p3][:, :n3], axis=AX.X,
                                                  op=AO.max,
                                                  apply_absolute_value=True)
                                vec.drain()
                            vec.tensor_reduce(t1[:, 6:7], t1[:, 3:6],
                                              axis=AX.X, op=AO.max)
                            vec.drain()
                            vec.reciprocal(t1[:, 7:8], t1[:, 6:7])
                            vec.drain()
                            vec.tensor_scalar(sdc(i), t1[:, 6:7], 1.0 / QMAX,
                                              None, op0=AO.mult)
                            vec.tensor_scalar(t1[:, 8:9], t1[:, 7:8], QMAX,
                                              None, op0=AO.mult)
                            vec.drain()
                            vec.tensor_scalar(t1[:, 9:10], sdc(i), 0.5, None,
                                              op0=AO.mult)
                            vec.drain()
                            for p3, (o3, n3) in enumerate(PCS):
                                q = qstep_of[u][p3]
                                if q >= 2:
                                    SQ.wait(vec, q - 2)
                                vec.tensor_scalar(qi[:, :n3],
                                                  srcs[p3][:, :n3],
                                                  t1[:, 8:9], None,
                                                  op0=AO.mult)
                                vec.drain()
                                inc(vec.tensor_scalar(qb[:, q % 2, :n3],
                                                      qi[:, :n3],
                                                      t1[:, 9:10], None,
                                                      op0=AO.mult), "VA", 1)
                                vec.drain()
                            inc(vec.tensor_copy(t1[:, 0:1], t1[:, 0:1]),
                                "VB", 1)
                        else:
                            ts_ = ([2 * i, 2 * i + 1] if kind == 'l'
                                   else [i])
                            t0 = ts_[0]
                            w = len(ts_)
                            c = 10 + 16 * (i % 2)
                            for tr, t in enumerate(ts_):
                                LB.wait(vec, t)
                                inc(vec.tensor_reduce(
                                    s_sum(t), xb[:, t % 2, :],
                                    axis=AX.X, op=AO.add), "VC", 1)
                            vec.wait_ge(sh("AA"), bAA + abs_pos(t0) - 1)
                            vec.tensor_scalar(
                                mstat[:, t0:t0 + w], sums[:, t0:t0 + w],
                                1.0 / D, None, op0=AO.mult)
                            vec.tensor_scalar(
                                t1[:, c + 2:c + 2 + w],
                                sums[:, n128 + t0:n128 + t0 + w],
                                1.0 / D, EPS, op0=AO.mult, op1=AO.add)
                            vec.drain()
                            vec.tensor_tensor(t1[:, c:c + w],
                                              mstat[:, t0:t0 + w],
                                              mstat[:, t0:t0 + w],
                                              op=AO.mult)
                            vec.drain()
                            vec.tensor_tensor(t1[:, c + 4:c + 4 + w],
                                              t1[:, c + 2:c + 2 + w],
                                              t1[:, c:c + w],
                                              op=AO.subtract)
                            vec.drain()
                            inc(vec.reciprocal(t1[:, c + 6:c + 6 + w],
                                               t1[:, c + 4:c + 4 + w]),
                                "TB", 1)
                            vec.drain()
                            vec.wait_ge(sh("AB"), bAB + lab_of[u])
                            vec.reciprocal(t1[:, c + 10:c + 10 + w],
                                           t1[:, c + 8:c + 8 + w])
                            vec.drain()
                            vec.tensor_tensor(t1[:, c + 10:c + 10 + w],
                                              t1[:, c + 10:c + 10 + w],
                                              t1[:, c + 6:c + 6 + w],
                                              op=AO.mult)
                            vec.drain()
                            vec.tensor_tensor(t1[:, c + 10:c + 10 + w],
                                              t1[:, c + 10:c + 10 + w],
                                              t1[:, c + 8:c + 8 + w],
                                              op=AO.add)
                            vec.drain()
                            vec.tensor_scalar(
                                mstat[:, n128 + t0:n128 + t0 + w],
                                t1[:, c + 10:c + 10 + w], 0.5, None,
                                op0=AO.mult)
                            vec.drain()
                            inc(vec.scalar_tensor_tensor(
                                t1[:, c + 12:c + 12 + w],
                                mstat[:, t0:t0 + w], -1.0,
                                mstat[:, n128 + t0:n128 + t0 + w],
                                op0=AO.mult, op1=AO.mult), "TC", 1)
                            for tr, t in enumerate(ts_):
                                vec.wait_ge(sh("AA"),
                                            bAA + abs_pos(t0) + tr)
                                vec.tensor_reduce(
                                    t1[:, c + 14 + tr:c + 15 + tr],
                                    ab[:, t % 2, :], axis=AX.X, op=AO.max)
                                vec.drain()
                                inc(vec.tensor_tensor(
                                    gy[:, :], gy[:, :],
                                    t1[:, c + 14 + tr:c + 15 + tr],
                                    op=AO.max), "VC", 1)
                                vec.drain()

                @blk.scalar
                def _(s):
                    eg(s)
                    for u in U:
                        kind, i = u
                        if kind in ('g', 'u'):
                            q = qstep_of[u][0]
                            s.wait_ge(sh("VA"), bVA + q + 1)
                            q_dst = qwg_h if kind == 'g' else qwu_h
                            SQ.issue(s.dma_start(
                                q_dst[i * 128:(i + 1) * 128, :],
                                qb[:, q % 2, :]))
                        elif kind == 'd':
                            for p3, (o3, n3) in enumerate(PCS):
                                q = qstep_of[u][p3]
                                s.wait_ge(sh("VA"), bVA + q + 1)
                                SQ.issue(s.dma_start(
                                    qwd_h[i * 128:(i + 1) * 128,
                                          o3:o3 + n3],
                                    qb[:, q % 2, :n3]))
                        else:
                            ts_ = ([2 * i, 2 * i + 1] if kind == 'l'
                                   else [i])
                            t0 = ts_[0]
                            w = len(ts_)
                            c = 10 + 16 * (i % 2)
                            for tr, t in enumerate(ts_):
                                s.wait_ge(sh("VC"), bVC + vcr_of[u] + tr)
                                inc(s.activation(ab[:, t % 2, :],
                                                 xb[:, t % 2, :], AF.Square,
                                                 accum_out=s_ssq(t)),
                                    "AA", 1)
                            s.wait_ge(sh("TB"), bTB + lab_of[u])
                            inc(s.activation(t1[:, c + 8:c + 8 + w],
                                             t1[:, c + 6:c + 6 + w],
                                             AF.Sqrt), "AB", 1)
                            s.wait_ge(sh("TC"), bTC + lab_of[u])
                            for tr, t in enumerate(ts_):
                                inc(s.activation(
                                    ab[:, t % 2, :], xb[:, t % 2, :],
                                    AF.Abs,
                                    bias=t1[:, c + 12 + tr:c + 13 + tr],
                                    scale=mstat[:, n128 + t:n128 + t + 1]),
                                    "AA", 1)

        def quant_weight(wap, nchunk, scol, q_dst):
            with nc.Block() as blk:
                bVA = sv("VA")
                LA, SQ = Stm("LA"), Stm("SQ")
                eg = entry_guard()

                @blk.sync
                def _(sync):
                    for i in range(nchunk):
                        if i >= 2:
                            sync.wait_ge(sh("VA"), bVA + (i - 1))
                        LA.issue(sync.dma_start(
                            xa[:, i % 2, :], wap[i * 128:(i + 1) * 128, :]))

                @blk.vector
                def _(vec):
                    eg(vec)
                    for i in range(nchunk):
                        LA.wait(vec, i)
                        if i >= 2:
                            SQ.wait(vec, i - 2)
                        w_t = xa[:, i % 2, :]
                        vec.tensor_reduce(t1[:, 0:1], w_t, axis=AX.X,
                                          op=AO.max, apply_absolute_value=True)
                        vec.drain()
                        vec.reciprocal(t1[:, 1:2], t1[:, 0:1])
                        vec.drain()
                        vec.tensor_scalar(scol(i), t1[:, 0:1], 1.0 / QMAX,
                                          None, op0=AO.mult)
                        vec.tensor_scalar(t1[:, 2:3], t1[:, 1:2], QMAX, None,
                                          op0=AO.mult)
                        vec.drain()
                        vec.tensor_scalar(qi[:, :], w_t, t1[:, 2:3],
                                          None, op0=AO.mult)
                        vec.drain()
                        inc(vec.tensor_copy(qb[:, i % 2, :], qi[:, :]),
                            "VA", 1)
                        vec.drain()

                @blk.scalar
                def _(s):
                    for i in range(nchunk):
                        s.wait_ge(sh("VA"), bVA + (i + 1))
                        SQ.issue(s.dma_start(q_dst[i * 128:(i + 1) * 128, :],
                                             qb[:, i % 2, :]))

        head_merged()

        def quant_weight_wide():
            PCS = [(0, 1024), (1024, 1024), (2048, 768)]
            np3 = len(PCS)
            with nc.Block() as blk:
                bVA, bVB = sv("VA"), sv("VB")
                LA, SQ = Stm("LA"), Stm("SQ")
                eg = entry_guard()

                @blk.sync
                def _(sync):
                    j = 0
                    for i in range(DC):
                        for p3, (o3, n3) in enumerate(PCS):
                            if j >= 2:
                                sync.wait_ge(sh("VB"), bVB + (j - 1))
                            LA.issue(sync.dma_start(
                                xa[:, j % 2, :n3],
                                wd_in[i * 128:(i + 1) * 128, o3:o3 + n3]))
                            j += 1
                        for p3, (o3, n3) in enumerate(PCS):
                            sync.wait_ge(sh("VB"), bVB + (j - 1))
                            LA.issue(sync.dma_start(
                                xa[:, j % 2, :n3],
                                wd_in[i * 128:(i + 1) * 128, o3:o3 + n3]))
                            j += 1

                @blk.vector
                def _(vec):
                    eg(vec)
                    j = 0
                    for i in range(DC):
                        for p3, (o3, n3) in enumerate(PCS):
                            LA.wait(vec, j)
                            vec.tensor_reduce(t1[:, p3:p3 + 1],
                                              xa[:, j % 2, :n3], axis=AX.X,
                                              op=AO.max,
                                              apply_absolute_value=True)
                            vec.drain()
                            inc(vec.tensor_copy(t1[:, 8:9], t1[:, 0:1]),
                                "VB", 1)
                            j += 1
                        vec.tensor_reduce(t1[:, 4:5], t1[:, 0:np3], axis=AX.X,
                                          op=AO.max)
                        vec.drain()
                        vec.reciprocal(t1[:, 5:6], t1[:, 4:5])
                        vec.drain()
                        vec.tensor_scalar(sdc(i), t1[:, 4:5], 1.0 / QMAX,
                                          None, op0=AO.mult)
                        vec.tensor_scalar(t1[:, 6:7], t1[:, 5:6], QMAX, None,
                                          op0=AO.mult)
                        vec.drain()
                        vec.tensor_scalar(t1[:, 7:8], sdc(i), 0.5, None,
                                          op0=AO.mult)
                        vec.drain()
                        for p3, (o3, n3) in enumerate(PCS):
                            LA.wait(vec, j)
                            jq = i * np3 + p3
                            if jq >= 2:
                                SQ.wait(vec, jq - 2)
                            vec.tensor_scalar(qi[:, :n3],
                                              xa[:, j % 2, :n3], t1[:, 6:7],
                                              None, op0=AO.mult)
                            vec.drain()
                            inc(vec.tensor_scalar(qb[:, jq % 2, :n3],
                                                  qi[:, :n3], t1[:, 7:8],
                                                  None, op0=AO.mult),
                                "VA", 1)
                            inc(vec.tensor_copy(t1[:, 8:9], t1[:, 0:1]),
                                "VB", 1)
                            vec.drain()
                            j += 1

                @blk.scalar
                def _(s):
                    for i in range(DC):
                        for p3, (o3, n3) in enumerate(PCS):
                            jq = i * np3 + p3
                            s.wait_ge(sh("VA"), bVA + (jq + 1))
                            SQ.issue(s.dma_start(
                                qwd_h[i * 128:(i + 1) * 128, o3:o3 + n3],
                                qb[:, jq % 2, :n3]))


        def ln_batch_mr(ntile, m_which, r_which):
            with nc.Block() as blk:
                bVA, bAA = sv("VA"), sv("AA")
                n = ntile

                @blk.vector
                def _(vec):
                    vec.tensor_scalar(mrr(m_which, n), sums[:, 0:n], 1.0 / D,
                                      None, op0=AO.mult)
                    vec.drain()
                    vec.tensor_tensor(t1[:, :n], mrr(m_which, n),
                                      mrr(m_which, n), op=AO.mult)
                    vec.drain()
                    vec.scalar_tensor_tensor(t2[:, :n], sums[:, n128:n128 + n],
                                             1.0 / D, t1[:, :n], op0=AO.mult,
                                             op1=AO.subtract)
                    vec.drain()
                    vec.tensor_scalar(t2[:, :n], t2[:, :n], EPS, None,
                                      op0=AO.add)
                    vec.drain()
                    inc(vec.reciprocal(t3[:, :n], t2[:, :n]), "VA", 1)
                    vec.drain()
                    vec.wait_ge(sh("AA"), bAA + 1)
                    vec.reciprocal(t2[:, :n], t4[:, :n])
                    vec.drain()
                    vec.tensor_tensor(t2[:, :n], t2[:, :n], t3[:, :n],
                                      op=AO.mult)
                    vec.drain()
                    vec.tensor_tensor(t2[:, :n], t2[:, :n], t4[:, :n],
                                      op=AO.add)
                    vec.drain()
                    vec.tensor_scalar(mrr(r_which, n), t2[:, :n], 0.5, None,
                                      op0=AO.mult)
                    vec.drain()

                @blk.scalar
                def _(s):
                    s.wait_ge(sh("VA"), bVA + 1)
                    inc(s.activation(t4[:, :n], t3[:, :n], AF.Sqrt), "AA", 1)

        def ln_stats_ymax(src_h, ntile, m_which, r_which, vrow_add, gdst):
            assert ntile % 2 == 0
            npair = ntile // 2

            def mp(w, t0):
                return mstat[:, w * n128 + t0:w * n128 + t0 + 2]

            with nc.Block() as blk:
                bVA, bVB, bVC = sv("VA"), sv("VB"), sv("VC")
                bAA, bAB = sv("AA"), sv("AB")
                LA = Stm("LA")

                @blk.sync
                def _(sync):
                    wait_all_stores(sync)
                    for t in range(ntile):
                        if t >= 2:
                            tp, tr = (t - 2) // 2, (t - 2) % 2
                            sync.wait_ge(sh("AA"), bAA + 4 * tp + 3 + tr)
                        LA.issue(sync.dma_start(
                            xa[:, t % 2, :], src_h[t * 128:(t + 1) * 128, :]))

                @blk.vector
                def _(vec):
                    for p in range(npair):
                        t0 = 2 * p
                        c = 10 + 16 * (p % 2)
                        for tr in range(2):
                            LA.wait(vec, t0 + tr)
                            if vrow_add:
                                vec.tensor_tensor(xa[:, (t0 + tr) % 2, :],
                                                  xa[:, (t0 + tr) % 2, :],
                                                  vrow[:, :], op=AO.add)
                                vec.drain()
                            inc(vec.tensor_reduce(s_sum(t0 + tr),
                                                  xa[:, (t0 + tr) % 2, :],
                                                  axis=AX.X, op=AO.add),
                                "VA", 1)
                        vec.wait_ge(sh("AA"), bAA + 4 * p + 2)
                        vec.tensor_scalar(mp(m_which, t0),
                                          sums[:, t0:t0 + 2], 1.0 / D,
                                          None, op0=AO.mult)
                        vec.tensor_scalar(t1[:, c + 2:c + 4],
                                          sums[:, n128 + t0:n128 + t0 + 2],
                                          1.0 / D, EPS, op0=AO.mult,
                                          op1=AO.add)
                        vec.drain()
                        vec.tensor_tensor(t1[:, c:c + 2], mp(m_which, t0),
                                          mp(m_which, t0), op=AO.mult)
                        vec.drain()
                        vec.tensor_tensor(t1[:, c + 4:c + 6],
                                          t1[:, c + 2:c + 4],
                                          t1[:, c:c + 2], op=AO.subtract)
                        vec.drain()
                        inc(vec.reciprocal(t1[:, c + 6:c + 8],
                                           t1[:, c + 4:c + 6]), "VB", 1)
                        vec.drain()
                        vec.wait_ge(sh("AB"), bAB + p + 1)
                        vec.reciprocal(t1[:, c + 10:c + 12],
                                       t1[:, c + 8:c + 10])
                        vec.drain()
                        vec.tensor_tensor(t1[:, c + 10:c + 12],
                                          t1[:, c + 10:c + 12],
                                          t1[:, c + 6:c + 8], op=AO.mult)
                        vec.drain()
                        vec.tensor_tensor(t1[:, c + 10:c + 12],
                                          t1[:, c + 10:c + 12],
                                          t1[:, c + 8:c + 10], op=AO.add)
                        vec.drain()
                        vec.tensor_scalar(mp(r_which, t0),
                                          t1[:, c + 10:c + 12], 0.5, None,
                                          op0=AO.mult)
                        vec.drain()
                        inc(vec.scalar_tensor_tensor(
                            t1[:, c + 12:c + 14], mp(m_which, t0), -1.0,
                            mp(r_which, t0), op0=AO.mult, op1=AO.mult),
                            "VC", 1)
                        for tr in range(2):
                            vec.wait_ge(sh("AA"), bAA + 4 * p + 3 + tr)
                            vec.tensor_reduce(t1[:, c + 14 + tr:c + 15 + tr],
                                              ab[:, (t0 + tr) % 2, :],
                                              axis=AX.X, op=AO.max)
                            vec.drain()
                            inc(vec.tensor_tensor(
                                gdst[:, :], gdst[:, :],
                                t1[:, c + 14 + tr:c + 15 + tr],
                                op=AO.max), "VA", 1)
                            vec.drain()

                @blk.scalar
                def _(s):
                    for p in range(npair):
                        t0 = 2 * p
                        c = 10 + 16 * (p % 2)
                        for tr in range(2):
                            s.wait_ge(sh("VA"), bVA + 4 * p + 1 + tr)
                            inc(s.activation(ab[:, (t0 + tr) % 2, :],
                                             xa[:, (t0 + tr) % 2, :],
                                             AF.Square,
                                             accum_out=s_ssq(t0 + tr)),
                                "AA", 1)
                        s.wait_ge(sh("VB"), bVB + p + 1)
                        inc(s.activation(t1[:, c + 8:c + 10],
                                         t1[:, c + 6:c + 8], AF.Sqrt),
                            "AB", 1)
                        s.wait_ge(sh("VC"), bVC + p + 1)
                        for tr in range(2):
                            inc(s.activation(
                                ab[:, (t0 + tr) % 2, :],
                                xa[:, (t0 + tr) % 2, :], AF.Abs,
                                bias=t1[:, c + 12 + tr:c + 13 + tr],
                                scale=mr(r_which, t0 + tr)), "AA", 1)

        def ln_stats(src_h, ntile, m_which, r_which, vrow_add):
            with nc.Block() as blk:
                bVA, bAA = sv("VA"), sv("AA")
                LA = Stm("LA")

                @blk.sync
                def _(sync):
                    wait_all_stores(sync)
                    for t in range(ntile):
                        if t >= 2:
                            sync.wait_ge(sh("AA"), bAA + (t - 1))
                        LA.issue(sync.dma_start(
                            xa[:, t % 2, :], src_h[t * 128:(t + 1) * 128, :]))

                @blk.vector
                def _(vec):
                    for t in range(ntile):
                        LA.wait(vec, t)
                        if vrow_add:
                            vec.tensor_tensor(xa[:, t % 2, :], xa[:, t % 2, :],
                                              vrow[:, :], op=AO.add)
                            vec.drain()
                        inc(vec.tensor_reduce(s_sum(t), xa[:, t % 2, :],
                                              axis=AX.X, op=AO.add), "VA", 1)

                @blk.scalar
                def _(s):
                    for t in range(ntile):
                        s.wait_ge(sh("VA"), bVA + t + 1)
                        inc(s.activation(ab[:, t % 2, :], xa[:, t % 2, :],
                                         AF.Square, accum_out=s_ssq(t)),
                            "AA", 1)

            ln_batch_mr(ntile, m_which, r_which)

        def ymax(src_h, ntile, m_which, r_which, vrow_add, gdst):
            with nc.Block() as blk:
                bVA, bVB, bAA = sv("VA"), sv("VB"), sv("AA")
                LB = Stm("LB")

                @blk.sync
                def _(sync):
                    wait_all_stores(sync)
                    for t in range(ntile):
                        if t >= 2:
                            sync.wait_ge(sh("AA"), bAA + (t - 1))
                        LB.issue(sync.dma_start(
                            xb[:, t % 2, :], src_h[t * 128:(t + 1) * 128, :]))

                @blk.vector
                def _(vec):
                    for t in range(ntile):
                        LB.wait(vec, t)
                        if vrow_add:
                            vec.tensor_tensor(xb[:, t % 2, :], xb[:, t % 2, :],
                                              vrow[:, :], op=AO.add)
                            vec.drain()
                        inc(vec.scalar_tensor_tensor(
                            t1[:, 32 + (t % 2):33 + (t % 2)], mr(m_which, t),
                            -1.0, mr(r_which, t), op0=AO.mult, op1=AO.mult),
                            "VB", 1)
                        vec.wait_ge(sh("AA"), bAA + t + 1)
                        vec.tensor_reduce(t1[:, 34:35], ab[:, t % 2, :],
                                          axis=AX.X, op=AO.max)
                        vec.drain()
                        inc(vec.tensor_tensor(gdst[:, :], gdst[:, :],
                                              t1[:, 34:35], op=AO.max),
                            "VA", 1)
                        vec.drain()

                @blk.scalar
                def _(s):
                    for t in range(ntile):
                        s.wait_ge(sh("VB"), bVB + t + 1)
                        inc(s.activation(ab[:, t % 2, :], xb[:, t % 2, :],
                                         AF.Abs,
                                         bias=t1[:, 32 + (t % 2):33 + (t % 2)],
                                         scale=mr(r_which, t)), "AA", 1)

        def armax(gsrc, cci, cco, clip_slot, csx_dst, s_dst):
            with nc.Block() as blk:
                bTA, bAA, bVA, bCC = sv("TA"), sv("AA"), sv("VA"), sv("CC")
                SG = Stm("SG")

                @blk.tensor
                def _(t):
                    inc(t.matmul(ps[6][0:1, 0:128], gsrc[:, :], idf[:, :],
                                 start=True, stop=True), "TA", 1)
                    t.wait_ge(sh("VA"), bVA + 2)
                    inc(t.matmul(ps[7][0:128, 0:2], onesf[:, :],
                                 s1r[0:1, 16:18], start=True, stop=True),
                        "TA", 1)

                @blk.vector
                def _(vec):
                    vec.wait_ge(sh("TA"), bTA + 1)
                    inc(vec.tensor_reduce(s1r[0:1, 9:10], ps[6][0:1, 0:128],
                                          axis=AX.X, op=AO.max), "VA", 1)
                    vec.wait_ge(sh("CC"), bCC + 1)
                    SG.wait(vec, 1)
                    vec.wait_ge(sh("AA"), bAA + 1)
                    vec.tensor_tensor(s1r[0:1, 12:13], s1r[0:1, 11:12],
                                      s1r[0:1, 14:15], op=AO.mult)
                    vec.drain()
                    vec.reciprocal(s1r[0:1, 13:14], s1r[0:1, 12:13])
                    vec.drain()
                    vec.tensor_scalar(s1r[0:1, 16:17], s1r[0:1, 13:14], QMAX,
                                      None, op0=AO.mult)
                    inc(vec.tensor_scalar(s1r[0:1, 17:18], s1r[0:1, 12:13],
                                          1.0 / QMAX, None, op0=AO.mult),
                        "VA", 1)
                    vec.drain()
                    vec.wait_ge(sh("TA"), bTA + 2)
                    vec.tensor_copy(csx_dst, ps[7][0:128, 0:1])
                    inc(vec.tensor_copy(s_dst, ps[7][0:128, 1:2]), "VA", 1)
                    vec.drain()

                @blk.scalar
                def _(s):
                    inc(s.activation(s1r[0:1, 11:12], clip_slot, AF.Sigmoid),
                        "AA", 1)

                @blk.gpsimd
                def _(g):
                    g.wait_ge(sh("VA"), bVA + 1)
                    SG.issue(g.dma_start(cci.ap()[:, :], s1r[0:1, 9:10]))
                    SG.wait(g, 0)
                    inc(g.collective_compute(
                        "AllReduce", AO.max, replica_groups=[list(range(8))],
                        ins=[cci.ap().opt()], outs=[cco.ap().opt()]), "CC", 1)
                    g.wait_ge(sh("CC"), bCC + 1)
                    SG.issue(g.dma_start(s1r[0:1, 14:15], cco.ap()[:, :]))

        def quant_apply(src_h, ntile, m_which, r_which, csx_col, vrow_add,
                        do_round, qxt_tok=None, ph0d=False):
            with nc.Block() as blk:
                bVA, bVB, bAA = sv("VA"), sv("VB"), sv("AA")
                LA, SQ = Stm("LA"), Stm("SQ")
                if qxt_tok is not None:
                    TRq = Stm("TR", per=DC)
                    sq0_done = sv("SQ0") + 16 * ((ntile + 1) // 2)
                    sq1_done = sv("SQ1") + 16 * (ntile // 2)
                if ph0d:
                    LWp, SOp = Stm("LW"), Stm("SO")
                    strips = ([('g', hc) for hc in range(DC)] +
                              [('u', hc) for hc in range(DC)] +
                              [('d', ic) for ic in range(ICN)])
                eg = entry_guard()

                @blk.sync
                def _(sync):
                    wait_all_stores(sync)
                    for t in range(ntile):
                        if t >= 2:
                            sync.wait_ge(sh("AA"), bAA + (t - 1))
                        LA.issue(sync.dma_start(
                            xa[:, t % 2, :], src_h[t * 128:(t + 1) * 128, :]))
                    if qxt_tok is not None:
                        sync.wait_ge(sh("SQ0"), sq0_done)
                        sync.wait_ge(sh("SQ1"), sq1_done)
                        for hc in range(DC):
                            TRq.issue(sync.dma_start(
                                qx_fm[:, hc, 0:qxt_tok],
                                qa_h[0:qxt_tok, hc * 128:(hc + 1) * 128],
                                transpose=True))
                        TRq.wait(sync, 0)

                dflat = diagc[:, :, :, :].rearrange("p b j q -> p b (j q)")

                def strip_info(k):
                    kind, j = strips[k]
                    if kind == 'g':
                        return (qwg_h[:, j * 128:(j + 1) * 128],
                                wgT_h[j * 128:(j + 1) * 128, :], IFF)
                    if kind == 'u':
                        return (qwu_h[:, j * 128:(j + 1) * 128],
                                wuT_h[j * 128:(j + 1) * 128, :], IFF)
                    return (qwd_h[:, j * 128:(j + 1) * 128],
                            wdT_h[j * 128:(j + 1) * 128, :], 1024)

                @blk.gpsimd
                def _(g):
                    if not ph0d:
                        return
                    for k in range(len(strips)):
                        _, pdst, pw = strip_info(k)
                        LWp.wait(g, k)
                        SOp.issue(g.dma_start(pdst, dflat[:, k % 2, 0:pw]))

                @blk.vector
                def _(vec):
                    eg(vec)
                    for t in range(ntile):
                        LA.wait(vec, t)
                        if vrow_add:
                            vec.tensor_tensor(xa[:, t % 2, :], xa[:, t % 2, :],
                                              vrow[:, :], op=AO.add)
                            vec.drain()
                        if do_round:
                            vec.tensor_tensor(t2[:, 32 + (t % 2):33 + (t % 2)],
                                              mr(r_which, t), csx_col,
                                              op=AO.mult)
                        else:
                            vec.tensor_copy(t2[:, 32 + (t % 2):33 + (t % 2)],
                                            mr(r_which, t))
                        vec.drain()
                        inc(vec.scalar_tensor_tensor(
                            t3[:, 32 + (t % 2):33 + (t % 2)], mr(m_which, t),
                            -1.0, t2[:, 32 + (t % 2):33 + (t % 2)],
                            op0=AO.mult, op1=AO.mult), "VB", 1)
                        if do_round:
                            vec.wait_ge(sh("AA"), bAA + (t + 1))
                            if t >= 2:
                                SQ.wait(vec, t - 2)
                            vec.tensor_scalar(qi[:, :], qt[:, t % 2, :],
                                              CLIP_LO, CLIP_HI, op0=AO.max,
                                              op1=AO.min)
                            vec.drain()
                            inc(vec.tensor_copy(qb[:, t % 2, :], qi[:, :]),
                                "VA", 1)
                            vec.drain()

                @blk.scalar
                def _(s):
                    eg(s)
                    kph = [0]

                    def ph0d_loads(upto):
                        while kph[0] < min(upto, len(strips)):
                            k = kph[0]
                            src, _, w = strip_info(k)
                            if k >= 2:
                                SOp.wait(s, k - 2)
                            LWp.issue(s.dma_start(dflat[:, k % 2, 0:w], src,
                                                  transpose=True))
                            kph[0] += 1

                    if ph0d:
                        for s_ in ("SQ0", "SQ1"):
                            s.wait_ge(sh(s_), sv(s_))
                    for t in range(ntile):
                        if ph0d:
                            ph0d_loads((t + 1) * len(strips) // ntile)
                        s.wait_ge(sh("VB"), bVB + t + 1)
                        if do_round:
                            if t >= 2:
                                s.wait_ge(sh("VA"), bVA + (t - 1))
                            inc(s.activation(
                                qt[:, t % 2, :], xa[:, t % 2, :], AF.Identity,
                                bias=t3[:, 32 + (t % 2):33 + (t % 2)],
                                scale=t2[:, 32 + (t % 2):33 + (t % 2)]),
                                "AA", 1)
                            s.wait_ge(sh("VA"), bVA + (t + 1))
                        else:
                            if t >= 2:
                                SQ.wait(s, t - 2)
                            aa_val = inc(s.activation(
                                qb[:, t % 2, :], xa[:, t % 2, :], AF.Identity,
                                bias=t3[:, 32 + (t % 2):33 + (t % 2)],
                                scale=t2[:, 32 + (t % 2):33 + (t % 2)]),
                                "AA", 1)
                            s.wait_ge(sh("AA"), aa_val)
                        SQ.issue(s.dma_start(qa_h[t * 128:(t + 1) * 128, :],
                                             qb[:, t % 2, :]))

        def qx_transpose(ttok):
            with nc.Block() as blk:
                TRq = Stm("TR", per=DC)

                @blk.sync
                def _(sync):
                    wait_all_stores(sync)
                    for hc in range(DC):
                        TRq.issue(sync.dma_start(
                            qx_fm[:, hc, 0:ttok],
                            qa_h[0:ttok, hc * 128:(hc + 1) * 128],
                            transpose=True))
                    TRq.wait(sync, 0)

        def ffn_gateup(ttot):
            NTl = _tiles(ttot)
            nnt = len(NTl)
            with nc.Block() as blk:
                bTA = sv("TA")
                bVA, bVB, bPFA = sv("VA"), sv("VB"), sv("PFA")
                LW = Stm("LW", per=2)
                HS = Stm("SG")
                eg = entry_guard()

                @blk.sync
                def _(sync):
                    for s_ in ("SO0", "SO1"):
                        sync.wait_ge(sh(s_), sv(s_))
                    for ic in range(ICN):
                        if ic >= 2:
                            sync.wait_ge(sh("TA"), bTA + (ic - 1) * nnt)
                        LW.issue(sync.dma_start(
                            wgc[:, ic % 2, :, :],
                            bass.AP(wgT_h.tensor, ic * 128,
                                    [[IFF, 128], [128 * IFF, DC], [1, 128]])))
                        LW.issue(sync.dma_start(
                            wuc[:, ic % 2, :, :],
                            bass.AP(wuT_h.tensor, ic * 128,
                                    [[IFF, 128], [128 * IFF, DC], [1, 128]])))

                @blk.vector
                def _(vec):
                    eg(vec)
                    for i in range(ICN):
                        vec.tensor_tensor(cgc(i), sgc(i), sc_c(SHS),
                                          op=AO.mult)
                        vec.tensor_tensor(cuc(i), suc(i), sc_c(SHS),
                                          op=AO.mult)
                    vec.drain()
                    inc(vec.tensor_copy(t1[:, 0:1], cgc(0)), "VB", 1)
                    k = 0
                    for ic in range(ICN):
                        for (nt0, nts) in NTl:
                            vec.wait_ge(sh("PFA"), bPFA + 2 * (k + 1))
                            if k >= 2:
                                HS.wait(vec, k - 2)
                            vec.tensor_tensor(ht[:, k % 2, :nts],
                                              st_t[:, 2 * (k % 2), :nts],
                                              st_t[:, 2 * (k % 2) + 1, :nts],
                                              op=AO.mult)
                            vec.drain()
                            vec.tensor_reduce(t1[:, 34:35], ht[:, k % 2, :nts],
                                              axis=AX.X, op=AO.max,
                                              apply_absolute_value=True)
                            vec.drain()
                            inc(vec.tensor_tensor(gh[:, :], gh[:, :],
                                                  t1[:, 34:35], op=AO.max),
                                "VA", 1)
                            vec.drain()
                            k += 1

                @blk.tensor
                def _(t):
                    k = 0
                    for ic in range(ICN):
                        LW.wait(t, ic)
                        for (nt0, nts) in NTl:
                            pg = ps[2 * (k % 2)]
                            pu = ps[2 * (k % 2) + 1]
                            if k >= 2:
                                t.wait_ge(sh("PFA"), bPFA + 2 * (k - 1))
                            for hc in range(DC):
                                t.matmul(pg[:, :nts], wgc[:, ic % 2, hc, :],
                                         qx_fm[:, hc, nt0:nt0 + nts],
                                         start=(hc == 0), stop=(hc == DC - 1))
                            inst = None
                            for hc in range(DC):
                                inst = t.matmul(pu[:, :nts],
                                                wuc[:, ic % 2, hc, :],
                                                qx_fm[:, hc, nt0:nt0 + nts],
                                                start=(hc == 0),
                                                stop=(hc == DC - 1))
                            inc(inst, "TA", 1)
                            k += 1

                @blk.scalar
                def _(s):
                    s.wait_ge(sh("VB"), bVB + 1)
                    k = 0
                    for ic in range(ICN):
                        for (nt0, nts) in NTl:
                            pg = ps[2 * (k % 2)]
                            pu = ps[2 * (k % 2) + 1]
                            s.wait_ge(sh("TA"), bTA + k + 1)
                            if k >= 2:
                                s.wait_ge(sh("VA"), bVA + (k - 1))
                            inc(s.activation(st_t[:, 2 * (k % 2), :nts],
                                             pg[:, :nts], AF.Silu,
                                             scale=cgc(ic)), "PFA", 1)
                            inc(s.activation(st_t[:, 2 * (k % 2) + 1, :nts],
                                             pu[:, :nts], AF.Copy,
                                             scale=cuc(ic)), "PFA", 1)
                            k += 1

                @blk.gpsimd
                def _(g):
                    k = 0
                    for ic in range(ICN):
                        for (nt0, nts) in NTl:
                            g.wait_ge(sh("VA"), bVA + k + 1)
                            HS.issue(g.dma_start(
                                bass.AP(h_h.tensor, ic * 128 * t_tot + nt0,
                                        [[t_tot, 128], [1, nts]]),
                                ht[:, k % 2, :nts]))
                            k += 1

        def srow_build():
            with nc.Block() as blk:
                bTA, bVA = sv("TA"), sv("VA")
                SQ = Stm("SQ")
                eg = entry_guard()

                @blk.vector
                def _(vec):
                    eg(vec)
                    for dc in range(DC):
                        vec.tensor_scalar(t1[:, 40 + dc:41 + dc], sdc(dc),
                                          sc_c(SHS), 0.5, op0=AO.mult,
                                          op1=AO.mult)
                    vec.drain()
                    inc(vec.tensor_copy(t1[:, 48:49], t1[:, 40:41]), "VA", 1)
                    vec.wait_ge(sh("TA"), bTA + 1)
                    inc(vec.tensor_copy(ar32[0:DC, :], ps[6][0:DC, 0:128]),
                        "VA", 1)
                    vec.drain()
                    for half in range(2):
                        vec.wait_ge(sh("TA"), bTA + 2 + half)
                        vec.tensor_copy(srow[:, half * 512:(half + 1) * 512],
                                        ps[7 - half][:, 0:512])
                        vec.drain()
                        inc(vec.tensor_copy(t1[:, 49:50], t1[:, 40:41]),
                            "VA", 1)

                @blk.tensor
                def _(t):
                    t.wait_ge(sh("VA"), bVA + 1)
                    inc(t.matmul(ps[6][0:DC, 0:128], t1[:, 40:40 + DC],
                                 idf[:, :], start=True, stop=True), "TA", 1)
                    SQ.wait(t, 1)
                    for half in range(2):
                        if half == 1:
                            t.wait_ge(sh("VA"), bVA + 3)
                        inc(t.matmul(ps[7 - half][:, 0:512], onesf[:, :],
                                     rowD[0:1, half * 512:(half + 1) * 512],
                                     start=True, stop=True), "TA", 1)

                @blk.scalar
                def _(s):
                    s.wait_ge(sh("VA"), bVA + 2)
                    SQ.issue(s.dma_start(
                        misc_h[0:D].rearrange("(c p) -> c p", p=128),
                        ar32[0:DC, :]))
                    SQ.wait(s, 0)
                    SQ.issue(s.dma_start(rowD[0:1, :], misc_h[None, 0:D]))

        def ffn_down(ttot, res_h, out_h, next_stats, final, tok_shift):
            NTl = _tiles(ttot)
            nnt = len(NTl)
            KS = []
            for ni, (nt0, nts) in enumerate(NTl):
                for ms in range((nts + 127) // 128):
                    KS.append((ni, nt0, nts, ms))
            ks_before = [0] * (nnt + 1)
            for ni in range(nnt):
                ks_before[ni + 1] = ks_before[ni] + (NTl[ni][1] + 127) // 128
            with nc.Block() as blk:
                bTA, bAA, bVA = sv("TA"), sv("AA"), sv("VA")
                bVB, bPFA, bAB = sv("VB"), sv("PFA"), sv("AB")
                bDGd = sv("DG")
                LW = Stm("LW")
                LH = Stm("LH")
                XB = Stm("LB")
                ST = Stm("SO" if final else "SG")
                eg = entry_guard()

                ICN2 = ICN // 2

                @blk.sync
                def _(sync):
                    wait_all_stores(sync)
                    LW.issue(sync.dma_start(
                        wdT_v[:, :, :],
                        bass.AP(wdT_h.tensor, 0,
                                [[1024, 128], [128 * 1024, ICN], [1, 1024]])))
                    def xb_issue(k):
                        kni, knt0, knts, kms = KS[k]
                        if k >= 2:
                            sync.wait_ge(sh("DG"), bDGd + (k - 1))
                        XB.issue(sync.dma_start(
                            xb[:, k % 2, :],
                            res_h[knt0 + kms * 128:knt0 + kms * 128 + 128,
                                  :]))

                    kk = 0
                    for ni, (nt0, nts) in enumerate(NTl):
                        for icp in range(ICN2):
                            if kk >= 2:
                                sync.wait_ge(sh("AA"), bAA + (kk - 1))
                            LH.issue(sync.dma_start(
                                hl[:, kk % 2, :, :nts],
                                bass.AP(h_h.tensor,
                                        (2 * icp) * 128 * t_tot + nt0,
                                        [[t_tot, 128], [128 * t_tot, 2],
                                         [1, nts]])))
                            kk += 1
                        if ni >= 1:
                            for k in range(ks_before[ni - 1], ks_before[ni]):
                                xb_issue(k)
                    for k in range(ks_before[nnt - 1], ks_before[nnt]):
                        xb_issue(k)

                @blk.gpsimd
                def _(g):
                    for s in ("SG0", "SG1"):
                        g.wait_ge(sh(s), sv(s))
                    for k, (ni, nt0, nts, ms) in enumerate(KS):
                        g.wait_ge(sh("DG"), bDGd + (k + 1))
                        row = nt0 + ms * 128 - tok_shift
                        ST.issue(g.dma_start(out_h[row:row + 128, :],
                                             ob[:, k % 2, :]))

                @blk.scalar
                def _(s):
                    def emit_square(k):
                        ni, nt0, nts, ms = KS[k]
                        s.wait_ge(sh("DG"), bDGd + (k + 1))
                        tt = (nt0 + ms * 128) // 128
                        inc(s.activation(ab[:, k % 2, :], ob[:, k % 2, :],
                                         AF.Square, accum_out=s_ssq(tt)),
                            "AB", 1)

                    kk = 0
                    for ni, (nt0, nts) in enumerate(NTl):
                        for icp in range(ICN2):
                            LH.wait(s, kk)
                            if kk >= 2:
                                s.wait_ge(sh("VB"), bVB + (kk - 1))
                            inc(s.activation(hq_ti[:, kk % 2, :, :nts],
                                             hl[:, kk % 2, :, :nts], AF.Copy,
                                             scale=sc_c(CSH)), "AA", 1)
                            kk += 1
                        if next_stats and ni >= 1:
                            for k in range(ks_before[ni - 1], ks_before[ni]):
                                emit_square(k)
                    if next_stats:
                        for k in range(ks_before[nnt - 1], ks_before[nnt]):
                            emit_square(k)

                @blk.vector
                def _(vec):
                    eg(vec)

                    def emit_evac(k):
                        ni, nt0, nts, ms = KS[k]
                        mts = min(128, nts - ms * 128)
                        pd0 = ps[2 * (k % 4)]
                        pd1 = ps[2 * (k % 4) + 1]
                        vec.wait_ge(sh("TA"), bTA + k + 1)
                        XB.wait(vec, k)
                        if k >= 2:
                            ST.wait(vec, k - 2)
                            if next_stats:
                                vec.wait_ge(sh("AB"), bAB + (k - 1))
                        for half, pd in ((0, pd0), (1, pd1)):
                            vec.scalar_tensor_tensor(
                                ob[:mts, k % 2, half * 512:(half + 1) * 512],
                                pd[:mts, :],
                                scal[:mts, SHS:SHS + 1],
                                xb[:mts, k % 2, half * 512:(half + 1) * 512],
                                op0=AO.mult, op1=AO.add)
                        vec.drain()
                        inc(vec.tensor_copy(t1[:, 50:51], t1[:, 50:51]),
                            "PFA", 2)
                        if final:
                            vec.tensor_tensor(ob[:mts, k % 2, :],
                                              ob[:mts, k % 2, :],
                                              vrow[:mts, :], op=AO.add)
                            vec.drain()
                        inc(vec.tensor_copy(t1[:, 51:52], t1[:, 51:52]),
                            "DG", 1)
                        if next_stats:
                            tt = (nt0 + ms * 128) // 128
                            vec.tensor_reduce(s_sum(tt), ob[:, k % 2, :],
                                              axis=AX.X, op=AO.add)
                            vec.drain()

                    kk = 0
                    for ni, (nt0, nts) in enumerate(NTl):
                        for icp in range(ICN2):
                            vec.wait_ge(sh("AA"), bAA + kk + 1)
                            if ni >= 2 and icp == 0:
                                vec.wait_ge(sh("TA"), bTA + ks_before[ni - 1])
                            inc(vec.tensor_scalar(
                                qh_v[:, ni % 2, 2 * icp:2 * icp + 2, 0:nts],
                                hq_ti[:, kk % 2, :, :nts],
                                -(QMAX + 1.0), QMAX, op0=AO.max, op1=AO.min),
                                "VB", 1)
                            vec.drain()
                            kk += 1
                        if ni >= 1:
                            for k in range(ks_before[ni - 1], ks_before[ni]):
                                emit_evac(k)
                    for k in range(ks_before[nnt - 1], ks_before[nnt]):
                        emit_evac(k)

                @blk.tensor
                def _(t):
                    LW.wait(t, 0)
                    for k, (ni, nt0, nts, ms) in enumerate(KS):
                        mts = min(128, nts - ms * 128)
                        pd0 = ps[2 * (k % 4)]
                        pd1 = ps[2 * (k % 4) + 1]
                        if k >= 4:
                            t.wait_ge(sh("PFA"), bPFA + 2 * (k - 3))
                        t.wait_ge(sh("VB"), bVB + (ni + 1) * ICN2)
                        inst = None
                        for ic in range(ICN):
                            t.matmul(pd0[:mts, :],
                                     qh_v[:, ni % 2, ic,
                                          ms * 128:ms * 128 + mts],
                                     wdT_v[:, ic, 0:512],
                                     start=(ic == 0), stop=(ic == ICN - 1))
                            inst = t.matmul(
                                pd1[:mts, :],
                                qh_v[:, ni % 2, ic,
                                     ms * 128:ms * 128 + mts],
                                wdT_v[:, ic, 512:1024],
                                start=(ic == 0), stop=(ic == ICN - 1))
                        inc(inst, "TA", 1)

        def attn_main():
            nnt = len(NT)
            OMS = []
            for ni, (nt0, nts) in enumerate(NT):
                OMS.append([ms for ms in range((nts + 127) // 128)
                            if nt0 + ms * 128 >= HALO])
            oms_before = [0] * (nnt + 1)
            for ni in range(nnt):
                oms_before[ni + 1] = oms_before[ni] + len(OMS[ni])

            def ko_of(ko):
                ni = next(i for i in range(nnt) if oms_before[i + 1] > ko)
                return ni, OMS[ni][ko - oms_before[ni]], NT[ni][0]

            with nc.Block() as blk:
                bTA, bTB, bTC = sv("TA"), sv("TB"), sv("TC")
                bAA, bVB, bVC = sv("AA"), sv("VB"), sv("VC")
                bPFA, bDG, bVA = sv("PFA"), sv("DG"), sv("VA")
                LW = Stm("LW")
                TR = Stm("TR", per=DC)
                XB = Stm("LB")
                XO = Stm("SG")
                LD = Stm("LH")
                CCQ = Stm("CQ")
                bCC = sv("CC")
                eg = entry_guard()

                @blk.sync
                def _(sync):
                    wait_all_stores(sync)
                    for kc in range(DC):
                        if kc >= 2:
                            sync.wait_ge(sh("VB"), bVB + (kc - 1))
                        LW.issue(sync.dma_start(
                            qt[:, kc % 2, :],
                            w1_in[kc * 128:(kc + 1) * 128, :]))
                    for kc in range(DC):
                        sync.wait_ge(sh("VB"), bVB + DC + kc - 1)
                        LW.issue(sync.dma_start(
                            qt[:, kc % 2, :],
                            w2_in[kc * 128:(kc + 1) * 128, :]))
                    ko = 0
                    for ni, (nt0, nts) in enumerate(NT):
                        if ni >= 2:
                            sync.wait_ge(sh("TA"), bTA + (ni - 1) * DC)
                            sync.wait_ge(sh("VC"), bVC + (ni - 1))
                        for hc in range(DC):
                            TR.issue(sync.dma_start(
                                x2f_v[:, ni % 2, hc, 0:nts],
                                qa_h[nt0:nt0 + nts, hc * 128:(hc + 1) * 128],
                                transpose=True))
                        for c in range(DC):
                            gstep = ni * DC + c
                            if gstep >= 2:
                                sync.wait_ge(sh("TB"), bTB + gstep - 1)
                            LD.issue(sync.dma_start(
                                diagc[:, gstep % 2, :, :],
                                bass.AP(diag_h.tensor,
                                        c * 128 * NTAP * 128,
                                        [[NTAP * 128, 128],
                                         [1, NTAP * 128]])))
                        for ms in OMS[ni]:
                            if ko >= 2:
                                skew = 2 if ko - 2 >= oms_before[nnt - 1] \
                                    else 0
                                sync.wait_ge(sh("VA"),
                                             bVA + (ko - 1) + skew)
                            XB.issue(sync.dma_start(
                                xb[:, ko % 2, :],
                                xp_h[nt0 + ms * 128:nt0 + ms * 128 + 128, :]))
                            ko += 1

                @blk.vector
                def _(vec):
                    eg(vec)
                    for kc in range(DC):
                        LW.wait(vec, kc)
                        vec.tensor_copy(w1_v[:, kc, :], qt[:, kc % 2, :])
                        vec.drain()
                        inc(vec.tensor_copy(t1[:, 0:1], t1[:, 0:1]), "VB", 1)
                    for kc in range(DC):
                        LW.wait(vec, DC + kc)
                        vec.tensor_copy(w2_v[:, kc, :], qt[:, kc % 2, :])
                        vec.drain()
                        inc(vec.tensor_copy(t1[:, 0:1], t1[:, 0:1]), "VB", 1)
                    ko = 0
                    for ni, (nt0, nts) in enumerate(NT):
                        lo = HALO if ni == 0 else 0
                        TR.wait(vec, ni)
                        vec.tensor_reduce(t2[:, 40:40 + DC],
                                          x2f_v[:, ni % 2, :, lo:nts],
                                          axis=AX.X, op=AO.add)
                        vec.drain()
                        inc(vec.tensor_tensor(tot8[:, :], tot8[:, :],
                                              t2[:, 40:40 + DC], op=AO.add),
                            "VC", 1)
                        vec.drain()
                        if ni == nnt - 1:
                            for b in range(4):
                                vec.tensor_scalar(
                                    ctr32[:, b * DC:(b + 1) * DC],
                                    tot8[:, :], bsel_b[:, b:b + 1],
                                    None, op0=AO.mult)
                            vec.drain()
                            inc(vec.tensor_copy(t1[:, 0:1], t1[:, 0:1]),
                                "VA", 1)
                            vec.wait_ge(sh("TA"), bTA + nnt * DC + 1)
                            inc(vec.tensor_copy(ar32[:, :],
                                                ps[6][0:32, 0:128]),
                                "VA", 1)
                            vec.drain()
                        if ni == 0:
                            vec.wait_ge(sh("AA"), bAA + DC)
                            for c in range(DC):
                                vec.tensor_scalar(pr_v[:, c, 128:256],
                                                  pr_v[:, c, 128:256],
                                                  sc_c(HMASK), None,
                                                  op0=AO.mult)
                            vec.drain()
                        for c in range(DC):
                            gstep = ni * DC + c
                            inc(vec.tensor_copy(t1[:, 0:1], t1[:, 0:1]),
                                "DG", 1)
                            if ni >= 1 and c == 0:
                                vec.wait_ge(sh("TC"), bTC + oms_before[ni])
                            vec.wait_ge(sh("TB"), bTB + gstep + 1)
                            vec.tensor_copy(z_v[:, c, 0:nts],
                                            ps[4 + (gstep % 2)][:, :nts])
                            vec.drain()
                            if ni < nnt - 1:
                                vec.tensor_copy(pr_v[:, c, 0:128],
                                                pr_v[:, c, nts:nts + 128])
                                vec.drain()
                            inc(vec.tensor_copy(t1[:, 1:2], t1[:, 1:2]),
                                "DG", 1)
                        for oi, ms in enumerate(OMS[ni]):
                            mts = min(128, nts - ms * 128)
                            po0 = ps[2 * (ko % 2)]
                            po1 = ps[2 * (ko % 2) + 1]
                            vec.wait_ge(sh("TC"), bTC + ko + 1)
                            XB.wait(vec, ko)
                            if ko >= 2:
                                XO.wait(vec, ko - 2)
                            for half, po in ((0, po0), (1, po1)):
                                vec.tensor_tensor(
                                    ob[:mts, ko % 2,
                                       half * 512:(half + 1) * 512],
                                    po[:mts, :],
                                    xb[:mts, ko % 2,
                                       half * 512:(half + 1) * 512],
                                    op=AO.add)
                            vec.drain()
                            inc(vec.tensor_copy(t1[:, 0:1], t1[:, 0:1]),
                                "PFA", 2)
                            inc(vec.tensor_copy(t1[:, 1:2], t1[:, 1:2]),
                                "VA", 1)
                            ko += 1

                @blk.scalar
                def _(s):
                    for ni, (nt0, nts) in enumerate(NT):
                        for mc in range(DC):
                            s.wait_ge(sh("TA"), bTA + ni * DC + mc + 1)
                            if ni >= 1:
                                s.wait_ge(sh("DG"), bDG + 2 * ((ni - 1) * DC
                                                               + mc + 1))
                            inc(s.activation(pr_v[:, mc, 128:128 + nts],
                                             ps[6 + (mc % 2)][:, :nts],
                                             AF.Identity,
                                             bias=b1f[:, mc:mc + 1],
                                             scale=1.0), "AA", 1)

                @blk.tensor
                def _(t):
                    for ni, (nt0, nts) in enumerate(NT):
                        t.wait_ge(sh("VB"), bVB + DC)
                        TR.wait(t, ni)
                        for mc in range(DC):
                            if ni * DC + mc >= 2:
                                t.wait_ge(sh("AA"), bAA + ni * DC + mc - 1)
                            inst = None
                            for kc in range(DC):
                                inst = t.matmul(
                                    ps[6 + (mc % 2)][:, :nts],
                                    w1_v[:, kc, mc * 128:(mc + 1) * 128],
                                    x2f_v[:, ni % 2, kc, 0:nts],
                                    start=(kc == 0), stop=(kc == DC - 1))
                            inc(inst, "TA", 1)
                        if ni == nnt - 1:
                            t.wait_ge(sh("VA"),
                                      bVA + oms_before[nnt - 1] + 1)
                            t.wait_ge(sh("AA"), bAA + nnt * DC)
                            inc(t.matmul(ps[6][0:32, 0:128], ctr32[:, :],
                                         idf[:, :], start=True, stop=True),
                                "TA", 1)
                        for c in range(DC):
                            gstep = ni * DC + c
                            LD.wait(t, gstep)
                            t.wait_ge(sh("DG"), bDG + 2 * gstep + 1)
                            t.wait_ge(sh("AA"), bAA + ni * DC + c + 1)
                            if ni == 0 and c == 0:
                                t.wait_ge(sh("GI"), 1)
                            if gstep >= 2:
                                t.wait_ge(sh("DG"), bDG + 2 * (gstep - 1))
                            pb = ps[4 + (gstep % 2)]
                            inst = None
                            for j in range(NTAP):
                                inst = t.matmul(
                                    pb[:, :nts], diagc[:, gstep % 2, j, :],
                                    pr_v[:, c, 128 - j:128 - j + nts],
                                    start=(j == 0), stop=(j == NTAP - 1))
                            inc(inst, "TB", 1)
                        t.wait_ge(sh("VB"), bVB + 2 * DC)
                        t.wait_ge(sh("DG"), bDG + 2 * (ni * DC + DC))
                        for oi, ms in enumerate(OMS[ni]):
                            mts = min(128, nts - ms * 128)
                            ko = oms_before[ni] + oi
                            po0 = ps[2 * (ko % 2)]
                            po1 = ps[2 * (ko % 2) + 1]
                            if ko >= 2:
                                t.wait_ge(sh("PFA"), bPFA + 2 * (ko - 1))
                            inst = None
                            for kc in range(DC):
                                t.matmul(po0[:mts, :],
                                         z_v[:, kc, ms * 128:ms * 128 + mts],
                                         w2_v[:, kc, 0:512],
                                         start=(kc == 0), stop=(kc == DC - 1))
                                inst = t.matmul(
                                    po1[:mts, :],
                                    z_v[:, kc, ms * 128:ms * 128 + mts],
                                    w2_v[:, kc, 512:1024],
                                    start=(kc == 0), stop=(kc == DC - 1))
                            inc(inst, "TC", 1)

                @blk.gpsimd
                def _(g):
                    kom1 = oms_before[nnt - 1]
                    for ko in range(oms_before[nnt]):
                        if ko == kom1:
                            g.wait_ge(sh("VA"), bVA + kom1 + 2)
                            CCQ.issue(g.dma_start(cc_t["i3"].ap()[:, :],
                                                  ar32[:, :]))
                            CCQ.wait(g, 0)
                            inc(g.collective_compute(
                                "AllReduce", AO.add,
                                replica_groups=[list(range(8))],
                                ins=[cc_t["i3"].ap().opt()],
                                outs=[cc_t["o3"].ap().opt()]), "CC", 1)
                            g.wait_ge(sh("CC"), bCC + 1)
                            CCQ.issue(g.dma_start(ar32[:, :],
                                                  cc_t["o3"].ap()[:, :]))
                        ni, ms, nt0 = ko_of(ko)
                        skew = 2 if ko >= kom1 else 0
                        g.wait_ge(sh("VA"), bVA + (ko + 1) + skew)
                        row = nt0 + ms * 128 - HALO
                        XO.issue(g.dma_start(xpp_h[row:row + 128, :],
                                             ob[:, ko % 2, :]))

        def ar3_vrow():
            with nc.Block() as blk:
                bTA, bVA = sv("TA"), sv("VA")
                eg = entry_guard()

                @blk.vector
                def _(vec):
                    eg(vec)
                    vec.wait_ge(sh("TA"), bTA + 1)
                    inc(vec.tensor_copy(allT[:, :], ps[7][:, 0:32]), "VA", 1)
                    vec.drain()
                    vec.tensor_scalar(t2[:, 0:DC], allT[:, 0:DC],
                                      bsel_b[:, 0:1], None, op0=AO.mult)
                    vec.drain()
                    for b in range(1, 4):
                        vec.scalar_tensor_tensor(
                            t2[:, 0:DC], allT[:, b * DC:(b + 1) * DC],
                            bsel_b[:, b:b + 1], t2[:, 0:DC],
                            op0=AO.mult, op1=AO.add)
                        vec.drain()
                    vec.tensor_copy(tot8[:, :], t2[:, 0:DC])
                    vec.drain()
                    inc(vec.tensor_copy(t8b[:, :], tot8[:, :]), "VA", 1)
                    vec.drain()
                    for mc in range(DC):
                        vec.wait_ge(sh("TA"), bTA + 2 + mc)
                        vec.tensor_tensor(totf[:, mc:mc + 1],
                                          ps[mc % 2][:, 0:1],
                                          b1T[:, mc:mc + 1], op=AO.add)
                        vec.drain()
                        inc(vec.tensor_copy(t1[:, 0:1], t1[:, 0:1]), "VA", 1)
                    vec.tensor_tensor(u8[:, :], totf[:, :], mwf[:, :, 0],
                                      op=AO.mult)
                    vec.drain()
                    inc(vec.tensor_copy(u8b[:, :], u8[:, :]), "VA", 1)
                    vec.drain()
                    vec.wait_ge(sh("TA"), bTA + DC + 3)
                    vec.tensor_copy(rowD[0:1, 0:512], ps[6][0:1, :])
                    vec.tensor_copy(rowD[0:1, 512:1024], ps[7][0:1, :])
                    vec.drain()
                    vec.tensor_tensor(rowD[0:1, :], rowD[0:1, :],
                                      rowD2[0:1, :], op=AO.add)
                    vec.drain()
                    inc(vec.tensor_copy(t1[:, 0:1], t1[:, 0:1]), "VA", 1)
                    for half in range(2):
                        vec.wait_ge(sh("TA"), bTA + DC + 4 + half)
                        vec.tensor_copy(vrow[:, half * 512:(half + 1) * 512],
                                        ps[4 + half][:, 0:512])
                        vec.drain()

                @blk.tensor
                def _(t):
                    for s_ in ("CQ0", "CQ1"):
                        t.wait_ge(sh(s_), sv(s_))
                    inc(t.matmul(ps[7][:, 0:32], ar32[:, :], idf[0:32, 0:32],
                                 start=True, stop=True), "TA", 1)
                    t.wait_ge(sh("VA"), bVA + 2)
                    for mc in range(DC):
                        if mc >= 2:
                            t.wait_ge(sh("VA"), bVA + 2 + (mc - 1))
                        inst = None
                        for kc in range(DC):
                            inst = t.matmul(ps[mc % 2][:, 0:1],
                                            w1_v[:, kc,
                                                 mc * 128:(mc + 1) * 128],
                                            t8b[:, kc:kc + 1],
                                            start=(kc == 0),
                                            stop=(kc == DC - 1))
                        inc(inst, "TA", 1)
                    t.wait_ge(sh("VA"), bVA + DC + 3)
                    for half, bank in ((0, 6), (1, 7)):
                        inst = None
                        for kc in range(DC):
                            inst = t.matmul(ps[bank][0:1, :],
                                            u8b[:, kc:kc + 1],
                                            w2_v[:, kc,
                                                 half * 512:(half + 1) * 512],
                                            start=(kc == 0),
                                            stop=(kc == DC - 1))
                        inc(inst, "TA", 1)
                    t.wait_ge(sh("VA"), bVA + DC + 4)
                    for half, bank in ((0, 4), (1, 5)):
                        inc(t.matmul(ps[bank][:, 0:512], onesf[:, :],
                                     rowD[0:1, half * 512:(half + 1) * 512],
                                     start=True, stop=True), "TA", 1)

                @blk.gpsimd
                def _(g):
                    g.memset(gy[:, :], 0.0)
                    g.memset(gh[:, :], 0.0)

        def _go():
            armax(gy, cc_t["i1"], cc_t["o1"], s1r[0:1, 0:1], sc_c(CSX),
                  sc_c(SHS))
            quant_apply(x_in, n128, 0, 1, sc_c(CSX), False, True,
                        qxt_tok=t_tot, ph0d=True)
            if stop_after == "quant1":
                return
            ffn_gateup(t_tot)
            if stop_after == "gateup1":
                return
            armax(gh, cc_t["i2"], cc_t["o2"], s1r[0:1, 1:2], sc_c(CSH),
                  sc_c(SHS))
            ffn_down(t_tot, x_in, xp_h, True, False, 0)
            if stop_after == "ffn1":
                return
            ln_batch_mr(n128, 2, 3)
            quant_apply(xp_h, n128, 2, 3, None, False, False)
            if stop_after == "x2q":
                return
            attn_main()
            if stop_after == "attn":
                return
            ar3_vrow()
            if stop_after == "ar3":
                return
            ln_stats_ymax(xpp_h, n128o, 4, 5, True, gy)
            armax(gy, cc_t["i4"], cc_t["o4"], s1r[0:1, 0:1], sc_c(CSX),
                  sc_c(SHS))
            quant_apply(xpp_h, n128o, 4, 5, sc_c(CSX), True, True,
                        qxt_tok=t_own)
            if stop_after == "quant2":
                return
            ffn_gateup(t_own)
            if stop_after == "gateup2":
                return
            armax(gh, cc_t["i5"], cc_t["o5"], s1r[0:1, 1:2], sc_c(CSH),
                  sc_c(SHS))
            ffn_down(t_own, xpp_h, out_ext, False, True, 0)

        _go()

        if dbg:
            with nc.Block() as blk:
                DQ = Stm("SQ")

                @blk.scalar
                def _(s):
                    DQ.issue(s.dma_start(dbg_h[:, 0:8], scal[:, :]))
                    DQ.issue(s.dma_start(dbg_h[:, 8:16], vrow[:, 0:8]))
                    DQ.issue(s.dma_start(dbg_h[:, 16:24], vrow[:, 8:16]))
                    DQ.issue(s.dma_start(dbg_h[:, 24:32], tot8[:, :]))
                    DQ.issue(s.dma_start(dbg_h[0:1, 32:64], s1r[:, :]))
                    DQ.wait(s, 4)

        with nc.Block() as blk:
            @blk.gpsimd
            def _(g):
                g.wait_ge(sh("SO0"), sv("SO0"))
                g.wait_ge(sh("SO1"), sv("SO1"))

    return nc


def kernel(**inputs):
    x = np.ascontiguousarray(np.asarray(inputs["L0_attn_input_data"],
                                        dtype=np.float32))
    B, S, _ = x.shape
    t_own = S // 2
    nc = build_nc(t_own)

    w1 = np.ascontiguousarray(inputs["L0_attn_W1"], dtype=np.float32)
    b1 = np.ascontiguousarray(inputs["L0_attn_bias1"], dtype=np.float32)
    w2 = np.ascontiguousarray(inputs["L0_attn_W2"], dtype=np.float32)
    b2 = np.ascontiguousarray(inputs["L0_attn_bias2"], dtype=np.float32)
    mw = np.ascontiguousarray(inputs["L0_attn_memory_weights"],
                              dtype=np.float32)
    cx = np.asarray(inputs["L0_ffn_clip_x"], dtype=np.float32).reshape(1, 1)
    cd = np.asarray(inputs["L0_ffn_clip_down"],
                    dtype=np.float32).reshape(1, 1)
    wg = np.ascontiguousarray(inputs["L0_ffn_w_gate"], dtype=np.float32)
    wu = np.ascontiguousarray(inputs["L0_ffn_w_up"], dtype=np.float32)
    wd = np.ascontiguousarray(inputs["L0_ffn_w_down"], dtype=np.float32)

    in_maps = []
    for c in range(8):
        b, h = c // 2, c % 2
        if h == 0:
            halo = np.zeros((HALO, D), dtype=np.float32)
            hmv = 0.0
        else:
            halo = x[b, t_own - HALO:t_own]
            hmv = 1.0
        shard = np.concatenate([halo, x[b, h * t_own:(h + 1) * t_own]],
                               axis=0)
        bsel = np.zeros((1, 4), dtype=np.float32)
        bsel[0, b] = 1.0
        in_maps.append({
            "x": np.ascontiguousarray(shard),
            "hmask": np.array([[hmv]], dtype=np.float32),
            "bsel": bsel,
            "W1": w1, "b1": b1, "W2": w2, "b2": b2, "mw": mw,
            "clip_x": cx, "clip_down": cd,
            "w_gate": wg, "w_up": wu, "w_down": wd,
        })

    res = run_bass_kernel_spmd(nc, in_maps, core_ids=list(range(8)))
    out = np.empty((B, S, D), dtype=np.float32)
    for c in range(8):
        b, h = c // 2, c % 2
        out[b, h * t_own:(h + 1) * t_own] = res.results[c]["out"]
    return out
